# version 17
# baseline (speedup 1.0000x reference)
import contextlib
import numpy as np
import ml_dtypes
import concourse.bass as bass
import concourse.mybir as mybir
from concourse.bass_utils import run_bass_kernel_spmd

F32 = mybir.dt.float32
BF16 = mybir.dt.bfloat16
AF = mybir.ActivationFunctionType
ALU = mybir.AluOpType
AX = mybir.AxisListType

NCORES = 8
T, D = 16384, 1024
TL = T // NCORES
NCH = D // 128
H, DK, SEGL = 16, 64, 64
NSEG = TL // SEGL
DI, NST, DCONV = 2048, 16, 4
NCH_I = DI // 128
DI_SH = DI // NCORES
MAGIC = 12582912.0
LN_EPS = 1e-5
ATTN_EPS = 1e-6
QMAX = 127.0
THRESH = 0.7
RG = [[0, 1, 2, 3, 4, 5, 6, 7]]
ENGS = ("y", "t", "v", "s", "g")


class Sem:
    def __init__(self, handle, name):
        self.h, self.name, self.n = handle, name, 0


class Rec:
    def __init__(self):
        self.cmds = {e: [] for e in ENGS}

    def _add(self, e, fn):
        self.cmds[e].append(fn)

    def y(self, fn): self._add("y", fn)
    def t(self, fn): self._add("t", fn)
    def v(self, fn): self._add("v", fn)
    def s(self, fn): self._add("s", fn)
    def g(self, fn): self._add("g", fn)

    def inc(self, e, sem, fn, k=1):
        sem.n += k
        self._add(e, lambda h, fn=fn, sem=sem, k=k: fn(h).then_inc(sem.h, k))

    def wait(self, e, sem, val=None):
        v = sem.n if val is None else val
        self._add(e, lambda h, sem=sem, v=v: h.wait_ge(sem.h, v))

    def barrier(self, p, engs=ENGS):
        for e in engs:
            for sem in p.all_sems:
                if sem.n:
                    self.wait(e, sem)

    def play(self, block):
        for e, meth in (("y", block.sync), ("t", block.tensor), ("v", block.vector),
                        ("s", block.scalar), ("g", block.gpsimd)):
            cmds = self.cmds[e]
            meth(lambda h, cmds=cmds: [fn(h) for fn in cmds])


class _P:
    pass


def wave_load(p, nblocks, emit_dma, wave=2, throttle=None):
    r = p.r
    nw = (nblocks + wave - 1) // wave
    wait_for = [None] * nblocks
    for w in range(nw):
        sem = p.s_wv[w % 4]
        r.wait("g", sem, sem.n)
        lo, hi = w * wave, min((w + 1) * wave, nblocks)
        if throttle is not None:
            throttle(lo)
        for i in range(lo, hi):
            r.inc("g", sem, lambda g, i=i: emit_dma(g, i), 16)
        for i in range(lo, hi):
            wait_for[i] = (sem, sem.n)
    return wait_for


def np_t(a):
    return np.ascontiguousarray(np.asarray(a, dtype=np.float32).T)


def chunk_views(buf, n, w):
    return [buf[:, c * w:(c + 1) * w] for c in range(n)]


def build_program(stages=("ffn1", "ffn2"), debug_taps=()):
    nc = bass.Bass("TRN2", target_bir_lowering=False, debug=False)
    p = _P()
    p.nc = nc
    p.debug_taps = set(debug_taps)
    p.dbg_outs = []
    p.stages = stages

    def din(name, shape, dt=F32):
        return nc.dram_tensor(name, list(shape), dt, kind="ExternalInput").ap()

    def dint(name, shape, dt=F32, space="Local"):
        return nc.dram_tensor(name, list(shape), dt, kind="Internal",
                              addr_space=space).ap()

    p.d_x = din("xT", [D, TL])
    p.d_wq = din("wqT", [D, D]); p.d_wk = din("wkT", [D, D])
    p.d_wv = din("wvT", [D, D]); p.d_wo = din("woT", [D, D])
    p.d_beta = din("beta_param", [1, H])
    p.d_norm_w = din("norm_w", [1, D]); p.d_norm_b = din("norm_b", [1, D])
    p.d_win = din("winT", [D, 2 * DI])
    p.d_convw = din("conv_w", [DI, DCONV]); p.d_convb = din("conv_b", [1, DI])
    p.d_wdt = din("wdtT", [DI, DI]); p.d_bdt = din("b_dt", [1, DI])
    p.d_wB = din("wBT", [DI, NST]); p.d_wC = din("wCT", [DI, NST])
    p.d_D = din("Dvec", [1, DI]); p.d_alog = din("A_log", [DI, NST])
    p.d_wout = din("woutT", [DI, D])
    p.d_fw = din("fwT", [D, D])
    p.d_gam = din("gamma", [1, D]); p.d_bet = din("beta", [1, D])
    p.d_ident = din("identity", [128, 128], BF16)
    p.d_cid = din("cid", [1, 1], mybir.dt.int32)
    p.d_czero = din("czero", [128, 1])
    p.d_identf = din("identityf", [128, 128])
    p.d_cmask = din("cmask", [128, SEGL])
    p.d_czmask = din("czmask", [128, NCORES])
    p.d_out = nc.dram_tensor("outT", [D, TL], F32, kind="ExternalOutput").ap()

    p.cc_mx_in = dint("cc_mx_in", [128, 1])
    p.cc_mx_out = dint("cc_mx_out", [NCORES * 128, 1])
    p.dr_wq = dint("dr_wq", [128, NCH * D], BF16)

    p.exit_stack = contextlib.ExitStack()
    st = p.exit_stack
    p.sb = lambda name, shape, dt=F32: st.enter_context(nc.sbuf_tensor(name, list(shape), dt))
    p.psb = lambda name, shape: st.enter_context(nc.psum_tensor(name, list(shape), F32))

    def sem(name):
        s = Sem(st.enter_context(nc.semaphore(name)), name)
        p.all_sems.append(s)
        return s
    p.all_sems = []

    p.b_x = p.sb("x_res", [128, NCH * TL])
    p.b_consts = p.sb("consts", [128, 16])
    p.b_ones1 = p.sb("ones1", [128, 1])
    p.b_ones128 = p.sb("ones128", [128, 128])
    p.b_ident = p.sb("identb", [128, 128], BF16)
    p.b_cmask = p.sb("cmaskb", [128, SEGL])
    p.b_czmask = p.sb("czmaskb", [128, NCORES])
    p.b_czmaskc = p.sb("czmaskc", [128, NCORES])
    p.b_sq = p.sb("sq_scr", [128, 4096])
    p.b_tmp = p.sb("tmp_scr", [128, 2048])
    p.b_rep = p.sb("rep", [128, 1024])
    p.b_str = p.sb("strow", [128, 512])
    p.b_str2 = p.sb("strow2", [1, 512])
    p.b_rcol = p.sb("rcol", [128, 48])
    p.b_trin = p.sb("trin", [128, 1024])
    p.b_identf = p.sb("identf", [128, 128])
    p.b_gam = p.sb("gam", [128, NCH]); p.b_bet = p.sb("bet", [128, NCH])
    p.b_nw = p.sb("normw", [128, NCH]); p.b_nb = p.sb("normb", [128, NCH])
    p.b_qsc = p.sb("qsc", [128, 4])
    p.b_cid = p.sb("cidb", [1, 1], mybir.dt.int32)
    p.b_czero = p.sb("czerob", [128, 1])
    p.b_qrow = p.sb("qrow", [64, 8])

    p.ps_mm = p.psb("ps_mm", [128, 512])
    p.ps_mm2 = p.psb("ps_mm2", [128, 512])

    p.s_ld = sem("s_ld"); p.s_wl = sem("s_wl")
    p.s_mm = sem("s_mm"); p.s_v = sem("s_v"); p.s_s = sem("s_s")
    p.s_g = sem("s_g"); p.s_st = sem("s_st"); p.s_cc = sem("s_cc")
    p.s_wg = sem("s_wg")
    p.s_st2 = sem("s_st2")
    p.s_sty = sem("s_sty")
    p.s_wv = [sem(f"s_wv{i}") for i in range(4)]
    p.s_br = sem("s_br")

    r = Rec()
    p.r = r

    stage_loads(p)
    with_scope = lambda f, *a: f(p, *a)
    if "attn_a" in stages:
        stage_attn_a(p)
    if "ffn1" in stages:
        stage_ffn(p, "ffn1")
    if "attn_b" in stages:
        stage_attn_b(p)
    if "ffn2" in stages:
        stage_ffn(p, "ffn2")
    stage_output(p)

    with nc.Block() as block:
        r.play(block)
    p.exit_stack.close()
    return nc, ["outT"] + p.dbg_outs


def dbg_tap(p, name, sb_ap, shape, dt=F32):
    if name not in p.debug_taps:
        return
    r = p.r
    d = p.nc.dram_tensor("dbg_" + name, list(shape), dt, kind="ExternalOutput").ap()
    p.dbg_outs.append("dbg_" + name)
    r.barrier(p, engs=("g",))
    r.inc("g", p.s_st, lambda g, d=d, a=sb_ap: g.dma_start(out=d[:], in_=a), 16)
    r.barrier(p)


def stage_loads(p):
    r = p.r
    x3 = p.d_x.rearrange("(c p) t -> c p t", p=128)
    for c in range(NCH):
        r.inc("y", p.s_ld, lambda y, c=c, x3=x3: y.dma_start(
            out=p.b_x[:, c * TL:(c + 1) * TL], in_=x3[c]), 16)
    r.inc("y", p.s_ld, lambda y: y.dma_start(out=p.b_ident[:], in_=p.d_ident[:]), 16)
    r.inc("y", p.s_ld, lambda y: y.dma_start(out=p.b_identf[:], in_=p.d_identf[:]), 16)
    r.inc("y", p.s_ld, lambda y: y.dma_start(out=p.b_cmask[:], in_=p.d_cmask[:]), 16)
    r.inc("y", p.s_ld, lambda y: y.dma_start(out=p.b_czmask[:], in_=p.d_czmask[:]), 16)
    for dst, srcv in ((p.b_gam, p.d_gam), (p.b_bet, p.d_bet),
                      (p.b_nw, p.d_norm_w), (p.b_nb, p.d_norm_b)):
        v3 = srcv.rearrange("o (c p) -> p c o", p=128)
        r.inc("y", p.s_ld,
              lambda y, dst=dst, v3=v3: y.dma_start(
                  out=dst[:].unsqueeze(2), in_=v3,
                  allow_slow_non_contiguous=True), 16)
    r.v(lambda v: v.memset(p.b_consts[:, 0:1], LN_EPS))
    r.v(lambda v: v.memset(p.b_consts[:, 1:2], 1.0))
    r.v(lambda v: v.memset(p.b_consts[:, 2:3], 0.0))
    r.v(lambda v: v.memset(p.b_consts[:, 3:4], -0.5))
    r.v(lambda v: v.memset(p.b_consts[:, 4:5], ATTN_EPS))
    r.v(lambda v: v.memset(p.b_consts[:, 5:6], -1.0))
    r.v(lambda v: v.memset(p.b_consts[:, 6:16], 0.0))
    r.wait("v", p.s_ld)
    r.inc("v", p.s_v, lambda v: v.tensor_scalar(
        p.b_czmaskc[:], p.b_czmask[:], -1.0, 1.0, ALU.mult, ALU.add))
    r.inc("v", p.s_v, lambda v: v.memset(p.b_ones1[:], 1.0))
    r.inc("v", p.s_v, lambda v: v.memset(p.b_ones128[:], 1.0))
    r.wait("y", p.s_ld)
    r.inc("y", p.s_ld, lambda y: y.dma_start(out=p.b_cid[:], in_=p.d_cid[:]), 16)
    r.inc("y", p.s_ld, lambda y: y.dma_start(out=p.b_czero[:], in_=p.d_czero[:]), 16)
    for e in ("t", "v", "s", "g", "y"):
        r.wait(e, p.s_ld)
        r.wait(e, p.s_v)
    def _load_cid_y(y):
        p.cid_y = p.nc.values_load(p.b_cid[0:1, 0:1], engines=[mybir.EngineType.SP])
        p.cy_blk = [p.cid_y * 2, p.cid_y * 2 + 1]
        p.cidp_y = (p.cid_y + NCORES - 1) % NCORES
    def _load_cid_g(g):
        p.cid_g = p.nc.values_load(p.b_cid[0:1, 0:1], engines=[mybir.EngineType.Pool])
        p.cidp_g = (p.cid_g + NCORES - 1) % NCORES
    r.y(_load_cid_y)
    r.g(_load_cid_g)


def emit_ln(p, x_view, dst_view, gam=None, bet=None, double=False, dst2_view=None,
            add_into=None):
    r, s_mm, s_v, s_s, s_g = p.r, p.s_mm, p.s_v, p.s_s, p.s_g
    need_p = add_into is not None
    emit_ln.ctr = getattr(emit_ln, "ctr", 0) + 1
    _c = emit_ln.ctr
    with p.nc.psum_tensor(f"ps_stat{_c}", [64, 512], F32) as lstat, \
         p.nc.psum_tensor(f"ps_tr{_c}", [128, 128], F32) as ltr, \
         p.nc.psum_tensor(f"ps_rep{_c}", [128, 1024], F32) as lrep:
        _emit_ln_body(p, x_view, dst_view, gam, bet, double, dst2_view, add_into,
                      lstat, ltr, lrep)


def _emit_ln_body(p, x_view, dst_view, gam, bet, double, dst2_view, add_into,
                  ps_stat_l, ps_tr_l, ps_rep_l):
    r, s_mm, s_v, s_s, s_g = p.r, p.s_mm, p.s_v, p.s_s, p.s_g
    need_p = add_into is not None
    for it in range(TL // 512):
        sl = slice(it * 512, (it + 1) * 512)
        sq0 = p.s_sq_base = 0
        for c in range(NCH):
            r.inc("s", s_s, lambda s, c=c, sl=sl: s.activation(
                out=p.b_sq[:, c * 512:(c + 1) * 512], in_=x_view[c][:, sl], func=AF.Square))
        r.wait("t", s_s)
        for c in range(NCH):
            r.t(lambda t, c=c, sl=sl: t.matmul(
                ps_stat_l[0:1, 0:512], p.b_ones1[:, 0:1], x_view[c][:, sl],
                start=(c == 0), stop=(c == NCH - 1)))
            fn = lambda t, c=c: t.matmul(
                ps_stat_l[32:33, 0:512], p.b_ones1[:, 0:1], p.b_sq[:, c * 512:(c + 1) * 512],
                start=(c == 0), stop=(c == NCH - 1))
            if c == NCH - 1:
                r.inc("t", s_mm, fn)
            else:
                r.t(fn)
        r.wait("v", s_mm)
        r.v(lambda v: v.tensor_copy(out=p.b_str[0:1, 0:512], in_=ps_stat_l[0:1, 0:512]))
        r.inc("v", s_v, lambda v: v.tensor_copy(out=p.b_str[32:33, 0:512], in_=ps_stat_l[32:33, 0:512]))
        r.wait("t", s_v)
        for j in range(4):
            r.inc("t", s_mm, lambda t, j=j: t.transpose(
                ps_tr_l[:, 0:128], p.b_str[:, j * 128:(j + 1) * 128], p.b_identf[:]))
            r.wait("v", s_mm)
            r.inc("v", s_v, lambda v, j=j: v.tensor_copy(
                out=p.b_rcol[:, j:j + 8:4].unsqueeze(2),
                in_=ps_tr_l[:, 0:33:32].unsqueeze(2)))
            r.wait("t", s_v)
        CM, CV, CR, CM2 = 8, 12, 16, 20
        CRP, CMP, CT = 24, 28, 32
        r.wait("v", s_v)
        r.v(lambda v: v.tensor_scalar(p.b_rcol[:, CM:CM + 4], p.b_rcol[:, 0:4], 1.0 / D, None, ALU.mult))
        r.inc("v", s_v, lambda v: v.tensor_scalar(p.b_rcol[:, CV:CV + 4], p.b_rcol[:, 4:8], 1.0 / D, None, ALU.mult))
        r.wait("v", s_v)
        r.inc("v", s_v, lambda v: v.tensor_tensor(p.b_rcol[:, CT:CT + 4], p.b_rcol[:, CM:CM + 4], p.b_rcol[:, CM:CM + 4], ALU.mult))
        r.wait("v", s_v)
        r.inc("v", s_v, lambda v: v.tensor_tensor(p.b_rcol[:, CV:CV + 4], p.b_rcol[:, CV:CV + 4], p.b_rcol[:, CT:CT + 4], ALU.subtract))
        r.wait("s", s_v)
        r.inc("s", s_s, lambda s: s.activation(out=p.b_rcol[:, CT:CT + 4], in_=p.b_rcol[:, CV:CV + 4], func=AF.Ln,
                                               bias=p.b_consts[:, 0:1], scale=p.b_consts[:, 1:2]))
        r.wait("s", s_s)
        r.inc("s", s_s, lambda s: s.activation(out=p.b_rcol[:, CR:CR + 4], in_=p.b_rcol[:, CT:CT + 4], func=AF.Exp,
                                               bias=p.b_consts[:, 2:3], scale=p.b_consts[:, 3:4]))
        if double:
            r.wait("v", s_s)
            r.inc("v", s_v, lambda v: v.tensor_scalar(p.b_rcol[:, CT:CT + 4], p.b_rcol[:, CV:CV + 4], LN_EPS, None, ALU.add))
            r.wait("v", s_v)
            r.inc("v", s_v, lambda v: v.reciprocal(p.b_rcol[:, CT + 4:CT + 8], p.b_rcol[:, CT:CT + 4]))
            r.wait("v", s_v)
            r.inc("v", s_v, lambda v: v.tensor_tensor(p.b_rcol[:, CT:CT + 4], p.b_rcol[:, CV:CV + 4], p.b_rcol[:, CT + 4:CT + 8], ALU.mult))
            r.wait("s", s_v)
            r.inc("s", s_s, lambda s: s.activation(out=p.b_rcol[:, CT:CT + 4], in_=p.b_rcol[:, CT:CT + 4], func=AF.Ln,
                                                   bias=p.b_consts[:, 0:1], scale=p.b_consts[:, 1:2]))
            r.wait("s", s_s)
            r.inc("s", s_s, lambda s: s.activation(out=p.b_rcol[:, CT:CT + 4], in_=p.b_rcol[:, CT:CT + 4], func=AF.Exp,
                                                   bias=p.b_consts[:, 2:3], scale=p.b_consts[:, 3:4]))
            r.wait("v", s_s)
            if need_p:
                r.inc("v", s_v, lambda v: v.tensor_copy(out=p.b_rcol[:, CRP:CRP + 4], in_=p.b_rcol[:, CR:CR + 4]))
                r.wait("v", s_v)
            r.inc("v", s_v, lambda v: v.tensor_tensor(p.b_rcol[:, CR:CR + 4], p.b_rcol[:, CR:CR + 4], p.b_rcol[:, CT:CT + 4], ALU.mult))
            r.wait("v", s_v)
        else:
            r.wait("v", s_s)
        r.inc("v", s_v, lambda v: v.tensor_tensor(p.b_rcol[:, CM2:CM2 + 4], p.b_rcol[:, CM:CM + 4], p.b_rcol[:, CR:CR + 4], ALU.mult))
        if need_p:
            r.inc("v", s_v, lambda v: v.tensor_tensor(p.b_rcol[:, CMP:CMP + 4], p.b_rcol[:, CM:CM + 4], p.b_rcol[:, CRP:CRP + 4], ALU.mult))
        r.wait("v", s_v)
        r.v(lambda v: v.tensor_copy(out=p.b_trin[:, 0:512:128].unsqueeze(2), in_=p.b_rcol[:, CR:CR + 4].unsqueeze(2)))
        ins_fn = lambda v: v.tensor_copy(out=p.b_trin[:, 32:512:128].unsqueeze(2), in_=p.b_rcol[:, CM2:CM2 + 4].unsqueeze(2))
        if need_p:
            r.v(ins_fn)
            r.v(lambda v: v.tensor_copy(out=p.b_trin[:, 64:512:128].unsqueeze(2), in_=p.b_rcol[:, CRP:CRP + 4].unsqueeze(2)))
            r.inc("v", s_v, lambda v: v.tensor_copy(out=p.b_trin[:, 512:1024:128].unsqueeze(2), in_=p.b_rcol[:, CMP:CMP + 4].unsqueeze(2)))
        else:
            r.inc("v", s_v, ins_fn)
        r.wait("t", s_v)
        for j in range(4):
            r.inc("t", s_mm, lambda t, j=j: t.transpose(
                ps_tr_l[:, 0:128], p.b_trin[:, j * 128:(j + 1) * 128], p.b_identf[:]))
            r.wait("v", s_mm)
            r.v(lambda v, j=j: v.tensor_copy(out=p.b_str[0:1, j * 128:(j + 1) * 128], in_=ps_tr_l[0:1, 0:128]))
            r.v(lambda v, j=j: v.tensor_copy(out=p.b_str[32:33, j * 128:(j + 1) * 128], in_=ps_tr_l[32:33, 0:128]))
            if need_p:
                r.inc("v", s_v, lambda v, j=j: v.tensor_copy(out=p.b_str[64:65, j * 128:(j + 1) * 128], in_=ps_tr_l[64:65, 0:128]))
            else:
                r.inc("v", s_v, lambda v, j=j: v.engine_nop())
            r.wait("t", s_v)
        if need_p:
            for j in range(4):
                r.inc("t", s_mm, lambda t, j=j: t.transpose(
                    ps_tr_l[:, 0:128], p.b_trin[:, 512 + j * 128:512 + (j + 1) * 128], p.b_identf[:]))
                r.wait("v", s_mm)
                r.inc("v", s_v, lambda v, j=j: v.tensor_copy(out=p.b_str2[0:1, j * 128:(j + 1) * 128], in_=ps_tr_l[0:1, 0:128]))
                r.wait("t", s_v)
        def apply_pass(r_base, m_row_ap, m_base, write_cb):
            r.wait("t", s_v)
            r.t(lambda t, sl=sl, r_base=r_base: t.matmul(
                ps_rep_l[:, 0:512], p.b_ones128[r_base:r_base + 1, :],
                p.b_str[r_base:r_base + 1, 0:512], start=True, stop=True))
            r.inc("t", s_mm, lambda t: t.matmul(
                ps_rep_l[:, 512:1024], p.b_ones128[m_base:m_base + 1, :],
                m_row_ap, start=True, stop=True))
            r.wait("v", s_mm)
            r.inc("v", s_v, lambda v: v.tensor_copy(out=p.b_rep[:, 0:1024], in_=ps_rep_l[:, 0:1024]))
            r.wait("v", s_v)
            r.wait("g", s_v)
            for c in range(NCH):
                slot = (c % 2) * 512
                slot2 = 1024 + (c % 2) * 512
                r.wait("v", s_g)
                r.inc("v", s_v, lambda v, c=c, sl=sl, slot=slot: v.tensor_tensor(
                    p.b_tmp[:, slot:slot + 512], x_view[c][:, sl], p.b_rep[:, 0:512], ALU.mult))
                r.wait("g", s_v)
                r.inc("g", s_g, lambda g, slot=slot, slot2=slot2: g.tensor_tensor(
                    p.b_tmp[:, slot2:slot2 + 512], p.b_tmp[:, slot:slot + 512],
                    p.b_rep[:, 512:1024], ALU.subtract))
                r.wait("v", s_g)
                write_cb(c, sl, slot2)
            r.wait("t", s_v)

        def main_write(c, sl2, slot2):
            if gam is None:
                fn = lambda v, c=c, sl2=sl2, slot2=slot2: v.tensor_copy(
                    out=dst_view[c][:, sl2], in_=p.b_tmp[:, slot2:slot2 + 512])
            else:
                fn = lambda v, c=c, sl2=sl2, slot2=slot2: v.tensor_scalar(
                    dst_view[c][:, sl2], p.b_tmp[:, slot2:slot2 + 512],
                    gam[:, c:c + 1], bet[:, c:c + 1], ALU.mult, ALU.add)
            r.inc("v", s_v, fn)
            if dst2_view is not None:
                if gam is None:
                    fn2 = lambda v, c=c, sl2=sl2, slot2=slot2: v.tensor_copy(
                        out=dst2_view[c][:, sl2], in_=p.b_tmp[:, slot2:slot2 + 512])
                else:
                    fn2 = lambda v, c=c, sl2=sl2, slot2=slot2: v.tensor_scalar(
                        dst2_view[c][:, sl2], p.b_tmp[:, slot2:slot2 + 512],
                        gam[:, c:c + 1], bet[:, c:c + 1], ALU.mult, ALU.add)
                r.inc("v", s_v, fn2)

        apply_pass(0, p.b_str[32:33, 0:512], 32, main_write)
        if need_p:
            def add_write(c, sl2, slot2):
                r.inc("v", s_v, lambda v, c=c, sl2=sl2, slot2=slot2: v.tensor_tensor(
                    add_into[c][:, sl2], add_into[c][:, sl2],
                    p.b_tmp[:, slot2:slot2 + 512], ALU.add))
            apply_pass(64, p.b_str2[0:1, 0:512], 0, add_write)


def stage_ffn_wq(p, wq_buf):
    r = p.r
    p.b_wq = wq_buf
    wqb = chunk_views(p.b_wq, NCH, D)
    fw3 = p.d_fw.rearrange("(c p) o -> c p o", p=128)
    stage_ffn_wq.ctr = getattr(stage_ffn_wq, "ctr", 0) + 1
    _c = stage_ffn_wq.ctr
    with p.nc.sbuf_tensor(f"fw32_{_c}", [128, NCH * D], F32) as fw32, \
         p.nc.psum_tensor(f"ps_wqstat{_c}", [64, 512], F32) as ps_stat_l:
        fb = chunk_views(fw32, NCH, D)
        for c in range(NCH):
            r.inc("y", p.s_wl, lambda y, c=c, fb=fb, fw3=fw3: y.dma_start(out=fb[c], in_=fw3[c]), 16)
        r.wait("s", p.s_wl)
        for c in range(NCH):
            pp = (c % 2) * 2048
            if c >= 2:
                r.wait("s", p.s_mm, p.s_mm.n - 1)
            r.inc("s", p.s_s, lambda s, c=c, pp=pp, fb=fb: s.activation(
                out=p.b_sq[:, pp:pp + 1024], in_=fb[c], func=AF.Abs))
            r.wait("t", p.s_s)
            r.t(lambda t, c=c, pp=pp: t.matmul(
                ps_stat_l[0:1, 0:512], p.b_ones1[:, 0:1], p.b_sq[:, pp:pp + 512],
                start=(c == 0), stop=False))
            r.inc("t", p.s_mm, lambda t, c=c, pp=pp: t.matmul(
                ps_stat_l[0:1, 0:512], p.b_ones1[:, 0:1], p.b_sq[:, pp + 512:pp + 1024],
                start=False, stop=(c == NCH - 1)))
        r.wait("v", p.s_mm)
        r.inc("v", p.s_v, lambda v: v.tensor_reduce(p.b_qrow[0:1, 0:1], ps_stat_l[0:1, 0:512], AX.X, ALU.add))
        r.wait("v", p.s_v)
        r.inc("v", p.s_v, lambda v: v.tensor_scalar(
            p.b_qrow[0:1, 1:2], p.b_qrow[0:1, 0:1], THRESH / (D * D), None, ALU.mult))
        r.wait("t", p.s_v)
        r.inc("t", p.s_mm, lambda t: t.matmul(
            p.ps_mm[:, 0:1], p.b_ones128[0:1, :], p.b_qrow[0:1, 1:2], start=True, stop=True))
        r.wait("v", p.s_mm)
        r.inc("v", p.s_v, lambda v: v.tensor_copy(out=p.b_qsc[:, 3:4], in_=p.ps_mm[:, 0:1]))
        r.wait("s", p.s_v)
        for c in range(NCH):
            pp = (c % 2) * 2048
            if c >= 2:
                r.wait("s", p.s_mm, p.s_mm.n - 1)
            r.inc("s", p.s_s, lambda s, c=c, pp=pp, fb=fb: s.activation(
                out=p.b_sq[:, pp:pp + 1024], in_=fb[c], func=AF.Abs))
            r.wait("v", p.s_s)
            qq = (c % 2) * 1024
            r.v(lambda v, pp=pp: v.scalar_tensor_tensor(
                p.b_sq[:, pp + 1024:pp + 2048], p.b_sq[:, pp:pp + 1024],
                p.b_qsc[:, 3:4], p.b_sq[:, pp:pp + 1024], ALU.is_gt, ALU.mult))
            r.inc("v", p.s_v, lambda v, pp=pp, qq=qq: v.tensor_scalar(
                p.b_tmp[:, qq:qq + 1024], p.b_sq[:, pp:pp + 1024],
                p.b_qsc[:, 3:4], None, ALU.is_gt))
            r.wait("t", p.s_v)
            r.t(lambda t, c=c, pp=pp: t.matmul(
                ps_stat_l[0:1, 0:512], p.b_ones1[:, 0:1], p.b_sq[:, pp + 1024:pp + 1536],
                start=(c == 0), stop=False))
            r.t(lambda t, c=c, pp=pp: t.matmul(
                ps_stat_l[0:1, 0:512], p.b_ones1[:, 0:1], p.b_sq[:, pp + 1536:pp + 2048],
                start=False, stop=(c == NCH - 1)))
            r.t(lambda t, c=c, qq=qq: t.matmul(
                ps_stat_l[32:33, 0:512], p.b_ones1[:, 0:1], p.b_tmp[:, qq:qq + 512],
                start=(c == 0), stop=False))
            r.inc("t", p.s_mm, lambda t, c=c, qq=qq: t.matmul(
                ps_stat_l[32:33, 0:512], p.b_ones1[:, 0:1], p.b_tmp[:, qq + 512:qq + 1024],
                start=False, stop=(c == NCH - 1)))
        r.wait("v", p.s_mm)
        r.v(lambda v: v.tensor_reduce(p.b_qrow[0:1, 2:3], ps_stat_l[0:1, 0:512], AX.X, ALU.add))
        r.inc("v", p.s_v, lambda v: v.tensor_reduce(p.b_qrow[0:1, 3:4], ps_stat_l[32:33, 0:512], AX.X, ALU.add))
        r.wait("v", p.s_v)
        r.inc("v", p.s_v, lambda v: v.tensor_scalar(p.b_qrow[0:1, 3:4], p.b_qrow[0:1, 3:4], 1.0, None, ALU.max))
        r.wait("v", p.s_v)
        r.inc("v", p.s_v, lambda v: v.reciprocal(p.b_qrow[0:1, 5:6], p.b_qrow[0:1, 3:4]))
        r.wait("v", p.s_v)
        r.inc("v", p.s_v, lambda v: v.tensor_tensor(
            p.b_qrow[0:1, 4:5], p.b_qrow[0:1, 2:3], p.b_qrow[0:1, 5:6], ALU.mult))
        r.wait("t", p.s_v)
        r.inc("t", p.s_mm, lambda t: t.matmul(
            p.ps_mm[:, 1:2], p.b_ones128[0:1, :], p.b_qrow[0:1, 4:5], start=True, stop=True))
        r.wait("v", p.s_mm)
        r.inc("v", p.s_v, lambda v: v.tensor_copy(out=p.b_qsc[:, 2:3], in_=p.ps_mm[:, 1:2]))
        r.wait("s", p.s_v)
        for c in range(NCH):
            pp = (c % 2) * 2048
            if c >= 2:
                r.wait("s", p.s_v, p.s_v.n - 1)
            r.inc("s", p.s_s, lambda s, c=c, pp=pp, fb=fb: s.activation(
                out=p.b_sq[:, pp:pp + 1024], in_=fb[c], func=AF.Sign))
            r.inc("s", p.s_s, lambda s, c=c, pp=pp, fb=fb: s.activation(
                out=p.b_sq[:, pp + 1024:pp + 2048], in_=fb[c], func=AF.Abs))
            r.wait("g", p.s_s)
            r.inc("g", p.s_g, lambda g, pp=pp: g.tensor_scalar(
                p.b_sq[:, pp:pp + 1024], p.b_sq[:, pp:pp + 1024],
                p.b_qsc[:, 2:3], None, ALU.mult))
            r.wait("v", p.s_g)
            r.inc("v", p.s_v, lambda v, c=c, pp=pp, wqb=wqb: v.scalar_tensor_tensor(
                wqb[c], p.b_sq[:, pp + 1024:pp + 2048], p.b_qsc[:, 3:4],
                p.b_sq[:, pp:pp + 1024], ALU.is_gt, ALU.mult))
        dbg_tap(p, "wq", p.b_wq[:], [128, NCH * D], BF16)
        r.barrier(p)


def stage_ffn(p, tag):
    r = p.r
    with p.nc.sbuf_tensor(tag + "_wq", [128, NCH * D], BF16) as wqbuf:
        if tag == "ffn1":
            stage_ffn_wq(p, wqbuf)
            r.wait("y", p.s_sty)
            r.inc("y", p.s_sty, lambda y, wqbuf=wqbuf: y.dma_start(
                out=p.dr_wq[:], in_=wqbuf[:]), 16)
            p.b_wq = wqbuf
        else:
            p.b_wq = wqbuf
            r.wait("y", p.s_ld)
            r.inc("y", p.s_ld, lambda y, wqbuf=wqbuf: y.dma_start(
                out=wqbuf[:], in_=p.dr_wq[:]), 16)
            for e in ("t", "v", "s", "g"):
                r.wait(e, p.s_ld)
        _stage_ffn_body(p, tag, wqbuf)


def _stage_ffn_body(p, tag, wqbuf):
    r = p.r
    with p.nc.sbuf_tensor(tag + "_xn", [128, NCH * TL], F32) as xf, \
         p.nc.sbuf_tensor(tag + "_xq", [128, NCH * (TL // 2)], BF16) as xq:
        xv = chunk_views(p.b_x, NCH, TL)
        xfv = chunk_views(xf, NCH, TL)
        emit_ln(p, xv, xfv, gam=p.b_gam, bet=p.b_bet, double=True)
        dbg_tap(p, tag + "_xn", xf[:], [128, NCH * TL])
        r.wait("v", p.s_v)
        r.inc("v", p.s_v, lambda v: v.tensor_reduce(
            p.b_qsc[:, 0:1], xf[:], AX.X, ALU.max, apply_absolute_value=True))
        r.wait("g", p.s_v)
        r.inc("g", p.s_st, lambda g: g.dma_start(
            out=p.cc_mx_in[:], in_=p.b_qsc[:, 0:1]), 16)
        r.wait("g", p.s_st)
        r.inc("g", p.s_cc, lambda g: g.collective_compute(
            "AllGather", ALU.bypass, ins=[p.cc_mx_in[:]], outs=[p.cc_mx_out[:]],
            replica_groups=RG))
        r.wait("g", p.s_cc)
        r.inc("g", p.s_st, lambda g: g.dma_start(
            out=p.b_tmp[0:1, 0:NCORES * 128],
            in_=p.cc_mx_out[:].rearrange("(a b) o -> a (b o)", a=1)), 16)
        r.wait("v", p.s_st)
        r.inc("v", p.s_v, lambda v: v.tensor_reduce(
            p.b_qrow[0:1, 5:6], p.b_tmp[0:1, 0:NCORES * 128], AX.X, ALU.max))
        r.wait("v", p.s_v)
        r.inc("v", p.s_v, lambda v: v.tensor_scalar(p.b_qrow[0:1, 6:7], p.b_qrow[0:1, 5:6],
                                      1e-8, 1.0 / QMAX, ALU.max, ALU.mult))
        r.wait("v", p.s_v)
        r.inc("v", p.s_v, lambda v: v.reciprocal(p.b_qrow[0:1, 7:8], p.b_qrow[0:1, 6:7]))
        r.wait("t", p.s_v)
        r.t(lambda t: t.matmul(p.ps_mm[:, 2:3], p.b_ones128[0:1, :],
                               p.b_qrow[0:1, 6:7], start=True, stop=True))
        r.inc("t", p.s_mm, lambda t: t.matmul(
            p.ps_mm[:, 3:4], p.b_ones128[0:1, :], p.b_qrow[0:1, 7:8],
            start=True, stop=True))
        r.wait("v", p.s_mm)
        r.v(lambda v: v.tensor_copy(out=p.b_qsc[:, 1:2], in_=p.ps_mm[:, 2:3]))
        r.inc("v", p.s_v, lambda v: v.tensor_copy(out=p.b_qsc[:, 0:1], in_=p.ps_mm[:, 3:4]))
        wqb = chunk_views(p.b_wq, NCH, D)
        HT = TL // 2
        xqv = chunk_views(xq, NCH, HT)
        for hf in range(2):
            t0 = hf * HT
            for c in range(NCH):
                sA = (c % 2) * 1024
                sB = 2048 + (c % 2) * 1024
                r.wait("v", p.s_v, p.s_v.n)
                r.inc("v", p.s_v, lambda v, c=c, t0=t0, sA=sA: v.tensor_scalar(
                    p.b_sq[:, sA:sA + HT], xfv[c][:, t0:t0 + HT],
                    p.b_qsc[:, 0:1], QMAX, ALU.mult, ALU.min))
                r.wait("v", p.s_v)
                r.inc("v", p.s_v, lambda v, sA=sA, sB=sB: v.tensor_scalar(
                    p.b_sq[:, sB:sB + HT], p.b_sq[:, sA:sA + HT], -QMAX, MAGIC, ALU.max, ALU.add))
                r.wait("v", p.s_v)
                fn = lambda v, c=c, sB=sB: v.tensor_scalar(
                    xqv[c], p.b_sq[:, sB:sB + HT], -MAGIC, p.b_qsc[:, 1:2], ALU.add, ALU.mult)
                r.inc("v", p.s_v, fn)
            r.wait("t", p.s_v)
            for oc in range(NCH):
                for it in range(HT // 512):
                    sl = slice(it * 512, (it + 1) * 512)
                    gsl = slice(t0 + it * 512, t0 + (it + 1) * 512)
                    for kc in range(NCH):
                        fn = lambda t, oc=oc, kc=kc, sl=sl: t.matmul(
                            p.ps_mm[:, 0:512], wqb[kc][:, oc * 128:(oc + 1) * 128],
                            xqv[kc][:, sl], start=(kc == 0), stop=(kc == NCH - 1))
                        if kc == NCH - 1:
                            r.inc("t", p.s_mm, fn)
                        else:
                            r.t(fn)
                    r.wait("v", p.s_mm)
                    r.inc("v", p.s_v, lambda v, oc=oc, gsl=gsl: v.tensor_tensor(
                        xv[oc][:, gsl], xv[oc][:, gsl], p.ps_mm[:, 0:512], ALU.add))
                    r.wait("t", p.s_v)
                    if tag == "ffn2" and hf == 1 and it == HT // 512 - 1:
                        o3s = p.d_out.rearrange("(c p) t -> c p t", p=128)
                        r.wait("g", p.s_v)
                        r.inc("g", p.s_st, lambda g, oc=oc, o3s=o3s: g.dma_start(
                            out=o3s[oc], in_=p.b_x[:, oc * TL:(oc + 1) * TL]), 16)
        r.barrier(p)


def stage_output(p):
    r = p.r
    if "ffn2" in p.stages:
        r.barrier(p)
        return
    o3 = p.d_out.rearrange("(c p) t -> c p t", p=128)
    r.barrier(p, engs=("g",))
    for c in range(NCH):
        r.inc("g", p.s_st, lambda g, c=c: g.dma_start(
            out=o3[c], in_=p.b_x[:, c * TL:(c + 1) * TL]), 16)
    r.wait("g", p.s_st)



def make_host_inputs(inputs):
    x = np.asarray(inputs["L0_attn_a_x"])[0]
    xT = np_t(x)
    ident = np.eye(128, dtype=ml_dtypes.bfloat16)
    cm1 = np.triu(np.full((SEGL, SEGL), -1e30, dtype=np.float32), k=1)
    cmask = np.ascontiguousarray(np.vstack([cm1, cm1]))
    base = {
        "wqT": np_t(inputs["L0_attn_a_W_q"]), "wkT": np_t(inputs["L0_attn_a_W_k"]),
        "wvT": np_t(inputs["L0_attn_a_W_v"]), "woT": np_t(inputs["L0_attn_a_W_o"]),
        "beta_param": np.asarray(inputs["L0_attn_a_beta_param"], np.float32).reshape(1, H),
        "norm_w": np.asarray(inputs["L0_attn_b_norm_w"], np.float32).reshape(1, D),
        "norm_b": np.asarray(inputs["L0_attn_b_norm_b"], np.float32).reshape(1, D),
        "winT": np_t(inputs["L0_attn_b_W_in"]),
        "conv_w": np.asarray(inputs["L0_attn_b_conv_w"], np.float32),
        "conv_b": np.asarray(inputs["L0_attn_b_conv_b"], np.float32).reshape(1, DI),
        "wdtT": np_t(inputs["L0_attn_b_W_dt"]),
        "b_dt": np.asarray(inputs["L0_attn_b_b_dt"], np.float32).reshape(1, DI),
        "wBT": np_t(inputs["L0_attn_b_W_B"]), "wCT": np_t(inputs["L0_attn_b_W_C"]),
        "Dvec": np.asarray(inputs["L0_attn_b_D"], np.float32).reshape(1, DI),
        "A_log": np.asarray(inputs["L0_attn_b_A_log"], np.float32),
        "woutT": np_t(inputs["L0_attn_b_W_out"]),
        "fwT": np_t(inputs["L0_ffn_weight"]),
        "gamma": np.asarray(inputs["L0_ffn_gamma"], np.float32).reshape(1, D),
        "beta": np.asarray(inputs["L0_ffn_beta"], np.float32).reshape(1, D),
        "identity": ident, "identityf": np.eye(128, dtype=np.float32), "cmask": cmask,
    }
    in_maps = []
    for c in range(NCORES):
        m = dict(base)
        m["xT"] = np.ascontiguousarray(xT[:, c * TL:(c + 1) * TL])
        m["cid"] = np.array([[c]], dtype=np.int32)
        m["czero"] = np.full((128, 1), 0.0 if c == 0 else 1.0, dtype=np.float32)
        zm = np.zeros((128, NCORES), dtype=np.float32)
        zm[:, :c] = 1.0
        m["czmask"] = zm
        in_maps.append(m)
    return in_maps


def assemble_output(results):
    outT = np.concatenate([r["outT"] for r in results], axis=1)
    return np.ascontiguousarray(outT.T)[None]



NPAIR = H // 2


class _nullcm:
    def __init__(self, v):
        self.v = v
    def __enter__(self):
        return self.v
    def __exit__(self, *a):
        return False
HT = TL // 2
NSEG_H = NSEG // 2
ISQRT_DK = 1.0 / np.sqrt(DK)


def stage_attn_a(p):
    r = p.r
    nc = p.nc
    PH = 99

    def dint(name, shape, dt=F32):
        return nc.dram_tensor(name, list(shape), dt, kind="Internal").ap()

    dr_sqskf = dint("dr_sqskf", [NSEG, 128, 1024], BF16)
    dr_pt = dint("dr_pt", [NSEG, 64, 1024], BF16)
    dr_vtm = dint("dr_vtm", [NSEG, 64, 1024], BF16)
    dr_asm = dint("dr_asm", [NSEG, 64, 1024], BF16)
    cc_z_in = dint("cc_z_in", [128, 8])
    cc_z_out = dint("cc_z_out", [NCORES * 128, 8])
    cc_mg_in = dint("cc_mg_in", [2 * 128, 1024], BF16)
    cc_mg_out = dint("cc_mg_out", [NCORES * 2 * 128, 1024], BF16)

    es = contextlib.ExitStack()
    sbl = lambda name, shape, dt=F32: es.enter_context(
        nc.sbuf_tensor(name, list(shape), dt))
    p.b_zss = sbl("zss", [128, NCH * NSEG])
    p.b_zs2 = sbl("zs2", [128, NCH * NCORES])
    p.b_zall = sbl("zall", [128, NPAIR * (NSEG + 1) * 2], BF16)
    p.b_rz = sbl("rzall", [128, H * NSEG])
    p.b_coef = sbl("coef", [128, 64])
    p.b_cm8 = sbl("cm8", [128, 8 * SEGL])
    p.b_vring = sbl("vring", [128, 2 * 1024], BF16)

    r.wait("v", p.s_ld)
    for pr in range(NPAIR):
        fn = lambda v, pr=pr: v.tensor_copy(
            out=p.b_cm8[:, pr * SEGL:(pr + 1) * SEGL], in_=p.b_cmask[:])
        if pr == NPAIR - 1:
            r.inc("v", p.s_v, fn)
        else:
            r.v(fn)

    with nc.sbuf_tensor("xn1", [128, NCH * TL], BF16) as xn1:
        xv = chunk_views(p.b_x, NCH, TL)
        xn1v = chunk_views(xn1, NCH, TL)
        r.barrier(p)
        emit_ln(p, xv, xn1v, double=False)
        dbg_tap(p, "xn1", xn1[:], [128, NCH * TL], BF16)
        r.barrier(p)
        if PH >= 1:
            _beta_coefs(p)
        for hf in range(2):
            if PH >= 2 + hf:
                _prep_half(p, hf, xn1v, dr_sqskf, dr_pt)
        if PH >= 4:
            _vproj(p, xn1v, dr_vtm)
        r.barrier(p)
    if PH >= 5:
        _zstage(p, cc_z_in, cc_z_out)
    r.barrier(p)

    es2 = contextlib.ExitStack()
    sb2 = lambda name, shape, dt=BF16: es2.enter_context(
        nc.sbuf_tensor(name, list(shape), dt))
    a_fm = sb2("a_fm", [128, NCH * TL])
    p.b_sring = sb2("sring", [128, 2 * 1024])
    p.b_ptv = sb2("ptv", [128, 2 * 1024])
    p.b_aring = sb2("aring", [128, 2 * 1024])
    p.b_vdu = sb2("vdu", [128, 2048])
    p.b_skt = sb2("sktb", [128, 1024])
    p.b_mg = sb2("mgbuf", [128, 2048])
    p.b_mgf = sb2("mgf", [128, 2048], F32)
    p.b_mst = sb2("mstbuf", [128, 1024])
    p.b_w = sb2("wbuf", [128, 1024])
    p.b_wf = sb2("wf", [128, 1024], F32)
    p.b_gt = sb2("gtbuf", [128, 1024])
    p.b_gtid = sb2("gtid", [128, 1024])
    p.b_mgag = sb2("mgag", [128, 2048])
    with nc.psum_tensor("ps_mg", [128, 1024], F32) as ps_mg, \
         nc.psum_tensor("ps_cba", [128, 1024], F32) as cba, \
         nc.psum_tensor("ps_cbb", [128, 1024], F32) as cbb:
        if PH >= 6:
            _chain(p, dr_sqskf, dr_pt, dr_vtm, dr_asm, ps_mg, cba, cbb)
        r.barrier(p)
        if PH >= 7:
            _mg_exchange(p, cc_mg_in, cc_mg_out, cba, cbb)
        r.barrier(p)
        if PH >= 8:
            _rerun(p, dr_sqskf, dr_asm, a_fm, ps_mg, cba, cbb)
        r.barrier(p)
        dbg_tap(p, "a_fm", a_fm[:], [128, NCH * TL], BF16)
        afmv = chunk_views(a_fm, NCH, TL)
        xvr = chunk_views(p.b_x, NCH, TL)

        def wo_out(j, it, psum):
            sl = slice(it * 512, (it + 1) * 512)
            r.inc("v", p.s_v, lambda v, j=j, sl=sl, psum=psum: v.tensor_tensor(
                xvr[j][:, sl], xvr[j][:, sl], psum[:, 0:512], ALU.add))
        if PH >= 9:
            _proj_fm(p, p.d_wo, afmv, TL, wo_out)
    r.barrier(p)
    es2.close()
    es.close()


def _beta_coefs(p):
    r = p.r
    r.wait("y", p.s_ld)
    r.inc("y", p.s_ld, lambda y: y.dma_start(
        out=p.b_coef[0:1, 48:64], in_=p.d_beta[:]), 16)
    r.wait("s", p.s_ld)
    r.inc("s", p.s_s, lambda s: s.activation(
        out=p.b_coef[0:1, 32:48], in_=p.b_coef[0:1, 48:64], func=AF.Exp,
        bias=p.b_consts[0:1, 2:3], scale=p.b_consts[0:1, 5:6]))
    r.wait("v", p.s_s)
    r.inc("v", p.s_v, lambda v: v.tensor_scalar(
        p.b_coef[0:1, 32:48], p.b_coef[0:1, 32:48], 1.0, None, ALU.add))
    r.wait("v", p.s_v)
    r.inc("v", p.s_v, lambda v: v.reciprocal(
        p.b_coef[0:1, 48:64], p.b_coef[0:1, 32:48]))
    r.wait("t", p.s_v)
    r.inc("t", p.s_mm, lambda t: t.matmul(
        p.ps_mm[:, 0:16], p.b_ones128[0:1, :], p.b_coef[0:1, 48:64],
        start=True, stop=True))
    r.wait("v", p.s_mm)
    r.inc("v", p.s_v, lambda v: v.tensor_copy(
        out=p.b_coef[:, 0:16], in_=p.ps_mm[:, 0:16]))
    r.wait("v", p.s_v)
    r.inc("v", p.s_v, lambda v: v.tensor_scalar(
        p.b_coef[:, 16:32], p.b_coef[:, 0:16], -1.0, 1.0, ALU.mult, ALU.add))
    r.wait("t", p.s_v)


def _proj_fm(p, wsrc, xviews, ntok, out_cb, wcol=None):
    r = p.r
    ng = ntok // 512
    mm0 = p.s_mm.n
    evlev = []
    ext = wcol is not None
    _proj_fm.ctr = getattr(_proj_fm, "ctr", 0) + 1
    with (p.nc.sbuf_tensor(f"wcolp{_proj_fm.ctr}", [128, 4 * 1024], BF16)
          if not ext else _nullcm(wcol)) as wcol:
        if ext:
            r.wait("g", p.s_mm)
            r.wait("t", p.s_v)

        def dma(g, j):
            slot = j % 4
            srcj = wsrc[:, j * 128:(j + 1) * 128].rearrange(
                "(kc q) o -> q kc o", q=128)
            return g.dma_start(
                out=wcol[:, slot * 1024:(slot + 1) * 1024].rearrange(
                    "q (kc o) -> q kc o", kc=NCH), in_=srcj)

        def throttle(lo):
            jmax = lo + 1
            if jmax >= 4:
                r.wait("g", p.s_mm, mm0 + (jmax - 3) * ng)
        wf = wave_load(p, NCH, dma, throttle=throttle)
        for j in range(NCH):
            slot = j % 4
            r.wait("t", *wf[j])
            for it in range(ng):
                sl = slice(it * 512, (it + 1) * 512)
                g_i = j * ng + it
                psum = p.ps_mm if g_i % 2 == 0 else p.ps_mm2
                if g_i >= 2:
                    r.wait("t", p.s_v, evlev[g_i - 2])
                for kc in range(NCH):
                    fn = lambda t, kc=kc, sl=sl, slot=slot, psum=psum: t.matmul(
                        psum[:, 0:512],
                        wcol[:, slot * 1024 + kc * 128:slot * 1024 + (kc + 1) * 128],
                        xviews[kc][:, sl], start=(kc == 0), stop=(kc == NCH - 1))
                    if kc == NCH - 1:
                        r.inc("t", p.s_mm, fn)
                    else:
                        r.t(fn)
                r.wait("v", p.s_mm)
                out_cb(j, it, psum)
                evlev.append(p.s_v.n)
        if not ext:
            r.barrier(p)


def _prep_half(p, hf, xn1v, dr_sqskf, dr_pt):
    r = p.r
    nc = p.nc
    s0 = hf * NSEG_H
    with nc.sbuf_tensor(f"qf{hf}", [128, NCH * HT], BF16) as qf, \
         nc.sbuf_tensor(f"kf{hf}", [128, NCH * HT], BF16) as kf, \
         nc.sbuf_tensor(f"sqs{hf}", [128, 2048], BF16) as sqs, \
         nc.sbuf_tensor(f"scr{hf}", [128, 1024], F32) as scr, \
         nc.sbuf_tensor(f"scrb{hf}", [128, 1024], BF16) as scrb, \
         nc.sbuf_tensor(f"scrb2{hf}", [128, 1024], BF16) as scrb2, \
         nc.sbuf_tensor(f"pstg{hf}", [128, 2048], BF16) as pstg, \
         nc.sbuf_tensor(f"ptsb{hf}", [128, 2048], BF16) as ptsb, \
         nc.sbuf_tensor(f"smrow{hf}", [128, 32], F32) as smrow:
        qfv = chunk_views(qf, NCH, HT)
        kfv = chunk_views(kf, NCH, HT)
        x_half = [xc[:, hf * HT:(hf + 1) * HT] for xc in xn1v]

        def q_out(j, it, psum):
            r.inc("v", p.s_v, lambda v, j=j, it=it, psum=psum: v.tensor_copy(
                out=qfv[j][:, it * 512:(it + 1) * 512], in_=psum[:, 0:512]))

        def k_out(j, it, psum):
            r.inc("v", p.s_v, lambda v, j=j, it=it, psum=psum: v.tensor_copy(
                out=kfv[j][:, it * 512:(it + 1) * 512], in_=psum[:, 0:512]))
        with nc.sbuf_tensor(f"wqk{hf}", [128, 4 * 1024], BF16) as wqk:
            _proj_fm(p, p.d_wq, x_half, HT, q_out, wcol=wqk)
            _proj_fm(p, p.d_wk, x_half, HT, k_out, wcol=wqk)
            r.barrier(p)

        stylev = {}
        for c in range(NCH):
            base = 0
            sq3 = sqs[:].rearrange("q (s w) -> q s w", w=128)
            if c >= 1:
                r.wait("v", p.s_sty, stylev[c - 1])
            for src, dlo, dhi in ((qfv[c], 0, 64), (kfv[c], 64, 128)):
                r.wait("v", p.s_s)
                r.inc("v", p.s_v, lambda v, src=src: v.tensor_scalar(
                    scr[:, 0:1024], src, 0.0, None, ALU.min))
                r.wait("s", p.s_v)
                r.inc("s", p.s_s, lambda s: s.activation(
                    out=scrb[:], in_=scr[:, 0:1024], func=AF.Exp))
                r.wait("v", p.s_s)
                r.inc("v", p.s_v, lambda v, src=src, sq3=sq3, dlo=dlo, dhi=dhi:
                      v.scalar_tensor_tensor(
                          sq3[:, :, dlo:dhi],
                          src.rearrange("q (s w) -> q s w", w=64),
                          p.b_consts[:, 2:3],
                          scrb[:].rearrange("q (s w) -> q s w", w=64),
                          ALU.max, ALU.add))
            r.wait("v", p.s_v)
            r.inc("v", p.s_v, lambda v, c=c, sq3=sq3: v.tensor_reduce(
                p.b_zss[:, c * NSEG + s0:c * NSEG + s0 + NSEG_H],
                sq3[:, :, 64:128], AX.X, ALU.add))
            r.wait("y", p.s_v)
            r.wait("y", p.s_sty)
            for ls in range(NSEG_H):
                fn = lambda y, c=c, ls=ls, base=base: y.dma_start(
                    out=dr_sqskf[s0 + ls, :, c * 128:(c + 1) * 128],
                    in_=sqs[:, base + ls * 128:base + (ls + 1) * 128])
                r.inc("y", p.s_sty, fn, 16)
            stylev[c] = p.s_sty.n
        r.barrier(p)

        ptlev = {}
        r.wait("v", p.s_v)
        r.inc("v", p.s_v, lambda v: v.memset(pstg[:], 0.0))
        r.wait("v", p.s_v)
        for lp in range(NSEG_H // 2):
            for e in range(2):
                segsl = slice((2 * lp + e) * 64, (2 * lp + e) * 64 + 64)
                psd = p.ps_mm if e == 0 else p.ps_mm2
                for h in range(H):
                    c, par = h // 2, (h % 2) * 64
                    fn = lambda t, c=c, par=par, segsl=segsl, psd=psd: t.matmul(
                        psd[par:par + 64, c * 64:c * 64 + 64],
                        qfv[c][par:par + 64, segsl], kfv[c][par:par + 64, segsl],
                        start=True, stop=True, tile_position=(par, par))
                    if h == H - 1 and e == 1:
                        r.inc("t", p.s_mm, fn)
                    else:
                        r.t(fn)
            r.wait("v", p.s_mm)
            for e, psd in ((0, p.ps_mm), (1, p.ps_mm2)):
                fn = lambda v, e=e, psd=psd: v.tensor_scalar(
                    scr[:, e * 512:e * 512 + 512], psd[:, 0:512], ISQRT_DK,
                    None, ALU.mult)
                if e == 1:
                    r.inc("v", p.s_v, fn)
                else:
                    r.v(fn)
            r.wait("v", p.s_v)
            r.v(lambda v: v.tensor_tensor(
                scr[:, 0:512], scr[:, 0:512], p.b_cm8[:], ALU.add))
            r.inc("v", p.s_v, lambda v: v.tensor_tensor(
                scr[:, 512:1024], scr[:, 512:1024], p.b_cm8[:], ALU.add))
            r.wait("v", p.s_v)
            r.inc("v", p.s_v, lambda v: v.tensor_reduce(
                smrow[:, 0:16],
                scr[:, 0:1024].rearrange("q (pr w) -> q pr w", w=64),
                AX.X, ALU.max))
            r.wait("v", p.s_v)
            r.inc("v", p.s_v, lambda v: v.tensor_tensor(
                scr[:, 0:1024].rearrange("q (pr w) -> q pr w", w=64),
                scr[:, 0:1024].rearrange("q (pr w) -> q pr w", w=64),
                smrow[:, 0:16].unsqueeze(2).broadcast_to([128, 16, 64]),
                ALU.subtract))
            r.wait("s", p.s_v)
            r.inc("s", p.s_s, lambda s: s.activation(
                out=scrb2[:], in_=scr[:, 0:1024], func=AF.Exp))
            r.wait("v", p.s_s)
            r.inc("v", p.s_v, lambda v: v.tensor_reduce(
                smrow[:, 16:32],
                scrb2[:].rearrange("q (pr w) -> q pr w", w=64),
                AX.X, ALU.add))
            r.wait("v", p.s_v)
            r.inc("v", p.s_v, lambda v: v.reciprocal(
                smrow[:, 0:16], smrow[:, 16:32]))
            r.wait("v", p.s_v)
            r.inc("v", p.s_v, lambda v: v.tensor_tensor(
                pstg[:, 0:2048].rearrange("q (pr w) -> q pr w", w=128)[:, :, 64:128],
                scrb2[:].rearrange("q (pr w) -> q pr w", w=64),
                smrow[:, 0:16].unsqueeze(2).broadcast_to([128, 16, 64]),
                ALU.mult))
            if lp >= 1:
                r.wait("v", p.s_sty, ptlev[lp - 1])
            for r4 in range(4):
                r.wait("t", p.s_v)
                for pr in range(4 * r4, 4 * r4 + 4):
                    fn = lambda t, pr=pr: t.transpose(
                        p.ps_mm2.bitcast(BF16)[:, (pr % 4) * 128:(pr % 4) * 128 + 128],
                        pstg[:, pr * 128:(pr + 1) * 128], p.b_ident[:])
                    if pr % 4 == 3:
                        r.inc("t", p.s_mm, fn)
                    else:
                        r.t(fn)
                r.wait("v", p.s_mm)
                r.inc("v", p.s_v, lambda v, r4=r4: v.tensor_copy(
                    out=ptsb[64:128, r4 * 512:(r4 + 1) * 512],
                    in_=p.ps_mm2.bitcast(BF16)[64:128, 0:512]))
            r.wait("y", p.s_v)
            r.wait("y", p.s_sty)
            r.inc("y", p.s_sty, lambda y, lp=lp, s0=s0: y.dma_start(
                out=dr_pt[s0 + 2 * lp, :, :], in_=ptsb[64:128, 0:1024]), 16)
            r.inc("y", p.s_sty, lambda y, lp=lp, s0=s0: y.dma_start(
                out=dr_pt[s0 + 2 * lp + 1, :, :], in_=ptsb[64:128, 1024:2048]), 16)
            ptlev[lp] = p.s_sty.n
        r.barrier(p)


def _vproj(p, xn1v, dr_vtm):
    r = p.r
    nc = p.nc
    with nc.sbuf_tensor("wcolv", [128, 16 * 512], BF16) as wcol:
        r.wait("g", p.s_wg)
        for kc in range(NCH):
            for jh in range(2):
                fn = lambda g, kc=kc, jh=jh: g.dma_start(
                    out=wcol[:, (kc * 2 + jh) * 512:(kc * 2 + jh + 1) * 512],
                    in_=p.d_wv[kc * 128:(kc + 1) * 128, jh * 512:(jh + 1) * 512])
                r.inc("g", p.s_wg, fn, 16)
        r.wait("t", p.s_wg)
        stylev = {}
        evlev = []
        for tt in range(16):
            if tt >= 2:
                r.wait("v", p.s_sty, stylev[tt - 2])
            for jh in range(2):
                g_i = tt * 2 + jh
                psum = p.ps_mm if jh == 0 else p.ps_mm2
                if g_i >= 2:
                    r.wait("t", p.s_v, evlev[g_i - 2])
                for kc in range(NCH):
                    fn = lambda t, kc=kc, jh=jh, tt=tt, psum=psum: t.matmul(
                        psum[:, 0:512], xn1v[kc][:, tt * 128:(tt + 1) * 128],
                        wcol[:, (kc * 2 + jh) * 512:(kc * 2 + jh + 1) * 512],
                        start=(kc == 0), stop=(kc == NCH - 1))
                    if kc == NCH - 1:
                        r.inc("t", p.s_mm, fn)
                    else:
                        r.t(fn)
                r.wait("v", p.s_mm)
                r.inc("v", p.s_v, lambda v, tt=tt, jh=jh, psum=psum: v.tensor_copy(
                    out=p.b_vring[:, (tt % 2) * 1024 + jh * 512:
                                  (tt % 2) * 1024 + (jh + 1) * 512],
                    in_=psum[:, 0:512]))
                evlev.append(p.s_v.n)
            r.wait("y", p.s_v)
            r.wait("y", p.s_sty)
            sl = (tt % 2) * 1024
            r.inc("y", p.s_sty, lambda y, tt=tt, sl=sl: y.dma_start(
                out=dr_vtm[2 * tt, :, :], in_=p.b_vring[0:64, sl:sl + 1024]), 16)
            r.inc("y", p.s_sty, lambda y, tt=tt, sl=sl: y.dma_start(
                out=dr_vtm[2 * tt + 1, :, :],
                in_=p.b_vring[64:128, sl:sl + 1024]), 16)
            stylev[tt] = p.s_sty.n
        r.barrier(p)


def _zstage(p, cc_z_in, cc_z_out):
    r = p.r
    r.wait("v", p.s_sty)
    for c in range(NCH):
        fn = lambda v, c=c: v.tensor_tensor_scan(
            p.b_zss[:, c * NSEG:(c + 1) * NSEG],
            p.b_ones1[:, 0:1].broadcast_to([128, NSEG]),
            p.b_zss[:, c * NSEG:(c + 1) * NSEG],
            p.b_consts[:, 2:3], ALU.mult, ALU.add)
        if c == NCH - 1:
            r.inc("v", p.s_v, fn)
        else:
            r.v(fn)
    r.wait("v", p.s_v)
    r.inc("v", p.s_v, lambda v: v.tensor_copy(
        out=p.b_zs2[:, 0:8].unsqueeze(2),
        in_=p.b_zss[:].rearrange("q (c s) -> q c s", s=NSEG)[:, :, NSEG - 1:NSEG]))
    r.wait("y", p.s_v)
    r.wait("y", p.s_sty)
    r.inc("y", p.s_sty, lambda y: y.dma_start(
        out=cc_z_in[:], in_=p.b_zs2[:, 0:8]), 16)
    r.wait("g", p.s_sty)
    r.inc("g", p.s_cc, lambda g: g.collective_compute(
        "AllGather", ALU.bypass, ins=[cc_z_in[:]], outs=[cc_z_out[:]],
        replica_groups=RG))
    r.wait("y", p.s_cc)
    r.wait("y", p.s_ld)
    for rk in range(NCORES):
        r.inc("y", p.s_ld, lambda y, rk=rk: y.dma_start(
            out=p.b_zs2[:, rk * NCH:(rk + 1) * NCH],
            in_=cc_z_out[rk * 128:(rk + 1) * 128, :]), 16)
    r.wait("v", p.s_ld)
    z3 = p.b_zs2[:].rearrange("q (rk c) -> q rk c", c=NCH)
    for c in range(NCH):
        r.inc("v", p.s_v, lambda v, c=c, z3=z3: v.tensor_tensor(
            z3[:, :, c], z3[:, :, c], p.b_czmask[:], ALU.mult))
        r.wait("v", p.s_v)
        r.inc("v", p.s_v, lambda v, c=c, z3=z3: v.tensor_reduce(
            p.b_coef[:, 32 + c:33 + c], z3[:, :, c], AX.X, ALU.add))
        r.wait("v", p.s_v)
    r.inc("v", p.s_v, lambda v: v.memset(p.b_zall[:], 0.0))
    r.wait("v", p.s_v)
    W2 = 2 * (NSEG + 1)
    for lo, hi in ((0, 64), (64, 128)):
        e = 0 if lo == 0 else 1
        zl0 = p.b_zall[lo:hi, :].rearrange(
            "q (c s) -> q c s", s=W2)[:, :, e:e + 1]
        zls = p.b_zall[lo:hi, :].rearrange(
            "q (c s) -> q c s", s=W2)[:, :, 2 + e:W2:2]
        r.inc("v", p.s_v, lambda v, zl0=zl0, lo=lo, hi=hi: v.tensor_copy(
            out=zl0, in_=p.b_coef[lo:hi, 32:40].unsqueeze(2)))
        r.wait("v", p.s_v)
        r.inc("v", p.s_v, lambda v, zls=zls, lo=lo, hi=hi: v.tensor_tensor(
            zls,
            p.b_zss[lo:hi, :].rearrange("q (c s) -> q c s", s=NSEG),
            p.b_coef[lo:hi, 32:40].unsqueeze(2).broadcast_to([64, NCH, NSEG]),
            ALU.add))
        r.wait("v", p.s_v)
    r.wait("t", p.s_v)


def _chain(p, dr_sqskf, dr_pt, dr_vtm, dr_asm, ps_mg, cba, cbb):
    r = p.r
    r.wait("v", p.s_cc)
    r.inc("v", p.s_v, lambda v: v.memset(p.b_mgf[:], 0.0))
    r.wait("v", p.s_v)
    r.inc("v", p.s_v, lambda v: v.memset(p.b_aring[:], 0.0))
    r.wait("v", p.s_v)
    r.inc("v", p.s_v, lambda v: v.memset(p.b_vdu[:], 0.0))
    r.wait("v", p.s_v)
    r.inc("v", p.s_v, lambda v: v.memset(p.b_skt[:], 0.0))
    r.wait("v", p.s_v)
    r.inc("v", p.s_v, lambda v: v.memset(p.b_ptv[0:64, :], 0.0))
    r.wait("v", p.s_v)
    for h in range(H):
        par, pr = (h % 2) * 64, h // 2
        base = pr * 256 + (h % 2) * 128 + 64
        fn = lambda v, par=par, base=base: v.tensor_copy(
            out=p.b_mgf[par:par + 64, base:base + 64],
            in_=p.b_identf[par:par + 64, par:par + 64])
        if h == H - 1:
            r.inc("v", p.s_v, fn)
        else:
            r.v(fn)
    r.wait("v", p.s_v)
    r.inc("v", p.s_v, lambda v: v.tensor_copy(
        out=p.b_mg[:], in_=p.b_mgf[:]))
    r.wait("t", p.s_v)
    r.wait("y", p.s_v)
    mmlev, st2lev = {}, {}
    p._st2lev = st2lev
    for s_ in range(NSEG):
        sl = (s_ % 2) * 1024
        ring = slice(sl, sl + 1024)
        if s_ >= 2:
            r.wait("y", p.s_mm, mmlev[s_ - 2])
        r.wait("y", p.s_ld)
        r.inc("y", p.s_ld, lambda y, s_=s_, ring=ring: y.dma_start(
            out=p.b_sring[:, ring], in_=dr_sqskf[s_, :, :]), 16)
        r.inc("y", p.s_ld, lambda y, s_=s_, ring=ring: y.dma_start(
            out=p.b_ptv[64:128, ring], in_=dr_pt[s_, :, :]), 16)
        r.inc("y", p.s_ld, lambda y, s_=s_, ring=ring: y.dma_start(
            out=p.b_vring[64:128, ring], in_=dr_vtm[s_, :, :]), 16)
        r.wait("t", p.s_ld)
        _seg_core(p, s_, ring, dr_asm, ps_mg, cba, cbb)
        mmlev[s_] = p.s_mm.n
        st2lev[s_] = p.s_st2.n


def _seg_core(p, s_, ring, dr_asm, ps_mg, cba, cbb):
    r = p.r
    SEGM = 127
    s16 = slice(s_ * 16, (s_ + 1) * 16)
    r.wait("t", p.s_v)
    if SEGM & 1:
        for h in range(H):
            c = h // 2
            zcol = c * 2 * (NSEG + 1) + 2 * s_ + (h % 2)
            fn = lambda t, h=h, c=c, zcol=zcol, ring=ring: t.matmul(
                p.ps_mm[:, h:h + 1],
                p.b_sring[:, ring][:, c * 128:(c + 1) * 128],
                p.b_zall[:, zcol:zcol + 1],
                start=True, stop=True)
            if h == H - 1:
                r.inc("t", p.s_mm, fn)
            else:
                r.t(fn)
        r.wait("v", p.s_mm)
        r.inc("v", p.s_v, lambda v: v.tensor_scalar(
            p.b_coef[:, 48:64], p.ps_mm[:, 0:16], p.b_consts[:, 4:5], None,
            ALU.add))
        r.wait("v", p.s_v)
        r.inc("v", p.s_v, lambda v, s16=s16: v.reciprocal(
            p.b_rz[:, s16], p.b_coef[:, 48:64]))
    r.wait("t", p.s_v)
    if SEGM & 2:
        for pr in range(NPAIR):
            psd = cba if pr < 4 else cbb
            off = (pr % 4) * 256
            fn = lambda t, pr=pr, psd=psd, off=off, ring=ring: t.matmul(
                psd[:, off:off + 256],
                p.b_sring[:, ring][:, pr * 128:(pr + 1) * 128],
                p.b_mg[:, pr * 256:(pr + 1) * 256],
                start=True, stop=True)
            if pr == NPAIR - 1:
                r.inc("t", p.s_mm, fn)
            else:
                r.t(fn)
    if SEGM & 4:
        for h in range(H):
            c, par = h // 2, (h % 2) * 64
            psd = p.ps_mm if h < 8 else p.ps_mm2
            off = (h % 8) * 64
            fn = lambda t, c=c, par=par, psd=psd, off=off, h=h, ring=ring: \
                t.matmul(
                    psd[0:64, off:off + 64],
                    p.b_ptv[:, ring][:, c * 128 + par:c * 128 + par + 64],
                    p.b_vring[:, ring][:, h * 64:h * 64 + 64],
                    start=True, stop=True)
            if h == H - 1:
                r.inc("t", p.s_mm, fn)
            else:
                r.t(fn)
    r.wait("v", p.s_mm)
    asmv = p.b_aring[0:64, ring]
    if SEGM & 8:
        if s_ >= 2:
            r.wait("v", p.s_st2, p._st2lev[s_ - 2])
        for k, psd in ((0, p.ps_mm), (1, p.ps_mm2)):
            r.inc("v", p.s_v, lambda v, k=k, psd=psd, asmv=asmv: v.tensor_tensor(
                asmv.rearrange("q (h w) -> q h w", w=64)[:, 8 * k:8 * k + 8, :],
                psd[0:64, 0:512].rearrange("q (h w) -> q h w", w=64),
                p.b_coef[0:64, 16 + 8 * k:24 + 8 * k].unsqueeze(2)
                    .broadcast_to([64, 8, 64]),
                ALU.mult))
            r.wait("v", p.s_v)
        r.inc("v", p.s_v, lambda v, s16=s16: v.tensor_tensor(
            p.b_coef[0:64, 32:48], p.b_coef[0:64, 0:16], p.b_rz[0:64, s16],
            ALU.mult))
        r.wait("v", p.s_v)
        for k, psd in ((0, cba), (1, cbb)):
            r.inc("v", p.s_v, lambda v, k=k, psd=psd: v.tensor_tensor(
                p.b_vdu[0:64, 512 * k:512 * k + 512].rearrange(
                    "q (h w) -> q h w", w=64),
                psd[0:64, 0:1024].rearrange("q (h w) -> q h w", w=128)[:, :, 0:64],
                p.b_coef[0:64, 32 + 8 * k:40 + 8 * k].unsqueeze(2)
                    .broadcast_to([64, 8, 64]),
                ALU.mult))
            r.wait("v", p.s_v)
        r.inc("v", p.s_v, lambda v, asmv=asmv: v.tensor_tensor(
            asmv, asmv, p.b_vdu[0:64, 0:1024], ALU.add))
        r.wait("y", p.s_v)
        r.wait("y", p.s_st2)
        r.inc("y", p.s_st2, lambda y, s_=s_, asmv=asmv: y.dma_start(
            out=dr_asm[s_, :, :], in_=asmv), 16)
    if SEGM & 16:
        r.wait("v", p.s_v)
        r.inc("v", p.s_v, lambda v, s16=s16: v.tensor_scalar(
            p.b_coef[64:128, 32:48], p.b_rz[64:128, s16], -1.0, None, ALU.mult))
        r.wait("v", p.s_v)
        for k, psd in ((0, cba), (1, cbb)):
            r.inc("v", p.s_v, lambda v, k=k, psd=psd: v.tensor_tensor(
                p.b_vdu[64:128, 1024 * k:1024 * k + 1024].rearrange(
                    "q (h w) -> q h w", w=128),
                psd[64:128, 0:1024].rearrange("q (h w) -> q h w", w=128),
                p.b_coef[64:128, 32 + 8 * k:40 + 8 * k].unsqueeze(2)
                    .broadcast_to([64, 8, 128]),
                ALU.mult))
            r.wait("v", p.s_v)
        r.inc("v", p.s_v, lambda v, ring=ring: v.tensor_tensor(
            p.b_vdu[64:128, 0:2048].rearrange(
                "q (h w) -> q h w", w=128)[:, :, 0:64],
            p.b_vdu[64:128, 0:2048].rearrange(
                "q (h w) -> q h w", w=128)[:, :, 0:64],
            p.b_vring[64:128, ring].rearrange("q (h w) -> q h w", w=64),
            ALU.add))
    if SEGM & 32:
        r.wait("t", p.s_v)
        for c in range(NCH):
            fn = lambda t, c=c, ring=ring: t.transpose(
                cbb.bitcast(BF16)[:, c * 128:(c + 1) * 128],
                p.b_sring[:, ring][:, c * 128:(c + 1) * 128], p.b_ident[:])
            if c == NCH - 1:
                r.inc("t", p.s_mm, fn)
            else:
                r.t(fn)
        r.wait("v", p.s_mm)
        r.inc("v", p.s_v, lambda v: v.tensor_copy(
            out=p.b_skt[64:128, 0:1024], in_=cbb.bitcast(BF16)[64:128, 0:1024]))
    if SEGM & 64:
        r.wait("t", p.s_v)
        for h in range(H):
            c, par = h // 2, (h % 2) * 64
            fn = lambda t, h=h, c=c, par=par: t.matmul(
                ps_mg[par:par + 64, c * 128:(c + 1) * 128],
                p.b_skt[:, c * 128 + par:c * 128 + par + 64],
                p.b_vdu[:, h * 128:(h + 1) * 128],
                start=True, stop=True, tile_position=(0, par))
            if h == H - 1:
                r.inc("t", p.s_mm, fn)
            else:
                r.t(fn)
        r.wait("v", p.s_mm)
        mge = p.b_mgf[0:64, :].rearrange("q (pr w) -> q pr w", w=256)[:, :, 0:128]
        mgo = p.b_mgf[64:128, :].rearrange(
            "q (pr w) -> q pr w", w=256)[:, :, 128:256]
        pse = ps_mg[0:64, :].rearrange("q (pr w) -> q pr w", w=128)
        pso = ps_mg[64:128, :].rearrange("q (pr w) -> q pr w", w=128)
        r.inc("v", p.s_v, lambda v, mge=mge, pse=pse: v.tensor_tensor(
            mge, mge, pse, ALU.add))
        r.wait("v", p.s_v)
        r.inc("v", p.s_v, lambda v, mgo=mgo, pso=pso: v.tensor_tensor(
            mgo, mgo, pso, ALU.add))
        r.wait("v", p.s_v)
        r.inc("v", p.s_v, lambda v: v.tensor_copy(
            out=p.b_mg[:], in_=p.b_mgf[:]))
    r.wait("t", p.s_v)


def _mg_exchange(p, cc_mg_in, cc_mg_out, cba, cbb):
    r = p.r
    r.wait("v", p.s_v)
    r.inc("v", p.s_v, lambda v: v.memset(p.b_gt[:], 0.0))
    r.wait("v", p.s_v)
    r.inc("v", p.s_v, lambda v: v.memset(p.b_skt[:], 0.0))
    r.wait("v", p.s_v)
    for rd in range(2):
        r.wait("t", p.s_v)
        for hh in range(8):
            h = rd * 8 + hh
            par, pr = (h % 2) * 64, h // 2
            base = pr * 256 + (h % 2) * 128 + (64 if par == 0 else 0)
            fn = lambda t, hh=hh, base=base: t.transpose(
                cba.bitcast(BF16)[:, hh * 128:(hh + 1) * 128],
                p.b_mg[:, base:base + 128], p.b_ident[:])
            if hh == 7:
                r.inc("t", p.s_mm, fn)
            else:
                r.t(fn)
        r.wait("v", p.s_mm)
        cbab = cba.bitcast(BF16)
        ge = cbab[0:64, 0:1024].rearrange("q (b w) -> q b w", w=256)[:, :, 0:64]
        go = cbab[64:128, 0:1024].rearrange(
            "q (b w) -> q b w", w=256)[:, :, 192:256]
        gt3 = p.b_gt[:, rd * 512:(rd + 1) * 512].rearrange(
            "q (b w) -> q b w", w=128)
        r.v(lambda v, ge=ge, gt3=gt3: v.tensor_copy(
            out=gt3[0:64, :, 0:64], in_=ge))
        r.inc("v", p.s_v, lambda v, go=go, gt3=gt3: v.tensor_copy(
            out=gt3[64:128, :, 64:128], in_=go))
        r.wait("t", p.s_v)
    mg3 = p.b_mg[:].rearrange("q (pr w) -> q pr w", w=256)
    sk3 = p.b_skt[:].rearrange("q (pr w) -> q pr w", w=128)
    r.wait("v", p.s_v)
    r.v(lambda v, mg3=mg3, sk3=sk3: v.tensor_copy(
        out=sk3[0:64, :, 0:64], in_=mg3[0:64, :, 0:64]))
    r.inc("v", p.s_v, lambda v, mg3=mg3, sk3=sk3: v.tensor_copy(
        out=sk3[64:128, :, 64:128], in_=mg3[64:128, :, 128:192]))
    r.wait("y", p.s_v)
    r.wait("y", p.s_sty)
    r.inc("y", p.s_sty, lambda y: y.dma_start(
        out=cc_mg_in[0:128, :], in_=p.b_gt[:]), 16)
    r.inc("y", p.s_sty, lambda y: y.dma_start(
        out=cc_mg_in[128:256, :], in_=p.b_skt[:]), 16)
    r.wait("g", p.s_sty)
    r.inc("g", p.s_cc, lambda g: g.collective_compute(
        "AllGather", ALU.bypass, ins=[cc_mg_in[:]], outs=[cc_mg_out[:]],
        replica_groups=RG))
    r.wait("v", p.s_cc)
    r.inc("v", p.s_v, lambda v: v.memset(p.b_gtid[:], 0.0))
    r.wait("v", p.s_v)
    for h in range(H):
        par = (h % 2) * 64
        fn = lambda v, h=h, par=par: v.tensor_copy(
            out=p.b_gtid[par:par + 64, h * 64:h * 64 + 64],
            in_=p.b_ident[par:par + 64, par:par + 64])
        if h == H - 1:
            r.inc("v", p.s_v, fn)
        else:
            r.v(fn)
    r.wait("v", p.s_v)
    r.inc("v", p.s_v, lambda v: v.tensor_scalar(
        p.b_coef[:, 48:56], p.b_czmask[:], -1.0, 1.0, ALU.mult, ALU.add))
    r.wait("v", p.s_v)
    r.inc("v", p.s_v, lambda v: v.memset(p.b_mst[:], 0.0))
    for rr in range(NCORES - 1):
        r.wait("y", p.s_v)
        r.wait("y", p.s_ld)
        r.inc("y", p.s_ld, lambda y, rr=rr: y.dma_start(
            out=p.b_mgag[:, 0:1024],
            in_=cc_mg_out[rr * 256:rr * 256 + 128, :]), 16)
        r.inc("y", p.s_ld, lambda y, rr=rr: y.dma_start(
            out=p.b_mgag[:, 1024:2048],
            in_=cc_mg_out[rr * 256 + 128:rr * 256 + 256, :]), 16)
        r.wait("v", p.s_ld)
        r.inc("v", p.s_v, lambda v, rr=rr: v.tensor_scalar(
            p.b_gt[:], p.b_mgag[:, 0:1024], p.b_czmask[:, rr:rr + 1],
            None, ALU.mult))
        r.wait("v", p.s_v)
        r.inc("v", p.s_v, lambda v, rr=rr: v.scalar_tensor_tensor(
            p.b_gt[:], p.b_gtid[:], p.b_coef[:, 48 + rr:49 + rr],
            p.b_gt[:], ALU.mult, ALU.add))
        r.inc("v", p.s_v, lambda v, rr=rr: v.tensor_scalar(
            p.b_mgag[:, 1024:2048], p.b_mgag[:, 1024:2048],
            p.b_czmask[:, rr:rr + 1], None, ALU.mult))
        r.wait("t", p.s_v)
        for h in range(H):
            par = (h % 2) * 64
            r.t(lambda t, h=h, par=par: t.matmul(
                cbb[par:par + 64, h * 64:h * 64 + 64],
                p.b_gt[par:par + 64, h * 64:h * 64 + 64],
                p.b_mst[par:par + 64, h * 64:h * 64 + 64],
                start=True, stop=False, tile_position=(par, par)))
            fn = lambda t, h=h, par=par: t.matmul(
                cbb[par:par + 64, h * 64:h * 64 + 64],
                p.b_ident[par:par + 64, par:par + 64],
                p.b_mgag[par:par + 64, 1024 + h * 64:1024 + h * 64 + 64],
                start=False, stop=True, tile_position=(par, par))
            if h == H - 1:
                r.inc("t", p.s_mm, fn)
            else:
                r.t(fn)
        r.wait("v", p.s_mm)
        r.inc("v", p.s_v, lambda v: v.tensor_copy(
            out=p.b_mst[:], in_=cbb[:, 0:1024]))
        r.wait("t", p.s_v)
    r.wait("v", p.s_v)
    r.inc("v", p.s_v, lambda v: v.memset(p.b_wf[:], 0.0))
    r.wait("v", p.s_v)
    wfe, wfo = _wview(p.b_wf)
    cbe, cbo = _wview(cbb[:, 0:1024])
    r.v(lambda v, wfe=wfe, cbe=cbe: v.tensor_copy(out=wfe, in_=cbe))
    r.inc("v", p.s_v, lambda v, wfo=wfo, cbo=cbo: v.tensor_copy(out=wfo, in_=cbo))
    r.wait("v", p.s_v)
    r.inc("v", p.s_v, lambda v: v.tensor_copy(out=p.b_w[:], in_=p.b_wf[:]))
    r.wait("t", p.s_v)
    r.wait("y", p.s_v)


def _wview(ap):
    e = ap[0:64, 0:1024].rearrange("q (h w) -> q h w", w=128)[:, :, 0:64]
    o = ap[64:128, 0:1024].rearrange("q (h w) -> q h w", w=128)[:, :, 64:128]
    return [e, o]


def _rerun(p, dr_sqskf, dr_asm, a_fm, ps_mg, cba, cbb):
    r = p.r
    r.wait("v", p.s_v)
    r.inc("v", p.s_v, lambda v: v.memset(p.b_skt[0:64, :], 0.0))
    r.wait("t", p.s_v)
    mmlev = {}
    for s_ in range(NSEG):
        sl = (s_ % 2) * 1024
        ring = slice(sl, sl + 1024)
        s16 = slice(s_ * 16, (s_ + 1) * 16)
        if s_ >= 2:
            r.wait("y", p.s_mm, mmlev[s_ - 2])
        r.wait("y", p.s_ld)
        r.inc("y", p.s_ld, lambda y, s_=s_, ring=ring: y.dma_start(
            out=p.b_sring[:, ring], in_=dr_sqskf[s_, :, :]), 16)
        r.inc("y", p.s_ld, lambda y, s_=s_, ring=ring: y.dma_start(
            out=p.b_aring[0:64, ring], in_=dr_asm[s_, :, :]), 16)
        r.wait("t", p.s_ld)
        r.wait("t", p.s_v)
        for h in range(H):
            c = h // 2
            fn = lambda t, h=h, c=c, ring=ring: t.matmul(
                cba[:, h * 64:h * 64 + 64],
                p.b_sring[:, ring][:, c * 128:(c + 1) * 128],
                p.b_w[:, h * 64:h * 64 + 64],
                start=True, stop=True)
            if h == H - 1:
                r.inc("t", p.s_mm, fn)
            else:
                r.t(fn)
        r.wait("v", p.s_mm)
        r.inc("v", p.s_v, lambda v, s16=s16: v.tensor_tensor(
            p.b_coef[0:64, 32:48], p.b_coef[0:64, 0:16], p.b_rz[0:64, s16],
            ALU.mult))
        r.wait("v", p.s_v)
        r.inc("v", p.s_v, lambda v: v.tensor_tensor(
            p.b_vdu[0:64, 0:1024].rearrange("q (h w) -> q h w", w=64),
            cba[0:64, 0:1024].rearrange("q (h w) -> q h w", w=64),
            p.b_coef[0:64, 32:48].unsqueeze(2).broadcast_to([64, 16, 64]),
            ALU.mult))
        r.wait("v", p.s_v)
        r.inc("v", p.s_v, lambda v, ring=ring: v.tensor_tensor(
            p.b_aring[0:64, ring], p.b_aring[0:64, ring],
            p.b_vdu[0:64, 0:1024], ALU.add))
        r.wait("t", p.s_v)
        for c in range(NCH):
            fn = lambda t, c=c, ring=ring: t.transpose(
                cbb.bitcast(BF16)[:, c * 128:(c + 1) * 128],
                p.b_aring[:, ring][:, c * 128:(c + 1) * 128], p.b_ident[:])
            if c == NCH - 1:
                r.inc("t", p.s_mm, fn)
            else:
                r.t(fn)
        r.wait("v", p.s_mm)
        afm3 = a_fm[:].rearrange("q (c w) -> q c w", w=TL)
        r.inc("v", p.s_v, lambda v, s_=s_, afm3=afm3: v.tensor_copy(
            out=afm3[:, :, s_ * 64:(s_ + 1) * 64],
            in_=cbb.bitcast(BF16)[:, 0:1024].rearrange(
                "q (c w) -> q c w", w=128)[:, :, 0:64]))
        r.wait("v", p.s_v)
        r.inc("v", p.s_v, lambda v, s16=s16: v.tensor_scalar(
            p.b_coef[64:128, 32:48], p.b_rz[64:128, s16], -1.0, None, ALU.mult))
        r.wait("v", p.s_v)
        r.inc("v", p.s_v, lambda v: v.tensor_tensor(
            p.b_vdu[64:128, 0:1024].rearrange("q (h w) -> q h w", w=64),
            cba[64:128, 0:1024].rearrange("q (h w) -> q h w", w=64),
            p.b_coef[64:128, 32:48].unsqueeze(2).broadcast_to([64, 16, 64]),
            ALU.mult))
        r.wait("t", p.s_v)
        for c in range(NCH):
            fn = lambda t, c=c, ring=ring: t.transpose(
                cbb.bitcast(BF16)[:, c * 128:(c + 1) * 128],
                p.b_sring[:, ring][:, c * 128:(c + 1) * 128], p.b_ident[:])
            if c == NCH - 1:
                r.inc("t", p.s_mm, fn)
            else:
                r.t(fn)
        r.wait("v", p.s_mm)
        r.inc("v", p.s_v, lambda v: v.tensor_copy(
            out=p.b_skt[64:128, 0:1024], in_=cbb.bitcast(BF16)[64:128, 0:1024]))
        r.wait("t", p.s_v)
        for h in range(H):
            c, par = h // 2, (h % 2) * 64
            fn = lambda t, h=h, c=c, par=par: t.matmul(
                ps_mg[par:par + 64, h * 64:h * 64 + 64],
                p.b_skt[:, c * 128 + par:c * 128 + par + 64],
                p.b_vdu[:, h * 64:h * 64 + 64],
                start=True, stop=True, tile_position=(0, par))
            if h == H - 1:
                r.inc("t", p.s_mm, fn)
            else:
                r.t(fn)
        r.wait("v", p.s_mm)
        wfe, wfo = _wview(p.b_wf)
        pge, pgo = _wview(ps_mg[:, 0:1024])
        r.inc("v", p.s_v, lambda v, wfe=wfe, pge=pge: v.tensor_tensor(
            wfe, wfe, pge, ALU.add))
        r.wait("v", p.s_v)
        r.inc("v", p.s_v, lambda v, wfo=wfo, pgo=pgo: v.tensor_tensor(
            wfo, wfo, pgo, ALU.add))
        r.wait("v", p.s_v)
        r.inc("v", p.s_v, lambda v: v.tensor_copy(out=p.b_w[:], in_=p.b_wf[:]))
        mmlev[s_] = p.s_mm.n
    r.wait("t", p.s_v)



TC = 2048
NTC = T // TC
NDT = DI_SH // 128


def stage_attn_b(p):
    r = p.r
    nc = p.nc

    def dint(name, shape, dt=F32):
        return nc.dram_tensor(name, list(shape), dt, kind="Internal").ap()

    cc_halo_in = dint("cc_halo_in", [DI, 4], BF16)
    cc_halo_out = dint("cc_halo_out", [NCORES * DI, 4], BF16)
    cc_dt_in = dint("cc_dt_in", [DI, TL], BF16)
    cc_dx_in = dint("cc_dx_in", [DI, TL], BF16)
    cc_bc_in = dint("cc_bc_in", [2 * NST, TL], BF16)
    cc_st_in = dint("cc_st_in", [128, 512], BF16)
    cc_st_out = dint("cc_st_out", [NCORES * 128, 512], BF16)
    dr_z = dint("dr_z", [DI, TL], BF16)
    dr_xc = dint("dr_xc", [DI, TL], BF16)

    p.b_cw = p.sb("cw", [128, NCH_I * DCONV])
    p.b_cb = p.sb("cb", [128, NCH_I])
    p.b_bdt = p.sb("bdt", [128, NCH_I])
    p.b_dvec = p.sb("dvec", [128, NCH_I])
    p.b_wbc = p.sb("wbc", [128, NCH_I * 2 * NST], BF16)
    p.ps_bc = p.psb("ps_bc", [32, 512])
    r.inc("y", p.s_wl, lambda y: y.dma_start(
        out=p.b_cw[:].rearrange("q (c k) -> q c k", c=NCH_I),
        in_=p.d_convw.rearrange("(c q) k -> q c k", q=128),
        allow_slow_non_contiguous=True), 16)
    for dst, srcv in ((p.b_cb, p.d_convb), (p.b_bdt, p.d_bdt), (p.b_dvec, p.d_D)):
        r.inc("y", p.s_wl, lambda y, dst=dst, srcv=srcv: y.dma_start(
            out=dst[:].unsqueeze(2),
            in_=srcv.rearrange("o (c q) -> q c o", q=128),
            allow_slow_non_contiguous=True), 16)

    mm_per_j = TL // 512

    import contextlib
    esA = contextlib.ExitStack()
    xcv = esA.enter_context(nc.sbuf_tensor("xconv", [128, NCH_I * (TL + 3)], BF16))
    wcol = esA.enter_context(nc.sbuf_tensor("wcol", [128, 4096], BF16))
    dstg = p.b_sq[:, 0:1024].bitcast(BF16)
    sstg = p.b_tmp
    xv = chunk_views(p.b_x, NCH, TL)
    xcvv = [xcv[:, c * (TL + 3):(c + 1) * (TL + 3)] for c in range(NCH_I)]
    wslots = [wcol[:, b * 1024:(b + 1) * 1024] for b in range(4)]
    wslots2 = [wcol[:, 0:2048], wcol[:, 2048:4096]]

    r.inc("v", p.s_v, lambda v: v.memset(
        xcv[:].rearrange("q (c w) -> q c w", c=NCH_I)[:, :, 0:3], 0.0))
    with nc.sbuf_tensor("xn3", [128, NCH * TL], BF16) as xn3:
        xn3v = chunk_views(xn3, NCH, TL)
        r.barrier(p)
        emit_ln(p, xv, xn3v, gam=p.b_nw, bet=p.b_nb, double=True, add_into=xv)
        dbg_tap(p, "xn3", xn3[:], [128, NCH * TL], BF16)
        r.barrier(p)

        mm0 = p.s_mm.n
        def win_dma(g, j):
            slot = j % 4
            srcj = p.d_win[:, j * 128:(j + 1) * 128].rearrange("(kc q) o -> q kc o", q=128)
            return g.dma_start(
                out=wslots[slot][:].rearrange("q (kc o) -> q kc o", kc=NCH), in_=srcj)
        def win_throttle(lo):
            jmax = lo + 1
            if jmax >= 4:
                r.wait("g", p.s_mm, mm0 + (jmax - 3) * mm_per_j)
        wf_win = wave_load(p, 2 * NCH_I, win_dma, throttle=win_throttle)
        for j in range(2 * NCH_I):
            slot = j % 4
            r.wait("t", *wf_win[j])
            is_z = j >= NCH_I
            for it in range(mm_per_j):
                sl = slice(it * 512, (it + 1) * 512)
                psum = p.ps_mm if it % 2 == 0 else p.ps_mm2
                if j > 0 or it > 1:
                    r.wait("t", p.s_v)
                for kc in range(NCH):
                    fn = lambda t, kc=kc, sl=sl, slot=slot, psum=psum: t.matmul(
                        psum[:, 0:512], wslots[slot][:, kc * 128:(kc + 1) * 128],
                        xn3v[kc][:, sl], start=(kc == 0), stop=(kc == NCH - 1))
                    if kc == NCH - 1:
                        r.inc("t", p.s_mm, fn)
                    else:
                        r.t(fn)
                r.wait("v", p.s_mm)
                if not is_z:
                    r.inc("v", p.s_v, lambda v, j=j, sl=sl, psum=psum: v.tensor_copy(
                        out=xcvv[j][:, 3 + sl.start:3 + sl.stop], in_=psum[:, 0:512]))
                else:
                    zc = j - NCH_I
                    pp = it % 2
                    r.wait("v", p.s_sty)
                    r.inc("v", p.s_v, lambda v, sl=sl, psum=psum, pp=pp: v.tensor_copy(
                        out=sstg[:, pp * 512:pp * 512 + 512], in_=psum[:, 0:512]))
                    r.wait("s", p.s_v)
                    r.inc("s", p.s_s, lambda s, pp=pp: s.activation(
                        out=dstg[:, pp * 512:pp * 512 + 512],
                        in_=sstg[:, pp * 512:pp * 512 + 512], func=AF.Silu))
                    r.wait("y", p.s_s)
                    r.inc("y", p.s_sty, lambda y, zc=zc, sl=sl, pp=pp: y.dma_start(
                        out=dr_z[zc * 128:(zc + 1) * 128, sl],
                        in_=dstg[:, pp * 512:pp * 512 + 512]), 16)
        r.barrier(p)
        dbg_tap(p, "xpart", xcv[:], [128, NCH_I * (TL + 3)], BF16)

    for c in range(NCH_I):
        r.inc("y", p.s_sty, lambda y, c=c: y.dma_start(
            out=cc_halo_in[c * 128:(c + 1) * 128, 0:4],
            in_=xcvv[c][:, TL - 1:TL + 3], allow_slow_non_contiguous=True), 16)
    r.wait("g", p.s_sty)
    r.inc("g", p.s_cc, lambda g: g.collective_compute(
        "AllGather", ALU.bypass, ins=[cc_halo_in[:]], outs=[cc_halo_out[:]],
        replica_groups=RG))
    r.wait("g", p.s_cc)
    ho4 = cc_halo_out.rearrange("(rk c q) k -> rk q c k", q=128, c=NCH_I)
    r.wait("y", p.s_cc)
    r.inc("y", p.s_ld, lambda y: y.dma_start(
        out=xcv[:].rearrange("q (c w) -> q c w", c=NCH_I)[:, :, 0:3],
        in_=ho4[bass.ts(p.cidp_y, 1)].squeeze(0)[:, :, 1:4],
        allow_slow_non_contiguous=True), 16)
    r.wait("v", p.s_ld)
    r.inc("v", p.s_v, lambda v: v.tensor_scalar(
        xcv[:].rearrange("q (c w) -> q c w", c=NCH_I)[:, :, 0:3],
        xcv[:].rearrange("q (c w) -> q c w", c=NCH_I)[:, :, 0:3],
        p.b_czero[:, 0:1], None, ALU.mult))
    r.wait("v", p.s_v)
    r.wait("g", p.s_v)
    r.wait("s", p.s_v)

    convlev = {}
    for c in range(NCH_I):
        accA = [sstg[:, 0:1024], sstg[:, 1024:2048]]
        accB = [p.b_sq[:, 2048:3072], p.b_sq[:, 3072:4096]]
        r.wait("v", p.s_s)
        lev = {0: None, 1: None}
        for step in range(4):
            for it in range(2):
                s0 = it * 1024
                acc = accA if it == 0 else accB
                if lev[it] is not None:
                    r.wait("v", p.s_v, lev[it])
                src_ap = acc[(step + 1) % 2]
                if step == 0:
                    fn = lambda v, c=c, s0=s0, acc=acc: v.scalar_tensor_tensor(
                        acc[0], xcvv[c][:, s0:s0 + 1024],
                        p.b_cw[:, c * 4:c * 4 + 1],
                        p.b_cb[:, c:c + 1].broadcast_to([128, 1024]),
                        ALU.mult, ALU.add)
                else:
                    fn = lambda v, c=c, s0=s0, acc=acc, step=step: \
                        v.scalar_tensor_tensor(
                            acc[step % 2], xcvv[c][:, s0 + step:s0 + step + 1024],
                            p.b_cw[:, c * 4 + step:c * 4 + step + 1],
                            acc[(step + 1) % 2], ALU.mult, ALU.add)
                r.inc("v", p.s_v, fn)
                lev[it] = p.s_v.n
        r.wait("s", p.s_v)
        r.inc("s", p.s_s, lambda s, c=c: s.activation(
            out=xcvv[c][:, 3:1027], in_=sstg[:, 1024:2048], func=AF.Silu))
        r.inc("s", p.s_s, lambda s, c=c: s.activation(
            out=xcvv[c][:, 1027:2051], in_=p.b_sq[:, 3072:4096], func=AF.Silu))
    r.barrier(p)
    dbg_tap(p, "xconv", xcv[:], [128, NCH_I * (TL + 3)], BF16)

    mm1 = p.s_mm.n
    def wdt_dma(g, i):
        j, h = divmod(i, 2)
        slot4 = i % 4
        srcj = p.d_wdt[h * 1024:(h + 1) * 1024, j * 128:(j + 1) * 128].rearrange(
            "(kc q) o -> q kc o", q=128)
        return g.dma_start(
            out=wslots[slot4][:].rearrange("q (kc o) -> q kc o", kc=NCH), in_=srcj)
    def wdt_throttle(i):
        j = i // 2
        if j >= 2:
            r.wait("g", p.s_mm, mm1 + (j - 1) * mm_per_j)
    wf_wdt = wave_load(p, 2 * NCH_I, wdt_dma, throttle=wdt_throttle)
    for j in range(NCH_I):
        r.wait("t", *wf_wdt[2 * j])
        r.wait("t", *wf_wdt[2 * j + 1])
        for it in range(mm_per_j):
            sl = slice(it * 512, (it + 1) * 512)
            pp = it % 2
            psum = p.ps_mm if pp == 0 else p.ps_mm2
            if j > 0 or it > 1:
                r.wait("t", p.s_v)
            for kc in range(NCH_I):
                slot4 = (2 * j + kc // 8) % 4
                kcl = kc % 8
                fn = lambda t, kcl=kcl, sl=sl, slot4=slot4, psum=psum, kc=kc: t.matmul(
                    psum[:, 0:512], wslots[slot4][:, kcl * 128:(kcl + 1) * 128],
                    xcvv[kc][:, 3 + sl.start:3 + sl.stop],
                    start=(kc == 0), stop=(kc == NCH_I - 1))
                if kc == NCH_I - 1:
                    r.inc("t", p.s_mm, fn)
                else:
                    r.t(fn)
            r.wait("v", p.s_mm)
            r.inc("v", p.s_v, lambda v, j=j, psum=psum, pp=pp: v.tensor_scalar(
                sstg[:, pp * 1024:pp * 1024 + 512], psum[:, 0:512],
                p.b_bdt[:, j:j + 1], None, ALU.add))
            r.wait("s", p.s_v)
            r.inc("s", p.s_s, lambda s, pp=pp: s.activation(
                out=sstg[:, pp * 1024 + 512:pp * 1024 + 1024],
                in_=sstg[:, pp * 1024:pp * 1024 + 512], func=AF.Abs))
            r.wait("s", p.s_s)
            r.inc("s", p.s_s, lambda s, pp=pp: s.activation(
                out=sstg[:, pp * 1024 + 512:pp * 1024 + 1024],
                in_=sstg[:, pp * 1024 + 512:pp * 1024 + 1024], func=AF.Exp,
                bias=p.b_consts[:, 2:3], scale=p.b_consts[:, 5:6]))
            r.wait("s", p.s_s)
            r.inc("s", p.s_s, lambda s, pp=pp: s.activation(
                out=sstg[:, pp * 1024 + 512:pp * 1024 + 1024],
                in_=sstg[:, pp * 1024 + 512:pp * 1024 + 1024], func=AF.Ln,
                bias=p.b_consts[:, 1:2], scale=p.b_consts[:, 1:2]))
            r.wait("v", p.s_s)
            r.wait("v", p.s_sty)
            r.inc("v", p.s_v, lambda v, pp=pp: v.scalar_tensor_tensor(
                dstg[:, pp * 512:pp * 512 + 512],
                sstg[:, pp * 1024:pp * 1024 + 512], p.b_consts[:, 2:3],
                sstg[:, pp * 1024 + 512:pp * 1024 + 1024], ALU.max, ALU.add))
            r.wait("v", p.s_v)
            r.inc("v", p.s_v, lambda v, j=j, sl=sl, pp=pp: v.tensor_tensor(
                dstg[:, 1024 + pp * 512:1024 + pp * 512 + 512],
                dstg[:, pp * 512:pp * 512 + 512],
                xcvv[j][:, 3 + sl.start:3 + sl.stop], ALU.mult))
            r.wait("y", p.s_v)
            r.inc("y", p.s_sty, lambda y, j=j, sl=sl, pp=pp: y.dma_start(
                out=cc_dt_in[j * 128:(j + 1) * 128, sl],
                in_=dstg[:, pp * 512:pp * 512 + 512]), 16)
            r.inc("y", p.s_sty, lambda y, j=j, sl=sl, pp=pp: y.dma_start(
                out=cc_dx_in[j * 128:(j + 1) * 128, sl],
                in_=dstg[:, 1024 + pp * 512:1024 + pp * 512 + 512]), 16)
    r.barrier(p)

    def wbc_dma(g, i):
        if i == 0:
            return g.dma_start(
                out=p.b_wbc[:].rearrange("q (kc m) -> q kc m", kc=NCH_I)[:, :, 0:NST],
                in_=p.d_wB.rearrange("(kc q) n -> q kc n", q=128))
        return g.dma_start(
            out=p.b_wbc[:].rearrange("q (kc m) -> q kc m", kc=NCH_I)[:, :, NST:2 * NST],
            in_=p.d_wC.rearrange("(kc q) n -> q kc n", q=128))
    wf_bc = wave_load(p, 2, wbc_dma)
    r.wait("t", *wf_bc[1])
    for it in range(mm_per_j):
        sl = slice(it * 512, (it + 1) * 512)
        for kc in range(NCH_I):
            fn = lambda t, kc=kc, sl=sl: t.matmul(
                p.ps_bc[0:32, 0:512], p.b_wbc[:, kc * 32:kc * 32 + 32],
                xcvv[kc][:, 3 + sl.start:3 + sl.stop],
                start=(kc == 0), stop=(kc == NCH_I - 1))
            if kc == NCH_I - 1:
                r.inc("t", p.s_mm, fn)
            else:
                r.t(fn)
        r.wait("v", p.s_mm)
        r.inc("v", p.s_v, lambda v, sl=sl: v.tensor_copy(
            out=p.b_sq[0:32, 1024:2048].bitcast(BF16)[:, sl], in_=p.ps_bc[0:32, 0:512]))
        r.wait("t", p.s_v)
    r.wait("y", p.s_v)
    r.inc("y", p.s_sty, lambda y: y.dma_start(out=cc_bc_in[:], in_=p.b_sq[0:32, 1024:2048].bitcast(BF16)[:, 0:TL]), 16)
    for c in range(NCH_I):
        r.inc("y", p.s_sty, lambda y, c=c: y.dma_start(
            out=dr_xc[c * 128:(c + 1) * 128, :], in_=xcvv[c][:, 3:3 + TL]), 16)
    r.wait("g", p.s_sty)

    r.barrier(p)
    esA.close()

    with nc.sbuf_tensor("ygbuf", [128, NCH_I * TL], BF16) as ygb, \
         nc.sbuf_tensor("dt_ch", [128, TL], BF16) as dtch, \
         nc.sbuf_tensor("dx_ch", [128, TL], BF16) as dxch, \
         nc.sbuf_tensor("brep", [128, 6 * 2048], BF16) as brep, \
         nc.sbuf_tensor("y_f", [128, TL], F32) as y_f:
        ststg = p.b_str.bitcast(BF16)
        ygv = chunk_views(ygb, NCH_I, TL)
        a_f = p.b_sq[:, 0:2048]
        u_f = p.b_sq[:, 2048:4096]
        h_f = p.b_tmp[:, 0:2048]
        amat = p.b_rep[:, 0:256]
        sst = p.b_trin[:, 0:800]
        dts = sst[:, 0:16]
        tdec = sst[:, 16:272]
        hloc = sst[:, 272:528]
        hstart = sst[:, 528:784]
        scr16 = sst[:, 784:800]
        al3 = p.d_alog.rearrange("(c q) n -> q c n", q=128)
        r.wait("y", p.s_ld)
        r.inc("y", p.s_ld, lambda y: y.dma_start(
            out=amat[:].rearrange("q (c n) -> q c n", n=NST), in_=al3,
            allow_slow_non_contiguous=True), 16)
        r.wait("s", p.s_ld)
        r.inc("s", p.s_s, lambda s: s.activation(out=amat[:], in_=amat[:], func=AF.Exp))
        r.wait("v", p.s_s)
        r.inc("v", p.s_v, lambda v: v.tensor_scalar(amat[:], amat[:], -1.0, None, ALU.mult))
        r.wait("s", p.s_v)

        sqb = p.b_sq.bitcast(BF16)
        tmb = p.b_tmp.bitcast(BF16)
        aslot = [sqb[:, 0:2048], sqb[:, 2048:4096]]
        uslot = [sqb[:, 4096:6144], sqb[:, 6144:8192]]
        hslot = [tmb[:, 0:2048], tmb[:, 2048:4096]]
        yfb = y_f.bitcast(BF16)
        pslot = [yfb[:, 0:2048], yfb[:, 2048:4096]]

        ldlev, scanlev, addlev, explev, umullev, prodlev = {}, {}, {}, {}, {}, {}
        r.wait("y", p.s_ld)
        r.inc("y", p.s_ld, lambda y: y.dma_start(
            out=dtch[:], in_=cc_dt_in[0:128, :]), 16)
        r.inc("y", p.s_ld, lambda y: y.dma_start(
            out=dxch[:], in_=cc_dx_in[0:128, :]), 16)
        ldlev[0] = p.s_ld.n
        r.wait("v", p.s_ld)
        r.inc("v", p.s_v, lambda v: v.tensor_reduce(
            dts[:, 0:1], dtch[:], AX.X, ALU.add))
        sprlev = {}
        for kk in range(3):
            n0 = kk
            bsl = (kk % 3) * 4096
            r.wait("y", p.s_br)
            r.inc("y", p.s_br, lambda y, n0=n0, bsl=bsl: y.dma_start(
                out=brep[:, bsl:bsl + 2048],
                in_=cc_bc_in[n0:n0 + 1, :].broadcast_to([128, TL]),
                allow_slow_non_contiguous=True), 16)
            r.inc("y", p.s_br, lambda y, n0=n0, bsl=bsl: y.dma_start(
                out=brep[:, bsl + 2048:bsl + 4096],
                in_=cc_bc_in[NST + n0:NST + n0 + 1, :].broadcast_to([128, TL]),
                allow_slow_non_contiguous=True), 16)
            sprlev[kk] = p.s_br.n

        NSTEP = NCH_I * NST
        for k in range(NSTEP):
            cc, n = divmod(k, NST)
            csl = slice(0, TL)
            col = k
            if k >= 2:
                r.wait("s", p.s_v, scanlev[k - 2])
            r.wait("s", p.s_ld, ldlev[cc])
            r.inc("s", p.s_s, lambda s, csl=csl, k=k, col=col: s.activation(
                out=aslot[k % 2], in_=dtch[:, csl], func=AF.Exp,
                scale=amat[:, col:col + 1]))
            explev[k] = p.s_s.n
            if n == NST - 1 and cc + 1 < NCH_I:
                pass
            r.wait("g", p.s_br, sprlev[k])
            r.wait("g", p.s_ld, ldlev[cc])
            if k >= 2:
                r.wait("g", p.s_v, scanlev[k - 2])
            r.inc("g", p.s_g, lambda g, csl=csl, k=k, bsl=(k % 3) * 4096:
                  g.tensor_tensor(uslot[k % 2], dxch[:, csl],
                                  brep[:, bsl:bsl + 2048], ALU.mult))
            umullev[k] = p.s_g.n
            if n == NST - 1 and cc + 1 < NCH_I:
                r.wait("y", p.s_ld)
                r.wait("y", p.s_s, explev[k])
                r.wait("y", p.s_g, umullev[k])
                r.wait("y", p.s_v)
                r.inc("y", p.s_ld, lambda y, cc=cc: y.dma_start(
                    out=dtch[:], in_=cc_dt_in[(cc + 1) * 128:(cc + 2) * 128, :]), 16)
                r.inc("y", p.s_ld, lambda y, cc=cc: y.dma_start(
                    out=dxch[:], in_=cc_dx_in[(cc + 1) * 128:(cc + 2) * 128, :]), 16)
                ldlev[cc + 1] = p.s_ld.n
            r.wait("v", p.s_s, explev[k])
            r.wait("v", p.s_g, umullev[k])
            r.inc("v", p.s_v, lambda v, k=k: v.tensor_tensor_scan(
                hslot[k % 2], aslot[k % 2], uslot[k % 2], p.b_consts[:, 2:3],
                ALU.mult, ALU.add))
            scanlev[k] = p.s_v.n
            if k >= 1:
                _p1_prod_add(p, k - 1, hslot, pslot, brep, ygv[(k - 1) // NST],
                             scanlev, prodlev, addlev, sprlev, hloc)
                if k + 2 < NSTEP:
                    n2 = (k + 2) % NST
                    bsl2 = ((k + 2) % 3) * 4096
                    r.wait("y", p.s_br)
                    r.wait("y", p.s_g, prodlev[k - 1])
                    r.inc("y", p.s_br, lambda y, n2=n2, bsl2=bsl2: y.dma_start(
                        out=brep[:, bsl2:bsl2 + 2048],
                        in_=cc_bc_in[n2:n2 + 1, :].broadcast_to([128, TL]),
                        allow_slow_non_contiguous=True), 16)
                    r.inc("y", p.s_br, lambda y, n2=n2, bsl2=bsl2: y.dma_start(
                        out=brep[:, bsl2 + 2048:bsl2 + 4096],
                        in_=cc_bc_in[NST + n2:NST + n2 + 1, :]
                            .broadcast_to([128, TL]),
                        allow_slow_non_contiguous=True), 16)
                    sprlev[k + 2] = p.s_br.n
            if n == 0 and cc >= 1:
                _p1_cc_tail(p, cc - 1, ygv, y_f, dts, tdec, scr16, amat,
                            dtch, addlev)
                r.wait("v", p.s_ld, ldlev[cc])
                r.wait("v", p.s_v)
                r.inc("v", p.s_v, lambda v, cc=cc: v.tensor_reduce(
                    dts[:, cc:cc + 1], dtch[:], AX.X, ALU.add))
        _p1_prod_add(p, NSTEP - 1, hslot, pslot, brep, ygv[NCH_I - 1],
                     scanlev, prodlev, addlev, sprlev, hloc)
        _p1_cc_tail(p, NCH_I - 1, ygv, y_f, dts, tdec, scr16, amat,
                    dtch, addlev)
        r.barrier(p)
        r.v(lambda v: v.tensor_copy(out=ststg[:, 0:256], in_=tdec[:]))
        r.inc("v", p.s_v, lambda v: v.tensor_copy(out=ststg[:, 256:512], in_=hloc[:]))
        r.wait("y", p.s_v)
        r.wait("y", p.s_sty)
        r.inc("y", p.s_sty, lambda y: y.dma_start(
            out=cc_st_in[:], in_=ststg[:, 0:512]), 16)
        r.wait("g", p.s_sty)
        r.inc("g", p.s_cc, lambda g: g.collective_compute(
            "AllGather", ALU.bypass, ins=[cc_st_in[:]], outs=[cc_st_out[:]],
            replica_groups=RG))
        r.wait("v", p.s_cc)
        r.inc("v", p.s_v, lambda v: v.memset(hstart[:], 0.0))
        r.wait("y", p.s_cc)
        for rr in range(NCORES - 1):
            r.wait("y", p.s_v)
            r.wait("y", p.s_ld)
            r.inc("y", p.s_ld, lambda y, rr=rr: y.dma_start(
                out=ststg[:, 512:1024], in_=cc_st_out[rr * 128:(rr + 1) * 128, :]),
                16)
            r.wait("v", p.s_ld)
            r.inc("v", p.s_v, lambda v, rr=rr: v.tensor_scalar(
                ststg[:, 512:768], ststg[:, 512:768],
                p.b_czmask[:, rr:rr + 1], p.b_czmaskc[:, rr:rr + 1],
                ALU.mult, ALU.add))
            r.wait("v", p.s_v)
            r.inc("v", p.s_v, lambda v, rr=rr: v.tensor_scalar(
                ststg[:, 768:1024], ststg[:, 768:1024],
                p.b_czmask[:, rr:rr + 1], None, ALU.mult))
            r.wait("v", p.s_v)
            r.inc("v", p.s_v, lambda v: v.tensor_tensor(
                hstart[:], hstart[:], ststg[:, 512:768], ALU.mult))
            r.wait("v", p.s_v)
            r.inc("v", p.s_v, lambda v: v.tensor_tensor(
                hstart[:], hstart[:], ststg[:, 768:1024], ALU.add))
            r.wait("v", p.s_v)
        cum_f = p.b_sq[:, 0:2048]
        decsl = hslot
        t1sl = [sqb[:, 4096:6144], sqb[:, 6144:8192]]
        zb = y_f.bitcast(BF16)[:, 0:2048]
        declev, t1lev, sttlev, cumlev, ld2 = {}, {}, {}, {}, {}
        r.wait("y", p.s_ld)
        r.inc("y", p.s_ld, lambda y: y.dma_start(
            out=dtch[:], in_=cc_dt_in[0:128, :]), 16)
        r.inc("y", p.s_ld, lambda y: y.dma_start(
            out=dxch[:], in_=dr_xc[0:128, :]), 16)
        r.inc("y", p.s_ld, lambda y: y.dma_start(
            out=zb, in_=dr_z[0:128, :]), 16)
        ld2[0] = p.s_ld.n
        r.wait("v", p.s_ld, ld2[0])
        r.wait("v", p.s_cc)
        r.inc("v", p.s_v, lambda v: v.tensor_tensor_scan(
            cum_f, p.b_ones1[:, 0:1].broadcast_to([128, TL]), dtch[:],
            p.b_consts[:, 2:3], ALU.mult, ALU.add))
        cumlev[0] = p.s_v.n
        for kk in range(2):
            bsl = (kk % 3) * 4096 + 2048
            r.wait("y", p.s_br)
            r.inc("y", p.s_br, lambda y, kk=kk, bsl=bsl: y.dma_start(
                out=brep[:, bsl:bsl + 2048],
                in_=cc_bc_in[NST + kk:NST + kk + 1, :].broadcast_to([128, TL]),
                allow_slow_non_contiguous=True), 16)
            sprlev[kk] = p.s_br.n
        for k in range(NSTEP):
            cc, n = divmod(k, NST)
            csl = slice(0, TL)
            if n == 0 and cc >= 1:
                _p2_gate(p, cc - 1, ygv, dxch, t1sl, zb, sttlev)
                r.wait("y", p.s_ld)
                r.wait("y", p.s_s)
                r.wait("y", p.s_v)
                r.inc("y", p.s_ld, lambda y, cc=cc: y.dma_start(
                    out=dtch[:], in_=cc_dt_in[cc * 128:(cc + 1) * 128, :]), 16)
                r.inc("y", p.s_ld, lambda y, cc=cc: y.dma_start(
                    out=dxch[:], in_=dr_xc[cc * 128:(cc + 1) * 128, :]), 16)
                r.inc("y", p.s_ld, lambda y, cc=cc: y.dma_start(
                    out=zb, in_=dr_z[cc * 128:(cc + 1) * 128, :]), 16)
                ld2[cc] = p.s_ld.n
                r.wait("v", p.s_ld, ld2[cc])
                r.wait("v", p.s_s)
                r.inc("v", p.s_v, lambda v: v.tensor_tensor_scan(
                    cum_f, p.b_ones1[:, 0:1].broadcast_to([128, TL]),
                    dtch[:], p.b_consts[:, 2:3], ALU.mult, ALU.add))
                cumlev[cc] = p.s_v.n
            if k >= 2:
                r.wait("s", p.s_v, t1lev[k - 2])
            r.wait("s", p.s_v, cumlev[cc])
            r.inc("s", p.s_s, lambda s, k=k: s.activation(
                out=decsl[k % 2], in_=cum_f, func=AF.Exp,
                scale=amat[:, k:k + 1]))
            declev[k] = p.s_s.n
            r.wait("v", p.s_br, sprlev[k])
            r.wait("v", p.s_s, declev[k])
            r.inc("v", p.s_v, lambda v, k=k, bsl=(k % 3) * 4096 + 2048:
                  v.tensor_tensor(t1sl[k % 2], decsl[k % 2],
                                  brep[:, bsl:bsl + 2048], ALU.mult))
            t1lev[k] = p.s_v.n
            r.wait("v", p.s_v)
            r.inc("v", p.s_v, lambda v, k=k, cc=cc: v.scalar_tensor_tensor(
                ygv[cc], t1sl[k % 2], hstart[:, k:k + 1], ygv[cc],
                ALU.mult, ALU.add))
            sttlev[k] = p.s_v.n
            k2 = k + 2
            if k2 < NSTEP:
                n2 = k2 % NST
                bsl2 = (k2 % 3) * 4096 + 2048
                r.wait("y", p.s_br)
                r.wait("y", p.s_v, t1lev.get(k - 1, 0))
                r.inc("y", p.s_br, lambda y, n2=n2, bsl2=bsl2: y.dma_start(
                    out=brep[:, bsl2:bsl2 + 2048],
                    in_=cc_bc_in[NST + n2:NST + n2 + 1, :].broadcast_to([128, TL]),
                    allow_slow_non_contiguous=True), 16)
                sprlev[k2] = p.s_br.n
        _p2_gate(p, NCH_I - 1, ygv, dxch, t1sl, zb, sttlev)
        r.barrier(p)

        if True:
            wcol2 = brep[:, 0:4096]
            mm2 = p.s_mm.n
            w2slots = [wcol2[:, b * 1024:(b + 1) * 1024] for b in range(4)]
            def wout_dma(g, i):
                j, h = divmod(i, 2)
                slot4 = i % 4
                srcj = p.d_wout[h * 1024:(h + 1) * 1024, j * 128:(j + 1) * 128].rearrange(
                    "(kc q) o -> q kc o", q=128)
                return g.dma_start(
                    out=w2slots[slot4][:].rearrange("q (kc o) -> q kc o", kc=NCH), in_=srcj)
            def wout_throttle(i):
                j = i // 2
                if j >= 2:
                    r.wait("g", p.s_mm, mm2 + (j - 1) * mm_per_j)
            wf_wo = wave_load(p, 2 * NCH, wout_dma, throttle=wout_throttle)
            for j in range(NCH):
                r.wait("t", *wf_wo[2 * j])
                r.wait("t", *wf_wo[2 * j + 1])
                for it in range(mm_per_j):
                    sl = slice(it * 512, (it + 1) * 512)
                    psum = p.ps_mm if it % 2 == 0 else p.ps_mm2
                    if j > 0 or it > 1:
                        r.wait("t", p.s_v)
                    for kc in range(NCH_I):
                        slot4 = (2 * j + kc // 8) % 4
                        kcl = kc % 8
                        fn = lambda t, kcl=kcl, sl=sl, slot4=slot4, psum=psum, kc=kc: t.matmul(
                            psum[:, 0:512], w2slots[slot4][:, kcl * 128:(kcl + 1) * 128],
                            ygv[kc][:, sl], start=(kc == 0), stop=(kc == NCH_I - 1))
                        if kc == NCH_I - 1:
                            r.inc("t", p.s_mm, fn)
                        else:
                            r.t(fn)
                    r.wait("v", p.s_mm)
                    r.inc("v", p.s_v, lambda v, j=j, sl=sl, psum=psum: v.tensor_tensor(
                        xv[j][:, sl], xv[j][:, sl], psum[:, 0:512], ALU.add))
            r.barrier(p)


def _p1_prod_add(p, k, hslot, pslot, brep, ygvcc, scanlev, prodlev, addlev,
                 sprlev, hloc):
    r = p.r
    n = k % 16
    bsl = (k % 3) * 4096 + 2048
    r.wait("g", p.s_v, scanlev[k])
    if k >= 2:
        r.wait("g", p.s_v, addlev[k - 2])
    r.wait("g", p.s_br, sprlev[k])
    r.g(lambda g, k=k: g.tensor_copy(
        out=hloc[:, k:k + 1], in_=hslot[k % 2][:, TL - 1:TL]))
    r.inc("g", p.s_g, lambda g, k=k, bsl=bsl: g.tensor_tensor(
        pslot[k % 2], hslot[k % 2], brep[:, bsl:bsl + 2048], ALU.mult))
    prodlev[k] = p.s_g.n
    r.wait("v", p.s_g, prodlev[k])
    if n == 0:
        r.inc("v", p.s_v, lambda v, k=k, ygvcc=ygvcc: v.tensor_copy(
            out=ygvcc, in_=pslot[k % 2]))
    else:
        r.inc("v", p.s_v, lambda v, k=k, ygvcc=ygvcc: v.tensor_tensor(
            ygvcc, ygvcc, pslot[k % 2], ALU.add))
    addlev[k] = p.s_v.n


def _p1_cc_tail(p, cc, ygv, y_f, dts, tdec, scr16, amat, dtch, addlev):
    r = p.r
    r.wait("v", p.s_s)
    r.wait("v", p.s_v)
    r.inc("v", p.s_v, lambda v, cc=cc: v.tensor_scalar(
        scr16[:], amat[:, cc * 16:(cc + 1) * 16], dts[:, cc:cc + 1],
        None, ALU.mult))
    r.wait("s", p.s_v)
    r.inc("s", p.s_s, lambda s, cc=cc: s.activation(
        out=tdec[:, cc * 16:(cc + 1) * 16], in_=scr16[:], func=AF.Exp))


def _p2_gate(p, cc, ygv, dxch, t1sl, zb, sttlev):
    r = p.r
    csl = slice(0, 2048)
    r.wait("v", p.s_v)
    r.inc("v", p.s_v, lambda v, cc=cc, csl=csl: v.scalar_tensor_tensor(
        t1sl[0], dxch[:, csl], p.b_dvec[:, cc:cc + 1], ygv[cc],
        ALU.mult, ALU.add))
    r.wait("v", p.s_v)
    r.inc("v", p.s_v, lambda v, cc=cc: v.tensor_tensor(
        ygv[cc], t1sl[0], zb, ALU.mult))


def kernel(**inputs):
    nc, out_names = build_program(stages=("attn_a", "ffn1", "attn_b", "ffn2"))
    in_maps = make_host_inputs(inputs)
    res = run_bass_kernel_spmd(nc, in_maps, core_ids=list(range(NCORES)))
    return assemble_output(res.results)


# version 18 (final)
# speedup vs baseline: 1.0365x; 1.0365x over previous
import contextlib
import numpy as np
import ml_dtypes
import concourse.bass as bass
import concourse.mybir as mybir
from concourse.bass_utils import run_bass_kernel_spmd

F32 = mybir.dt.float32
BF16 = mybir.dt.bfloat16
AF = mybir.ActivationFunctionType
ALU = mybir.AluOpType
AX = mybir.AxisListType

NCORES = 8
T, D = 16384, 1024
TL = T // NCORES
NCH = D // 128
H, DK, SEGL = 16, 64, 64
NSEG = TL // SEGL
DI, NST, DCONV = 2048, 16, 4
NCH_I = DI // 128
DI_SH = DI // NCORES
MAGIC = 12582912.0
LN_EPS = 1e-5
ATTN_EPS = 1e-6
QMAX = 127.0
THRESH = 0.7
RG = [[0, 1, 2, 3, 4, 5, 6, 7]]
ENGS = ("y", "t", "v", "s", "g")


class Sem:
    def __init__(self, handle, name):
        self.h, self.name, self.n = handle, name, 0


class Rec:
    def __init__(self):
        self.cmds = {e: [] for e in ENGS}

    def _add(self, e, fn):
        self.cmds[e].append(fn)

    def y(self, fn): self._add("y", fn)
    def t(self, fn): self._add("t", fn)
    def v(self, fn): self._add("v", fn)
    def s(self, fn): self._add("s", fn)
    def g(self, fn): self._add("g", fn)

    def inc(self, e, sem, fn, k=1):
        sem.n += k
        self._add(e, lambda h, fn=fn, sem=sem, k=k: fn(h).then_inc(sem.h, k))

    def wait(self, e, sem, val=None):
        v = sem.n if val is None else val
        self._add(e, lambda h, sem=sem, v=v: h.wait_ge(sem.h, v))

    def barrier(self, p, engs=ENGS):
        for e in engs:
            for sem in p.all_sems:
                if sem.n:
                    self.wait(e, sem)

    def play(self, block):
        for e, meth in (("y", block.sync), ("t", block.tensor), ("v", block.vector),
                        ("s", block.scalar), ("g", block.gpsimd)):
            cmds = self.cmds[e]
            meth(lambda h, cmds=cmds: [fn(h) for fn in cmds])


class _P:
    pass


def wave_load(p, nblocks, emit_dma, wave=2, throttle=None):
    r = p.r
    nw = (nblocks + wave - 1) // wave
    wait_for = [None] * nblocks
    for w in range(nw):
        sem = p.s_wv[w % 4]
        r.wait("g", sem, sem.n)
        lo, hi = w * wave, min((w + 1) * wave, nblocks)
        if throttle is not None:
            throttle(lo)
        for i in range(lo, hi):
            r.inc("g", sem, lambda g, i=i: emit_dma(g, i), 16)
        for i in range(lo, hi):
            wait_for[i] = (sem, sem.n)
    return wait_for


def np_t(a):
    return np.ascontiguousarray(np.asarray(a, dtype=np.float32).T)


def chunk_views(buf, n, w):
    return [buf[:, c * w:(c + 1) * w] for c in range(n)]


def build_program(stages=("ffn1", "ffn2"), debug_taps=()):
    nc = bass.Bass("TRN2", target_bir_lowering=False, debug=False)
    p = _P()
    p.nc = nc
    p.debug_taps = set(debug_taps)
    p.dbg_outs = []
    p.stages = stages

    def din(name, shape, dt=F32):
        return nc.dram_tensor(name, list(shape), dt, kind="ExternalInput").ap()

    def dint(name, shape, dt=F32, space="Local"):
        return nc.dram_tensor(name, list(shape), dt, kind="Internal",
                              addr_space=space).ap()

    p.d_x = din("xT", [D, TL])
    p.d_wq = din("wqT", [D, D]); p.d_wk = din("wkT", [D, D])
    p.d_wv = din("wvT", [D, D]); p.d_wo = din("woT", [D, D])
    p.d_beta = din("beta_param", [1, H])
    p.d_norm_w = din("norm_w", [1, D]); p.d_norm_b = din("norm_b", [1, D])
    p.d_win = din("winT", [D, 2 * DI])
    p.d_convw = din("conv_w", [DI, DCONV]); p.d_convb = din("conv_b", [1, DI])
    p.d_wdt = din("wdtT", [DI, DI]); p.d_bdt = din("b_dt", [1, DI])
    p.d_wB = din("wBT", [DI, NST]); p.d_wC = din("wCT", [DI, NST])
    p.d_D = din("Dvec", [1, DI]); p.d_alog = din("A_log", [DI, NST])
    p.d_wout = din("woutT", [DI, D])
    p.d_fw = din("fwT", [D, D])
    p.d_gam = din("gamma", [1, D]); p.d_bet = din("beta", [1, D])
    p.d_ident = din("identity", [128, 128], BF16)
    p.d_cid = din("cid", [1, 1], mybir.dt.int32)
    p.d_czero = din("czero", [128, 1])
    p.d_identf = din("identityf", [128, 128])
    p.d_cmask = din("cmask", [128, SEGL])
    p.d_czmask = din("czmask", [128, NCORES])
    p.d_out = nc.dram_tensor("outT", [D, TL], F32, kind="ExternalOutput").ap()

    p.cc_mx_in = dint("cc_mx_in", [128, 1])
    p.cc_mx_out = dint("cc_mx_out", [NCORES * 128, 1])
    p.dr_wq = dint("dr_wq", [128, NCH * D], BF16)

    p.exit_stack = contextlib.ExitStack()
    st = p.exit_stack
    p.sb = lambda name, shape, dt=F32: st.enter_context(nc.sbuf_tensor(name, list(shape), dt))
    p.psb = lambda name, shape: st.enter_context(nc.psum_tensor(name, list(shape), F32))

    def sem(name):
        s = Sem(st.enter_context(nc.semaphore(name)), name)
        p.all_sems.append(s)
        return s
    p.all_sems = []

    p.b_x = p.sb("x_res", [128, NCH * TL])
    p.b_consts = p.sb("consts", [128, 16])
    p.b_ones1 = p.sb("ones1", [128, 1])
    p.b_ones128 = p.sb("ones128", [128, 128])
    p.b_ident = p.sb("identb", [128, 128], BF16)
    p.b_cmask = p.sb("cmaskb", [128, SEGL])
    p.b_czmask = p.sb("czmaskb", [128, NCORES])
    p.b_czmaskc = p.sb("czmaskc", [128, NCORES])
    p.b_sq = p.sb("sq_scr", [128, 4096])
    p.b_tmp = p.sb("tmp_scr", [128, 2048])
    p.b_rep = p.sb("rep", [128, 1024])
    p.b_str = p.sb("strow", [128, 512])
    p.b_str2 = p.sb("strow2", [1, 512])
    p.b_rcol = p.sb("rcol", [128, 48])
    p.b_trin = p.sb("trin", [128, 1024])
    p.b_identf = p.sb("identf", [128, 128])
    p.b_gam = p.sb("gam", [128, NCH]); p.b_bet = p.sb("bet", [128, NCH])
    p.b_nw = p.sb("normw", [128, NCH]); p.b_nb = p.sb("normb", [128, NCH])
    p.b_qsc = p.sb("qsc", [128, 4])
    p.b_cid = p.sb("cidb", [1, 1], mybir.dt.int32)
    p.b_czero = p.sb("czerob", [128, 1])
    p.b_qrow = p.sb("qrow", [64, 8])

    p.ps_mm = p.psb("ps_mm", [128, 512])
    p.ps_mm2 = p.psb("ps_mm2", [128, 512])

    p.s_ld = sem("s_ld"); p.s_wl = sem("s_wl")
    p.s_mm = sem("s_mm"); p.s_v = sem("s_v"); p.s_s = sem("s_s")
    p.s_g = sem("s_g"); p.s_st = sem("s_st"); p.s_cc = sem("s_cc")
    p.s_wg = sem("s_wg")
    p.s_st2 = sem("s_st2")
    p.s_sty = sem("s_sty")
    p.s_wv = [sem(f"s_wv{i}") for i in range(4)]
    p.s_br = sem("s_br")

    r = Rec()
    p.r = r

    stage_loads(p)
    with_scope = lambda f, *a: f(p, *a)
    if "attn_a" in stages:
        stage_attn_a(p)
    if "ffn1" in stages:
        stage_ffn(p, "ffn1")
    if "attn_b" in stages:
        stage_attn_b(p)
    if "ffn2" in stages:
        stage_ffn(p, "ffn2")
    stage_output(p)

    with nc.Block() as block:
        r.play(block)
    p.exit_stack.close()
    return nc, ["outT"] + p.dbg_outs


def dbg_tap(p, name, sb_ap, shape, dt=F32):
    if name not in p.debug_taps:
        return
    r = p.r
    d = p.nc.dram_tensor("dbg_" + name, list(shape), dt, kind="ExternalOutput").ap()
    p.dbg_outs.append("dbg_" + name)
    r.barrier(p, engs=("g",))
    r.inc("g", p.s_st, lambda g, d=d, a=sb_ap: g.dma_start(out=d[:], in_=a), 16)
    r.barrier(p)


def stage_loads(p):
    r = p.r
    x3 = p.d_x.rearrange("(c p) t -> c p t", p=128)
    for c in range(NCH):
        r.inc("y", p.s_ld, lambda y, c=c, x3=x3: y.dma_start(
            out=p.b_x[:, c * TL:(c + 1) * TL], in_=x3[c]), 16)
    r.inc("y", p.s_ld, lambda y: y.dma_start(out=p.b_ident[:], in_=p.d_ident[:]), 16)
    r.inc("y", p.s_ld, lambda y: y.dma_start(out=p.b_identf[:], in_=p.d_identf[:]), 16)
    r.inc("y", p.s_ld, lambda y: y.dma_start(out=p.b_cmask[:], in_=p.d_cmask[:]), 16)
    r.inc("y", p.s_ld, lambda y: y.dma_start(out=p.b_czmask[:], in_=p.d_czmask[:]), 16)
    for dst, srcv in ((p.b_gam, p.d_gam), (p.b_bet, p.d_bet),
                      (p.b_nw, p.d_norm_w), (p.b_nb, p.d_norm_b)):
        v3 = srcv.rearrange("o (c p) -> p c o", p=128)
        r.inc("y", p.s_ld,
              lambda y, dst=dst, v3=v3: y.dma_start(
                  out=dst[:].unsqueeze(2), in_=v3,
                  allow_slow_non_contiguous=True), 16)
    r.v(lambda v: v.memset(p.b_consts[:, 0:1], LN_EPS))
    r.v(lambda v: v.memset(p.b_consts[:, 1:2], 1.0))
    r.v(lambda v: v.memset(p.b_consts[:, 2:3], 0.0))
    r.v(lambda v: v.memset(p.b_consts[:, 3:4], -0.5))
    r.v(lambda v: v.memset(p.b_consts[:, 4:5], ATTN_EPS))
    r.v(lambda v: v.memset(p.b_consts[:, 5:6], -1.0))
    r.v(lambda v: v.memset(p.b_consts[:, 6:16], 0.0))
    r.wait("v", p.s_ld)
    r.inc("v", p.s_v, lambda v: v.tensor_scalar(
        p.b_czmaskc[:], p.b_czmask[:], -1.0, 1.0, ALU.mult, ALU.add))
    r.inc("v", p.s_v, lambda v: v.memset(p.b_ones1[:], 1.0))
    r.inc("v", p.s_v, lambda v: v.memset(p.b_ones128[:], 1.0))
    r.wait("y", p.s_ld)
    r.inc("y", p.s_ld, lambda y: y.dma_start(out=p.b_cid[:], in_=p.d_cid[:]), 16)
    r.inc("y", p.s_ld, lambda y: y.dma_start(out=p.b_czero[:], in_=p.d_czero[:]), 16)
    for e in ("t", "v", "s", "g", "y"):
        r.wait(e, p.s_ld)
        r.wait(e, p.s_v)
    def _load_cid_y(y):
        p.cid_y = p.nc.values_load(p.b_cid[0:1, 0:1], engines=[mybir.EngineType.SP])
        p.cy_blk = [p.cid_y * 2, p.cid_y * 2 + 1]
        p.cidp_y = (p.cid_y + NCORES - 1) % NCORES
    def _load_cid_g(g):
        p.cid_g = p.nc.values_load(p.b_cid[0:1, 0:1], engines=[mybir.EngineType.Pool])
        p.cidp_g = (p.cid_g + NCORES - 1) % NCORES
    r.y(_load_cid_y)
    r.g(_load_cid_g)


def emit_ln(p, x_view, dst_view, gam=None, bet=None, double=False, dst2_view=None,
            add_into=None):
    r, s_mm, s_v, s_s, s_g = p.r, p.s_mm, p.s_v, p.s_s, p.s_g
    need_p = add_into is not None
    emit_ln.ctr = getattr(emit_ln, "ctr", 0) + 1
    _c = emit_ln.ctr
    with p.nc.psum_tensor(f"ps_stat{_c}", [64, 512], F32) as lstat, \
         p.nc.psum_tensor(f"ps_tr{_c}", [128, 128], F32) as ltr, \
         p.nc.psum_tensor(f"ps_rep{_c}", [128, 1024], F32) as lrep:
        _emit_ln_body(p, x_view, dst_view, gam, bet, double, dst2_view, add_into,
                      lstat, ltr, lrep)


def _emit_ln_body(p, x_view, dst_view, gam, bet, double, dst2_view, add_into,
                  ps_stat_l, ps_tr_l, ps_rep_l):
    r, s_mm, s_v, s_s, s_g = p.r, p.s_mm, p.s_v, p.s_s, p.s_g
    need_p = add_into is not None
    for it in range(TL // 512):
        sl = slice(it * 512, (it + 1) * 512)
        sq0 = p.s_sq_base = 0
        for c in range(NCH):
            r.inc("s", s_s, lambda s, c=c, sl=sl: s.activation(
                out=p.b_sq[:, c * 512:(c + 1) * 512], in_=x_view[c][:, sl], func=AF.Square))
        r.wait("t", s_s)
        for c in range(NCH):
            r.t(lambda t, c=c, sl=sl: t.matmul(
                ps_stat_l[0:1, 0:512], p.b_ones1[:, 0:1], x_view[c][:, sl],
                start=(c == 0), stop=(c == NCH - 1)))
            fn = lambda t, c=c: t.matmul(
                ps_stat_l[32:33, 0:512], p.b_ones1[:, 0:1], p.b_sq[:, c * 512:(c + 1) * 512],
                start=(c == 0), stop=(c == NCH - 1))
            if c == NCH - 1:
                r.inc("t", s_mm, fn)
            else:
                r.t(fn)
        r.wait("v", s_mm)
        r.v(lambda v: v.tensor_copy(out=p.b_str[0:1, 0:512], in_=ps_stat_l[0:1, 0:512]))
        r.inc("v", s_v, lambda v: v.tensor_copy(out=p.b_str[32:33, 0:512], in_=ps_stat_l[32:33, 0:512]))
        r.wait("t", s_v)
        for j in range(4):
            r.inc("t", s_mm, lambda t, j=j: t.transpose(
                ps_tr_l[:, 0:128], p.b_str[:, j * 128:(j + 1) * 128], p.b_identf[:]))
            r.wait("v", s_mm)
            r.inc("v", s_v, lambda v, j=j: v.tensor_copy(
                out=p.b_rcol[:, j:j + 8:4].unsqueeze(2),
                in_=ps_tr_l[:, 0:33:32].unsqueeze(2)))
            r.wait("t", s_v)
        CM, CV, CR, CM2 = 8, 12, 16, 20
        CRP, CMP, CT = 24, 28, 32
        r.wait("v", s_v)
        r.v(lambda v: v.tensor_scalar(p.b_rcol[:, CM:CM + 4], p.b_rcol[:, 0:4], 1.0 / D, None, ALU.mult))
        r.inc("v", s_v, lambda v: v.tensor_scalar(p.b_rcol[:, CV:CV + 4], p.b_rcol[:, 4:8], 1.0 / D, None, ALU.mult))
        r.wait("v", s_v)
        r.inc("v", s_v, lambda v: v.tensor_tensor(p.b_rcol[:, CT:CT + 4], p.b_rcol[:, CM:CM + 4], p.b_rcol[:, CM:CM + 4], ALU.mult))
        r.wait("v", s_v)
        r.inc("v", s_v, lambda v: v.tensor_tensor(p.b_rcol[:, CV:CV + 4], p.b_rcol[:, CV:CV + 4], p.b_rcol[:, CT:CT + 4], ALU.subtract))
        r.wait("s", s_v)
        r.inc("s", s_s, lambda s: s.activation(out=p.b_rcol[:, CT:CT + 4], in_=p.b_rcol[:, CV:CV + 4], func=AF.Ln,
                                               bias=p.b_consts[:, 0:1], scale=p.b_consts[:, 1:2]))
        r.wait("s", s_s)
        r.inc("s", s_s, lambda s: s.activation(out=p.b_rcol[:, CR:CR + 4], in_=p.b_rcol[:, CT:CT + 4], func=AF.Exp,
                                               bias=p.b_consts[:, 2:3], scale=p.b_consts[:, 3:4]))
        if double:
            r.wait("v", s_s)
            r.inc("v", s_v, lambda v: v.tensor_scalar(p.b_rcol[:, CT:CT + 4], p.b_rcol[:, CV:CV + 4], LN_EPS, None, ALU.add))
            r.wait("v", s_v)
            r.inc("v", s_v, lambda v: v.reciprocal(p.b_rcol[:, CT + 4:CT + 8], p.b_rcol[:, CT:CT + 4]))
            r.wait("v", s_v)
            r.inc("v", s_v, lambda v: v.tensor_tensor(p.b_rcol[:, CT:CT + 4], p.b_rcol[:, CV:CV + 4], p.b_rcol[:, CT + 4:CT + 8], ALU.mult))
            r.wait("s", s_v)
            r.inc("s", s_s, lambda s: s.activation(out=p.b_rcol[:, CT:CT + 4], in_=p.b_rcol[:, CT:CT + 4], func=AF.Ln,
                                                   bias=p.b_consts[:, 0:1], scale=p.b_consts[:, 1:2]))
            r.wait("s", s_s)
            r.inc("s", s_s, lambda s: s.activation(out=p.b_rcol[:, CT:CT + 4], in_=p.b_rcol[:, CT:CT + 4], func=AF.Exp,
                                                   bias=p.b_consts[:, 2:3], scale=p.b_consts[:, 3:4]))
            r.wait("v", s_s)
            if need_p:
                r.inc("v", s_v, lambda v: v.tensor_copy(out=p.b_rcol[:, CRP:CRP + 4], in_=p.b_rcol[:, CR:CR + 4]))
                r.wait("v", s_v)
            r.inc("v", s_v, lambda v: v.tensor_tensor(p.b_rcol[:, CR:CR + 4], p.b_rcol[:, CR:CR + 4], p.b_rcol[:, CT:CT + 4], ALU.mult))
            r.wait("v", s_v)
        else:
            r.wait("v", s_s)
        r.inc("v", s_v, lambda v: v.tensor_tensor(p.b_rcol[:, CM2:CM2 + 4], p.b_rcol[:, CM:CM + 4], p.b_rcol[:, CR:CR + 4], ALU.mult))
        if need_p:
            r.inc("v", s_v, lambda v: v.tensor_tensor(p.b_rcol[:, CMP:CMP + 4], p.b_rcol[:, CM:CM + 4], p.b_rcol[:, CRP:CRP + 4], ALU.mult))
        r.wait("v", s_v)
        r.v(lambda v: v.tensor_copy(out=p.b_trin[:, 0:512:128].unsqueeze(2), in_=p.b_rcol[:, CR:CR + 4].unsqueeze(2)))
        ins_fn = lambda v: v.tensor_copy(out=p.b_trin[:, 32:512:128].unsqueeze(2), in_=p.b_rcol[:, CM2:CM2 + 4].unsqueeze(2))
        if need_p:
            r.v(ins_fn)
            r.v(lambda v: v.tensor_copy(out=p.b_trin[:, 64:512:128].unsqueeze(2), in_=p.b_rcol[:, CRP:CRP + 4].unsqueeze(2)))
            r.inc("v", s_v, lambda v: v.tensor_copy(out=p.b_trin[:, 512:1024:128].unsqueeze(2), in_=p.b_rcol[:, CMP:CMP + 4].unsqueeze(2)))
        else:
            r.inc("v", s_v, ins_fn)
        r.wait("t", s_v)
        for j in range(4):
            r.inc("t", s_mm, lambda t, j=j: t.transpose(
                ps_tr_l[:, 0:128], p.b_trin[:, j * 128:(j + 1) * 128], p.b_identf[:]))
            r.wait("v", s_mm)
            r.v(lambda v, j=j: v.tensor_copy(out=p.b_str[0:1, j * 128:(j + 1) * 128], in_=ps_tr_l[0:1, 0:128]))
            r.v(lambda v, j=j: v.tensor_copy(out=p.b_str[32:33, j * 128:(j + 1) * 128], in_=ps_tr_l[32:33, 0:128]))
            if need_p:
                r.inc("v", s_v, lambda v, j=j: v.tensor_copy(out=p.b_str[64:65, j * 128:(j + 1) * 128], in_=ps_tr_l[64:65, 0:128]))
            else:
                r.inc("v", s_v, lambda v, j=j: v.engine_nop())
            r.wait("t", s_v)
        if need_p:
            for j in range(4):
                r.inc("t", s_mm, lambda t, j=j: t.transpose(
                    ps_tr_l[:, 0:128], p.b_trin[:, 512 + j * 128:512 + (j + 1) * 128], p.b_identf[:]))
                r.wait("v", s_mm)
                r.inc("v", s_v, lambda v, j=j: v.tensor_copy(out=p.b_str2[0:1, j * 128:(j + 1) * 128], in_=ps_tr_l[0:1, 0:128]))
                r.wait("t", s_v)
        def apply_pass(r_base, m_row_ap, m_base, write_cb):
            r.wait("t", s_v)
            r.t(lambda t, sl=sl, r_base=r_base: t.matmul(
                ps_rep_l[:, 0:512], p.b_ones128[r_base:r_base + 1, :],
                p.b_str[r_base:r_base + 1, 0:512], start=True, stop=True))
            r.inc("t", s_mm, lambda t: t.matmul(
                ps_rep_l[:, 512:1024], p.b_ones128[m_base:m_base + 1, :],
                m_row_ap, start=True, stop=True))
            r.wait("v", s_mm)
            r.inc("v", s_v, lambda v: v.tensor_copy(out=p.b_rep[:, 0:1024], in_=ps_rep_l[:, 0:1024]))
            r.wait("v", s_v)
            r.wait("g", s_v)
            for c in range(NCH):
                slot = (c % 2) * 512
                slot2 = 1024 + (c % 2) * 512
                r.wait("v", s_g)
                r.inc("v", s_v, lambda v, c=c, sl=sl, slot=slot: v.tensor_tensor(
                    p.b_tmp[:, slot:slot + 512], x_view[c][:, sl], p.b_rep[:, 0:512], ALU.mult))
                r.wait("g", s_v)
                r.inc("g", s_g, lambda g, slot=slot, slot2=slot2: g.tensor_tensor(
                    p.b_tmp[:, slot2:slot2 + 512], p.b_tmp[:, slot:slot + 512],
                    p.b_rep[:, 512:1024], ALU.subtract))
                r.wait("v", s_g)
                write_cb(c, sl, slot2)
            r.wait("t", s_v)

        def main_write(c, sl2, slot2):
            if gam is None:
                fn = lambda v, c=c, sl2=sl2, slot2=slot2: v.tensor_copy(
                    out=dst_view[c][:, sl2], in_=p.b_tmp[:, slot2:slot2 + 512])
            else:
                fn = lambda v, c=c, sl2=sl2, slot2=slot2: v.tensor_scalar(
                    dst_view[c][:, sl2], p.b_tmp[:, slot2:slot2 + 512],
                    gam[:, c:c + 1], bet[:, c:c + 1], ALU.mult, ALU.add)
            r.inc("v", s_v, fn)
            if dst2_view is not None:
                if gam is None:
                    fn2 = lambda v, c=c, sl2=sl2, slot2=slot2: v.tensor_copy(
                        out=dst2_view[c][:, sl2], in_=p.b_tmp[:, slot2:slot2 + 512])
                else:
                    fn2 = lambda v, c=c, sl2=sl2, slot2=slot2: v.tensor_scalar(
                        dst2_view[c][:, sl2], p.b_tmp[:, slot2:slot2 + 512],
                        gam[:, c:c + 1], bet[:, c:c + 1], ALU.mult, ALU.add)
                r.inc("v", s_v, fn2)

        apply_pass(0, p.b_str[32:33, 0:512], 32, main_write)
        if need_p:
            def add_write(c, sl2, slot2):
                r.inc("v", s_v, lambda v, c=c, sl2=sl2, slot2=slot2: v.tensor_tensor(
                    add_into[c][:, sl2], add_into[c][:, sl2],
                    p.b_tmp[:, slot2:slot2 + 512], ALU.add))
            apply_pass(64, p.b_str2[0:1, 0:512], 0, add_write)


def stage_ffn_wq(p, wq_buf):
    r = p.r
    p.b_wq = wq_buf
    wqb = chunk_views(p.b_wq, NCH, D)
    fw3 = p.d_fw.rearrange("(c p) o -> c p o", p=128)
    stage_ffn_wq.ctr = getattr(stage_ffn_wq, "ctr", 0) + 1
    _c = stage_ffn_wq.ctr
    with p.nc.sbuf_tensor(f"fw32_{_c}", [128, NCH * D], F32) as fw32, \
         p.nc.psum_tensor(f"ps_wqstat{_c}", [64, 512], F32) as ps_stat_l:
        fb = chunk_views(fw32, NCH, D)
        for c in range(NCH):
            r.inc("y", p.s_wl, lambda y, c=c, fb=fb, fw3=fw3: y.dma_start(out=fb[c], in_=fw3[c]), 16)
        r.wait("s", p.s_wl)
        for c in range(NCH):
            pp = (c % 2) * 2048
            if c >= 2:
                r.wait("s", p.s_mm, p.s_mm.n - 1)
            r.inc("s", p.s_s, lambda s, c=c, pp=pp, fb=fb: s.activation(
                out=p.b_sq[:, pp:pp + 1024], in_=fb[c], func=AF.Abs))
            r.wait("t", p.s_s)
            r.t(lambda t, c=c, pp=pp: t.matmul(
                ps_stat_l[0:1, 0:512], p.b_ones1[:, 0:1], p.b_sq[:, pp:pp + 512],
                start=(c == 0), stop=False))
            r.inc("t", p.s_mm, lambda t, c=c, pp=pp: t.matmul(
                ps_stat_l[0:1, 0:512], p.b_ones1[:, 0:1], p.b_sq[:, pp + 512:pp + 1024],
                start=False, stop=(c == NCH - 1)))
        r.wait("v", p.s_mm)
        r.inc("v", p.s_v, lambda v: v.tensor_reduce(p.b_qrow[0:1, 0:1], ps_stat_l[0:1, 0:512], AX.X, ALU.add))
        r.wait("v", p.s_v)
        r.inc("v", p.s_v, lambda v: v.tensor_scalar(
            p.b_qrow[0:1, 1:2], p.b_qrow[0:1, 0:1], THRESH / (D * D), None, ALU.mult))
        r.wait("t", p.s_v)
        r.inc("t", p.s_mm, lambda t: t.matmul(
            p.ps_mm[:, 0:1], p.b_ones128[0:1, :], p.b_qrow[0:1, 1:2], start=True, stop=True))
        r.wait("v", p.s_mm)
        r.inc("v", p.s_v, lambda v: v.tensor_copy(out=p.b_qsc[:, 3:4], in_=p.ps_mm[:, 0:1]))
        r.wait("s", p.s_v)
        for c in range(NCH):
            pp = (c % 2) * 2048
            if c >= 2:
                r.wait("s", p.s_mm, p.s_mm.n - 1)
            r.inc("s", p.s_s, lambda s, c=c, pp=pp, fb=fb: s.activation(
                out=p.b_sq[:, pp:pp + 1024], in_=fb[c], func=AF.Abs))
            r.wait("v", p.s_s)
            qq = (c % 2) * 1024
            r.v(lambda v, pp=pp: v.scalar_tensor_tensor(
                p.b_sq[:, pp + 1024:pp + 2048], p.b_sq[:, pp:pp + 1024],
                p.b_qsc[:, 3:4], p.b_sq[:, pp:pp + 1024], ALU.is_gt, ALU.mult))
            r.inc("v", p.s_v, lambda v, pp=pp, qq=qq: v.tensor_scalar(
                p.b_tmp[:, qq:qq + 1024], p.b_sq[:, pp:pp + 1024],
                p.b_qsc[:, 3:4], None, ALU.is_gt))
            r.wait("t", p.s_v)
            r.t(lambda t, c=c, pp=pp: t.matmul(
                ps_stat_l[0:1, 0:512], p.b_ones1[:, 0:1], p.b_sq[:, pp + 1024:pp + 1536],
                start=(c == 0), stop=False))
            r.t(lambda t, c=c, pp=pp: t.matmul(
                ps_stat_l[0:1, 0:512], p.b_ones1[:, 0:1], p.b_sq[:, pp + 1536:pp + 2048],
                start=False, stop=(c == NCH - 1)))
            r.t(lambda t, c=c, qq=qq: t.matmul(
                ps_stat_l[32:33, 0:512], p.b_ones1[:, 0:1], p.b_tmp[:, qq:qq + 512],
                start=(c == 0), stop=False))
            r.inc("t", p.s_mm, lambda t, c=c, qq=qq: t.matmul(
                ps_stat_l[32:33, 0:512], p.b_ones1[:, 0:1], p.b_tmp[:, qq + 512:qq + 1024],
                start=False, stop=(c == NCH - 1)))
        r.wait("v", p.s_mm)
        r.v(lambda v: v.tensor_reduce(p.b_qrow[0:1, 2:3], ps_stat_l[0:1, 0:512], AX.X, ALU.add))
        r.inc("v", p.s_v, lambda v: v.tensor_reduce(p.b_qrow[0:1, 3:4], ps_stat_l[32:33, 0:512], AX.X, ALU.add))
        r.wait("v", p.s_v)
        r.inc("v", p.s_v, lambda v: v.tensor_scalar(p.b_qrow[0:1, 3:4], p.b_qrow[0:1, 3:4], 1.0, None, ALU.max))
        r.wait("v", p.s_v)
        r.inc("v", p.s_v, lambda v: v.reciprocal(p.b_qrow[0:1, 5:6], p.b_qrow[0:1, 3:4]))
        r.wait("v", p.s_v)
        r.inc("v", p.s_v, lambda v: v.tensor_tensor(
            p.b_qrow[0:1, 4:5], p.b_qrow[0:1, 2:3], p.b_qrow[0:1, 5:6], ALU.mult))
        r.wait("t", p.s_v)
        r.inc("t", p.s_mm, lambda t: t.matmul(
            p.ps_mm[:, 1:2], p.b_ones128[0:1, :], p.b_qrow[0:1, 4:5], start=True, stop=True))
        r.wait("v", p.s_mm)
        r.inc("v", p.s_v, lambda v: v.tensor_copy(out=p.b_qsc[:, 2:3], in_=p.ps_mm[:, 1:2]))
        r.wait("s", p.s_v)
        for c in range(NCH):
            pp = (c % 2) * 2048
            if c >= 2:
                r.wait("s", p.s_v, p.s_v.n - 1)
            r.inc("s", p.s_s, lambda s, c=c, pp=pp, fb=fb: s.activation(
                out=p.b_sq[:, pp:pp + 1024], in_=fb[c], func=AF.Sign))
            r.inc("s", p.s_s, lambda s, c=c, pp=pp, fb=fb: s.activation(
                out=p.b_sq[:, pp + 1024:pp + 2048], in_=fb[c], func=AF.Abs))
            r.wait("g", p.s_s)
            r.inc("g", p.s_g, lambda g, pp=pp: g.tensor_scalar(
                p.b_sq[:, pp:pp + 1024], p.b_sq[:, pp:pp + 1024],
                p.b_qsc[:, 2:3], None, ALU.mult))
            r.wait("v", p.s_g)
            r.inc("v", p.s_v, lambda v, c=c, pp=pp, wqb=wqb: v.scalar_tensor_tensor(
                wqb[c], p.b_sq[:, pp + 1024:pp + 2048], p.b_qsc[:, 3:4],
                p.b_sq[:, pp:pp + 1024], ALU.is_gt, ALU.mult))
        dbg_tap(p, "wq", p.b_wq[:], [128, NCH * D], BF16)
        r.barrier(p)


def stage_ffn(p, tag):
    r = p.r
    with p.nc.sbuf_tensor(tag + "_wq", [128, NCH * D], BF16) as wqbuf:
        if tag == "ffn1":
            stage_ffn_wq(p, wqbuf)
            r.wait("y", p.s_sty)
            r.inc("y", p.s_sty, lambda y, wqbuf=wqbuf: y.dma_start(
                out=p.dr_wq[:], in_=wqbuf[:]), 16)
            p.b_wq = wqbuf
        else:
            p.b_wq = wqbuf
            r.wait("y", p.s_ld)
            r.inc("y", p.s_ld, lambda y, wqbuf=wqbuf: y.dma_start(
                out=wqbuf[:], in_=p.dr_wq[:]), 16)
            for e in ("t", "v", "s", "g"):
                r.wait(e, p.s_ld)
        _stage_ffn_body(p, tag, wqbuf)


def _stage_ffn_body(p, tag, wqbuf):
    r = p.r
    with p.nc.sbuf_tensor(tag + "_xn", [128, NCH * TL], F32) as xf, \
         p.nc.sbuf_tensor(tag + "_xq", [128, NCH * (TL // 2)], BF16) as xq:
        xv = chunk_views(p.b_x, NCH, TL)
        xfv = chunk_views(xf, NCH, TL)
        emit_ln(p, xv, xfv, gam=p.b_gam, bet=p.b_bet, double=True)
        dbg_tap(p, tag + "_xn", xf[:], [128, NCH * TL])
        r.wait("v", p.s_v)
        r.inc("v", p.s_v, lambda v: v.tensor_reduce(
            p.b_qsc[:, 0:1], xf[:], AX.X, ALU.max, apply_absolute_value=True))
        r.wait("g", p.s_v)
        r.inc("g", p.s_st, lambda g: g.dma_start(
            out=p.cc_mx_in[:], in_=p.b_qsc[:, 0:1]), 16)
        r.wait("g", p.s_st)
        r.inc("g", p.s_cc, lambda g: g.collective_compute(
            "AllGather", ALU.bypass, ins=[p.cc_mx_in[:]], outs=[p.cc_mx_out[:]],
            replica_groups=RG))
        r.wait("g", p.s_cc)
        r.inc("g", p.s_st, lambda g: g.dma_start(
            out=p.b_tmp[0:1, 0:NCORES * 128],
            in_=p.cc_mx_out[:].rearrange("(a b) o -> a (b o)", a=1)), 16)
        r.wait("v", p.s_st)
        r.inc("v", p.s_v, lambda v: v.tensor_reduce(
            p.b_qrow[0:1, 5:6], p.b_tmp[0:1, 0:NCORES * 128], AX.X, ALU.max))
        r.wait("v", p.s_v)
        r.inc("v", p.s_v, lambda v: v.tensor_scalar(p.b_qrow[0:1, 6:7], p.b_qrow[0:1, 5:6],
                                      1e-8, 1.0 / QMAX, ALU.max, ALU.mult))
        r.wait("v", p.s_v)
        r.inc("v", p.s_v, lambda v: v.reciprocal(p.b_qrow[0:1, 7:8], p.b_qrow[0:1, 6:7]))
        r.wait("t", p.s_v)
        r.t(lambda t: t.matmul(p.ps_mm[:, 2:3], p.b_ones128[0:1, :],
                               p.b_qrow[0:1, 6:7], start=True, stop=True))
        r.inc("t", p.s_mm, lambda t: t.matmul(
            p.ps_mm[:, 3:4], p.b_ones128[0:1, :], p.b_qrow[0:1, 7:8],
            start=True, stop=True))
        r.wait("v", p.s_mm)
        r.v(lambda v: v.tensor_copy(out=p.b_qsc[:, 1:2], in_=p.ps_mm[:, 2:3]))
        r.inc("v", p.s_v, lambda v: v.tensor_copy(out=p.b_qsc[:, 0:1], in_=p.ps_mm[:, 3:4]))
        wqb = chunk_views(p.b_wq, NCH, D)
        HT = TL // 2
        xqv = chunk_views(xq, NCH, HT)
        for hf in range(2):
            t0 = hf * HT
            for c in range(NCH):
                sA = (c % 2) * 1024
                sB = 2048 + (c % 2) * 1024
                r.wait("v", p.s_v, p.s_v.n)
                r.inc("v", p.s_v, lambda v, c=c, t0=t0, sA=sA: v.tensor_scalar(
                    p.b_sq[:, sA:sA + HT], xfv[c][:, t0:t0 + HT],
                    p.b_qsc[:, 0:1], QMAX, ALU.mult, ALU.min))
                r.wait("v", p.s_v)
                r.inc("v", p.s_v, lambda v, sA=sA, sB=sB: v.tensor_scalar(
                    p.b_sq[:, sB:sB + HT], p.b_sq[:, sA:sA + HT], -QMAX, MAGIC, ALU.max, ALU.add))
                r.wait("v", p.s_v)
                fn = lambda v, c=c, sB=sB: v.tensor_scalar(
                    xqv[c], p.b_sq[:, sB:sB + HT], -MAGIC, p.b_qsc[:, 1:2], ALU.add, ALU.mult)
                r.inc("v", p.s_v, fn)
            r.wait("t", p.s_v)
            for oc in range(NCH):
                for it in range(HT // 512):
                    sl = slice(it * 512, (it + 1) * 512)
                    gsl = slice(t0 + it * 512, t0 + (it + 1) * 512)
                    for kc in range(NCH):
                        fn = lambda t, oc=oc, kc=kc, sl=sl: t.matmul(
                            p.ps_mm[:, 0:512], wqb[kc][:, oc * 128:(oc + 1) * 128],
                            xqv[kc][:, sl], start=(kc == 0), stop=(kc == NCH - 1))
                        if kc == NCH - 1:
                            r.inc("t", p.s_mm, fn)
                        else:
                            r.t(fn)
                    r.wait("v", p.s_mm)
                    r.inc("v", p.s_v, lambda v, oc=oc, gsl=gsl: v.tensor_tensor(
                        xv[oc][:, gsl], xv[oc][:, gsl], p.ps_mm[:, 0:512], ALU.add))
                    r.wait("t", p.s_v)
                    if tag == "ffn2" and hf == 1 and it == HT // 512 - 1:
                        o3s = p.d_out.rearrange("(c p) t -> c p t", p=128)
                        r.wait("g", p.s_v)
                        r.inc("g", p.s_st, lambda g, oc=oc, o3s=o3s: g.dma_start(
                            out=o3s[oc], in_=p.b_x[:, oc * TL:(oc + 1) * TL]), 16)
        r.barrier(p)


def stage_output(p):
    r = p.r
    if "ffn2" in p.stages:
        r.barrier(p)
        return
    o3 = p.d_out.rearrange("(c p) t -> c p t", p=128)
    r.barrier(p, engs=("g",))
    for c in range(NCH):
        r.inc("g", p.s_st, lambda g, c=c: g.dma_start(
            out=o3[c], in_=p.b_x[:, c * TL:(c + 1) * TL]), 16)
    r.wait("g", p.s_st)



def make_host_inputs(inputs):
    x = np.asarray(inputs["L0_attn_a_x"])[0]
    xT = np_t(x)
    ident = np.eye(128, dtype=ml_dtypes.bfloat16)
    cm1 = np.triu(np.full((SEGL, SEGL), -1e30, dtype=np.float32), k=1)
    cmask = np.ascontiguousarray(np.vstack([cm1, cm1]))
    base = {
        "wqT": np_t(inputs["L0_attn_a_W_q"]), "wkT": np_t(inputs["L0_attn_a_W_k"]),
        "wvT": np_t(inputs["L0_attn_a_W_v"]), "woT": np_t(inputs["L0_attn_a_W_o"]),
        "beta_param": np.asarray(inputs["L0_attn_a_beta_param"], np.float32).reshape(1, H),
        "norm_w": np.asarray(inputs["L0_attn_b_norm_w"], np.float32).reshape(1, D),
        "norm_b": np.asarray(inputs["L0_attn_b_norm_b"], np.float32).reshape(1, D),
        "winT": np_t(inputs["L0_attn_b_W_in"]),
        "conv_w": np.asarray(inputs["L0_attn_b_conv_w"], np.float32),
        "conv_b": np.asarray(inputs["L0_attn_b_conv_b"], np.float32).reshape(1, DI),
        "wdtT": np_t(inputs["L0_attn_b_W_dt"]),
        "b_dt": np.asarray(inputs["L0_attn_b_b_dt"], np.float32).reshape(1, DI),
        "wBT": np_t(inputs["L0_attn_b_W_B"]), "wCT": np_t(inputs["L0_attn_b_W_C"]),
        "Dvec": np.asarray(inputs["L0_attn_b_D"], np.float32).reshape(1, DI),
        "A_log": np.asarray(inputs["L0_attn_b_A_log"], np.float32),
        "woutT": np_t(inputs["L0_attn_b_W_out"]),
        "fwT": np_t(inputs["L0_ffn_weight"]),
        "gamma": np.asarray(inputs["L0_ffn_gamma"], np.float32).reshape(1, D),
        "beta": np.asarray(inputs["L0_ffn_beta"], np.float32).reshape(1, D),
        "identity": ident, "identityf": np.eye(128, dtype=np.float32), "cmask": cmask,
    }
    in_maps = []
    for c in range(NCORES):
        m = dict(base)
        m["xT"] = np.ascontiguousarray(xT[:, c * TL:(c + 1) * TL])
        m["cid"] = np.array([[c]], dtype=np.int32)
        m["czero"] = np.full((128, 1), 0.0 if c == 0 else 1.0, dtype=np.float32)
        zm = np.zeros((128, NCORES), dtype=np.float32)
        zm[:, :c] = 1.0
        m["czmask"] = zm
        in_maps.append(m)
    return in_maps


def assemble_output(results):
    outT = np.concatenate([r["outT"] for r in results], axis=1)
    return np.ascontiguousarray(outT.T)[None]



NPAIR = H // 2


class _nullcm:
    def __init__(self, v):
        self.v = v
    def __enter__(self):
        return self.v
    def __exit__(self, *a):
        return False
HT = TL // 2
NSEG_H = NSEG // 2
ISQRT_DK = 1.0 / np.sqrt(DK)


def stage_attn_a(p):
    r = p.r
    nc = p.nc
    PH = 99

    def dint(name, shape, dt=F32):
        return nc.dram_tensor(name, list(shape), dt, kind="Internal").ap()

    dr_sqskf = dint("dr_sqskf", [NSEG, 128, 1024], BF16)
    dr_pt = dint("dr_pt", [NSEG, 64, 1024], BF16)
    dr_vtm = dint("dr_vtm", [NSEG, 64, 1024], BF16)
    dr_asm = dint("dr_asm", [NSEG, 64, 1024], BF16)
    cc_z_in = dint("cc_z_in", [128, 8])
    cc_z_out = dint("cc_z_out", [NCORES * 128, 8])
    cc_mg_in = dint("cc_mg_in", [2 * 128, 1024], BF16)
    cc_mg_out = dint("cc_mg_out", [NCORES * 2 * 128, 1024], BF16)

    es = contextlib.ExitStack()
    sbl = lambda name, shape, dt=F32: es.enter_context(
        nc.sbuf_tensor(name, list(shape), dt))
    p.b_zss = sbl("zss", [128, NCH * NSEG])
    p.b_zs2 = sbl("zs2", [128, NCH * NCORES])
    p.b_zall = sbl("zall", [128, NPAIR * (NSEG + 1) * 2], BF16)
    p.b_rz = sbl("rzall", [128, H * NSEG])
    p.b_coef = sbl("coef", [128, 64])
    p.b_cm8 = sbl("cm8", [128, 8 * SEGL])
    p.b_vring = sbl("vring", [128, 2 * 1024], BF16)

    r.wait("v", p.s_ld)
    for pr in range(NPAIR):
        fn = lambda v, pr=pr: v.tensor_copy(
            out=p.b_cm8[:, pr * SEGL:(pr + 1) * SEGL], in_=p.b_cmask[:])
        if pr == NPAIR - 1:
            r.inc("v", p.s_v, fn)
        else:
            r.v(fn)

    with nc.sbuf_tensor("xn1", [128, NCH * TL], BF16) as xn1:
        xv = chunk_views(p.b_x, NCH, TL)
        xn1v = chunk_views(xn1, NCH, TL)
        r.barrier(p)
        emit_ln(p, xv, xn1v, double=False)
        dbg_tap(p, "xn1", xn1[:], [128, NCH * TL], BF16)
        r.barrier(p)
        if PH >= 1:
            _beta_coefs(p)
        for hf in range(2):
            if PH >= 2 + hf:
                _prep_half(p, hf, xn1v, dr_sqskf, dr_pt)
        if PH >= 4:
            _vproj(p, xn1v, dr_vtm)
        r.barrier(p)
    if PH >= 5:
        _zstage(p, cc_z_in, cc_z_out)
    r.barrier(p)

    es2 = contextlib.ExitStack()
    sb2 = lambda name, shape, dt=BF16: es2.enter_context(
        nc.sbuf_tensor(name, list(shape), dt))
    a_fm = sb2("a_fm", [128, NCH * TL])
    p.b_sring = sb2("sring", [128, 2 * 1024])
    p.b_ptv = sb2("ptv", [128, 2 * 1024])
    p.b_aring = sb2("aring", [128, 2 * 1024])
    p.b_vdu = sb2("vdu", [128, 2048])
    p.b_skt = sb2("sktb", [128, 1024])
    p.b_mg = sb2("mgbuf", [128, 2048])
    p.b_mgf = sb2("mgf", [128, 2048], F32)
    p.b_mst = sb2("mstbuf", [128, 1024])
    p.b_w = sb2("wbuf", [128, 1024])
    p.b_wf = sb2("wf", [128, 1024], F32)
    p.b_gt = sb2("gtbuf", [128, 1024])
    p.b_gtid = sb2("gtid", [128, 1024])
    p.b_mgag = sb2("mgag", [128, 2048])
    with nc.psum_tensor("ps_mg", [128, 1024], F32) as ps_mg, \
         nc.psum_tensor("ps_cba", [128, 1024], F32) as cba, \
         nc.psum_tensor("ps_cbb", [128, 1024], F32) as cbb:
        if PH >= 6:
            _chain(p, dr_sqskf, dr_pt, dr_vtm, dr_asm, ps_mg, cba, cbb)
        r.barrier(p)
        if PH >= 7:
            _mg_exchange(p, cc_mg_in, cc_mg_out, cba, cbb)
        r.barrier(p)
        if PH >= 8:
            _rerun(p, dr_sqskf, dr_asm, a_fm, ps_mg, cba, cbb)
        r.barrier(p)
        dbg_tap(p, "a_fm", a_fm[:], [128, NCH * TL], BF16)
        afmv = chunk_views(a_fm, NCH, TL)
        xvr = chunk_views(p.b_x, NCH, TL)

        def wo_out(j, it, psum):
            sl = slice(it * 512, (it + 1) * 512)
            r.inc("v", p.s_v, lambda v, j=j, sl=sl, psum=psum: v.tensor_tensor(
                xvr[j][:, sl], xvr[j][:, sl], psum[:, 0:512], ALU.add))
        if PH >= 9:
            _proj_fm(p, p.d_wo, afmv, TL, wo_out)
    r.barrier(p)
    es2.close()
    es.close()


def _beta_coefs(p):
    r = p.r
    r.wait("y", p.s_ld)
    r.inc("y", p.s_ld, lambda y: y.dma_start(
        out=p.b_coef[0:1, 48:64], in_=p.d_beta[:]), 16)
    r.wait("s", p.s_ld)
    r.inc("s", p.s_s, lambda s: s.activation(
        out=p.b_coef[0:1, 32:48], in_=p.b_coef[0:1, 48:64], func=AF.Exp,
        bias=p.b_consts[0:1, 2:3], scale=p.b_consts[0:1, 5:6]))
    r.wait("v", p.s_s)
    r.inc("v", p.s_v, lambda v: v.tensor_scalar(
        p.b_coef[0:1, 32:48], p.b_coef[0:1, 32:48], 1.0, None, ALU.add))
    r.wait("v", p.s_v)
    r.inc("v", p.s_v, lambda v: v.reciprocal(
        p.b_coef[0:1, 48:64], p.b_coef[0:1, 32:48]))
    r.wait("t", p.s_v)
    r.inc("t", p.s_mm, lambda t: t.matmul(
        p.ps_mm[:, 0:16], p.b_ones128[0:1, :], p.b_coef[0:1, 48:64],
        start=True, stop=True))
    r.wait("v", p.s_mm)
    r.inc("v", p.s_v, lambda v: v.tensor_copy(
        out=p.b_coef[:, 0:16], in_=p.ps_mm[:, 0:16]))
    r.wait("v", p.s_v)
    r.inc("v", p.s_v, lambda v: v.tensor_scalar(
        p.b_coef[:, 16:32], p.b_coef[:, 0:16], -1.0, 1.0, ALU.mult, ALU.add))
    r.wait("t", p.s_v)


def _proj_fm(p, wsrc, xviews, ntok, out_cb, wcol=None):
    r = p.r
    ng = ntok // 512
    mm0 = p.s_mm.n
    evlev = []
    ext = wcol is not None
    _proj_fm.ctr = getattr(_proj_fm, "ctr", 0) + 1
    with (p.nc.sbuf_tensor(f"wcolp{_proj_fm.ctr}", [128, 4 * 1024], BF16)
          if not ext else _nullcm(wcol)) as wcol:
        if ext:
            r.wait("g", p.s_mm)
            r.wait("t", p.s_v)

        def dma(g, j):
            slot = j % 4
            srcj = wsrc[:, j * 128:(j + 1) * 128].rearrange(
                "(kc q) o -> q kc o", q=128)
            return g.dma_start(
                out=wcol[:, slot * 1024:(slot + 1) * 1024].rearrange(
                    "q (kc o) -> q kc o", kc=NCH), in_=srcj)

        def throttle(lo):
            jmax = lo + 1
            if jmax >= 4:
                r.wait("g", p.s_mm, mm0 + (jmax - 3) * ng)
        wf = wave_load(p, NCH, dma, throttle=throttle)
        for j in range(NCH):
            slot = j % 4
            r.wait("t", *wf[j])
            for it in range(ng):
                sl = slice(it * 512, (it + 1) * 512)
                g_i = j * ng + it
                psum = p.ps_mm if g_i % 2 == 0 else p.ps_mm2
                if g_i >= 2:
                    r.wait("t", p.s_v, evlev[g_i - 2])
                for kc in range(NCH):
                    fn = lambda t, kc=kc, sl=sl, slot=slot, psum=psum: t.matmul(
                        psum[:, 0:512],
                        wcol[:, slot * 1024 + kc * 128:slot * 1024 + (kc + 1) * 128],
                        xviews[kc][:, sl], start=(kc == 0), stop=(kc == NCH - 1))
                    if kc == NCH - 1:
                        r.inc("t", p.s_mm, fn)
                    else:
                        r.t(fn)
                r.wait("v", p.s_mm)
                out_cb(j, it, psum)
                evlev.append(p.s_v.n)
        if not ext:
            r.barrier(p)


def _prep_half(p, hf, xn1v, dr_sqskf, dr_pt):
    r = p.r
    nc = p.nc
    s0 = hf * NSEG_H
    with nc.sbuf_tensor(f"qf{hf}", [128, NCH * HT], BF16) as qf, \
         nc.sbuf_tensor(f"kf{hf}", [128, NCH * HT], BF16) as kf, \
         nc.sbuf_tensor(f"sqs{hf}", [128, 2048], BF16) as sqs, \
         nc.sbuf_tensor(f"scr{hf}", [128, 1024], F32) as scr, \
         nc.sbuf_tensor(f"scrb{hf}", [128, 1024], BF16) as scrb, \
         nc.sbuf_tensor(f"scrb2{hf}", [128, 1024], BF16) as scrb2, \
         nc.sbuf_tensor(f"pstg{hf}", [128, 2048], BF16) as pstg, \
         nc.sbuf_tensor(f"ptsb{hf}", [128, 2048], BF16) as ptsb, \
         nc.sbuf_tensor(f"smrow{hf}", [128, 32], F32) as smrow:
        qfv = chunk_views(qf, NCH, HT)
        kfv = chunk_views(kf, NCH, HT)
        x_half = [xc[:, hf * HT:(hf + 1) * HT] for xc in xn1v]

        def q_out(j, it, psum):
            r.inc("v", p.s_v, lambda v, j=j, it=it, psum=psum: v.tensor_copy(
                out=qfv[j][:, it * 512:(it + 1) * 512], in_=psum[:, 0:512]))

        def k_out(j, it, psum):
            r.inc("v", p.s_v, lambda v, j=j, it=it, psum=psum: v.tensor_copy(
                out=kfv[j][:, it * 512:(it + 1) * 512], in_=psum[:, 0:512]))
        with nc.sbuf_tensor(f"wqk{hf}", [128, 4 * 1024], BF16) as wqk:
            _proj_fm(p, p.d_wq, x_half, HT, q_out, wcol=wqk)
            _proj_fm(p, p.d_wk, x_half, HT, k_out, wcol=wqk)
            r.barrier(p)

        stylev = {}
        for c in range(NCH):
            base = 0
            sq3 = sqs[:].rearrange("q (s w) -> q s w", w=128)
            if c >= 1:
                r.wait("v", p.s_sty, stylev[c - 1])
            for src, dlo, dhi in ((qfv[c], 0, 64), (kfv[c], 64, 128)):
                r.wait("v", p.s_s)
                r.inc("v", p.s_v, lambda v, src=src: v.tensor_scalar(
                    scr[:, 0:1024], src, 0.0, None, ALU.min))
                r.wait("s", p.s_v)
                r.inc("s", p.s_s, lambda s: s.activation(
                    out=scrb[:], in_=scr[:, 0:1024], func=AF.Exp))
                r.wait("v", p.s_s)
                r.inc("v", p.s_v, lambda v, src=src, sq3=sq3, dlo=dlo, dhi=dhi:
                      v.scalar_tensor_tensor(
                          sq3[:, :, dlo:dhi],
                          src.rearrange("q (s w) -> q s w", w=64),
                          p.b_consts[:, 2:3],
                          scrb[:].rearrange("q (s w) -> q s w", w=64),
                          ALU.max, ALU.add))
            r.wait("v", p.s_v)
            r.inc("v", p.s_v, lambda v, c=c, sq3=sq3: v.tensor_reduce(
                p.b_zss[:, c * NSEG + s0:c * NSEG + s0 + NSEG_H],
                sq3[:, :, 64:128], AX.X, ALU.add))
            r.wait("y", p.s_v)
            r.wait("y", p.s_sty)
            for ls in range(NSEG_H):
                fn = lambda y, c=c, ls=ls, base=base: y.dma_start(
                    out=dr_sqskf[s0 + ls, :, c * 128:(c + 1) * 128],
                    in_=sqs[:, base + ls * 128:base + (ls + 1) * 128])
                r.inc("y", p.s_sty, fn, 16)
            stylev[c] = p.s_sty.n
        r.barrier(p)

        ptlev = {}
        r.wait("v", p.s_v)
        r.inc("v", p.s_v, lambda v: v.memset(pstg[:], 0.0))
        r.wait("v", p.s_v)
        for lp in range(NSEG_H // 2):
            for e in range(2):
                segsl = slice((2 * lp + e) * 64, (2 * lp + e) * 64 + 64)
                psd = p.ps_mm if e == 0 else p.ps_mm2
                for h in range(H):
                    c, par = h // 2, (h % 2) * 64
                    fn = lambda t, c=c, par=par, segsl=segsl, psd=psd: t.matmul(
                        psd[par:par + 64, c * 64:c * 64 + 64],
                        qfv[c][par:par + 64, segsl], kfv[c][par:par + 64, segsl],
                        start=True, stop=True, tile_position=(par, par))
                    if h == H - 1 and e == 1:
                        r.inc("t", p.s_mm, fn)
                    else:
                        r.t(fn)
            r.wait("v", p.s_mm)
            for e, psd in ((0, p.ps_mm), (1, p.ps_mm2)):
                fn = lambda v, e=e, psd=psd: v.tensor_scalar(
                    scr[:, e * 512:e * 512 + 512], psd[:, 0:512], ISQRT_DK,
                    None, ALU.mult)
                if e == 1:
                    r.inc("v", p.s_v, fn)
                else:
                    r.v(fn)
            r.wait("v", p.s_v)
            r.v(lambda v: v.tensor_tensor(
                scr[:, 0:512], scr[:, 0:512], p.b_cm8[:], ALU.add))
            r.inc("v", p.s_v, lambda v: v.tensor_tensor(
                scr[:, 512:1024], scr[:, 512:1024], p.b_cm8[:], ALU.add))
            r.wait("v", p.s_v)
            r.inc("v", p.s_v, lambda v: v.tensor_reduce(
                smrow[:, 0:16],
                scr[:, 0:1024].rearrange("q (pr w) -> q pr w", w=64),
                AX.X, ALU.max))
            r.wait("v", p.s_v)
            r.inc("v", p.s_v, lambda v: v.tensor_tensor(
                scr[:, 0:1024].rearrange("q (pr w) -> q pr w", w=64),
                scr[:, 0:1024].rearrange("q (pr w) -> q pr w", w=64),
                smrow[:, 0:16].unsqueeze(2).broadcast_to([128, 16, 64]),
                ALU.subtract))
            r.wait("s", p.s_v)
            r.inc("s", p.s_s, lambda s: s.activation(
                out=scrb2[:], in_=scr[:, 0:1024], func=AF.Exp))
            r.wait("v", p.s_s)
            r.inc("v", p.s_v, lambda v: v.tensor_reduce(
                smrow[:, 16:32],
                scrb2[:].rearrange("q (pr w) -> q pr w", w=64),
                AX.X, ALU.add))
            r.wait("v", p.s_v)
            r.inc("v", p.s_v, lambda v: v.reciprocal(
                smrow[:, 0:16], smrow[:, 16:32]))
            r.wait("v", p.s_v)
            r.inc("v", p.s_v, lambda v: v.tensor_tensor(
                pstg[:, 0:2048].rearrange("q (pr w) -> q pr w", w=128)[:, :, 64:128],
                scrb2[:].rearrange("q (pr w) -> q pr w", w=64),
                smrow[:, 0:16].unsqueeze(2).broadcast_to([128, 16, 64]),
                ALU.mult))
            if lp >= 1:
                r.wait("v", p.s_sty, ptlev[lp - 1])
            for r4 in range(4):
                r.wait("t", p.s_v)
                for pr in range(4 * r4, 4 * r4 + 4):
                    fn = lambda t, pr=pr: t.transpose(
                        p.ps_mm2.bitcast(BF16)[:, (pr % 4) * 128:(pr % 4) * 128 + 128],
                        pstg[:, pr * 128:(pr + 1) * 128], p.b_ident[:])
                    if pr % 4 == 3:
                        r.inc("t", p.s_mm, fn)
                    else:
                        r.t(fn)
                r.wait("v", p.s_mm)
                r.inc("v", p.s_v, lambda v, r4=r4: v.tensor_copy(
                    out=ptsb[64:128, r4 * 512:(r4 + 1) * 512],
                    in_=p.ps_mm2.bitcast(BF16)[64:128, 0:512]))
            r.wait("y", p.s_v)
            r.wait("y", p.s_sty)
            r.inc("y", p.s_sty, lambda y, lp=lp, s0=s0: y.dma_start(
                out=dr_pt[s0 + 2 * lp, :, :], in_=ptsb[64:128, 0:1024]), 16)
            r.inc("y", p.s_sty, lambda y, lp=lp, s0=s0: y.dma_start(
                out=dr_pt[s0 + 2 * lp + 1, :, :], in_=ptsb[64:128, 1024:2048]), 16)
            ptlev[lp] = p.s_sty.n
        r.barrier(p)


def _vproj(p, xn1v, dr_vtm):
    r = p.r
    nc = p.nc
    with nc.sbuf_tensor("wcolv", [128, 16 * 512], BF16) as wcol:
        r.wait("g", p.s_wg)
        for kc in range(NCH):
            for jh in range(2):
                fn = lambda g, kc=kc, jh=jh: g.dma_start(
                    out=wcol[:, (kc * 2 + jh) * 512:(kc * 2 + jh + 1) * 512],
                    in_=p.d_wv[kc * 128:(kc + 1) * 128, jh * 512:(jh + 1) * 512])
                r.inc("g", p.s_wg, fn, 16)
        r.wait("t", p.s_wg)
        stylev = {}
        evlev = []
        for tt in range(16):
            if tt >= 2:
                r.wait("v", p.s_sty, stylev[tt - 2])
            for jh in range(2):
                g_i = tt * 2 + jh
                psum = p.ps_mm if jh == 0 else p.ps_mm2
                if g_i >= 2:
                    r.wait("t", p.s_v, evlev[g_i - 2])
                for kc in range(NCH):
                    fn = lambda t, kc=kc, jh=jh, tt=tt, psum=psum: t.matmul(
                        psum[:, 0:512], xn1v[kc][:, tt * 128:(tt + 1) * 128],
                        wcol[:, (kc * 2 + jh) * 512:(kc * 2 + jh + 1) * 512],
                        start=(kc == 0), stop=(kc == NCH - 1))
                    if kc == NCH - 1:
                        r.inc("t", p.s_mm, fn)
                    else:
                        r.t(fn)
                r.wait("v", p.s_mm)
                r.inc("v", p.s_v, lambda v, tt=tt, jh=jh, psum=psum: v.tensor_copy(
                    out=p.b_vring[:, (tt % 2) * 1024 + jh * 512:
                                  (tt % 2) * 1024 + (jh + 1) * 512],
                    in_=psum[:, 0:512]))
                evlev.append(p.s_v.n)
            r.wait("y", p.s_v)
            r.wait("y", p.s_sty)
            sl = (tt % 2) * 1024
            r.inc("y", p.s_sty, lambda y, tt=tt, sl=sl: y.dma_start(
                out=dr_vtm[2 * tt, :, :], in_=p.b_vring[0:64, sl:sl + 1024]), 16)
            r.inc("y", p.s_sty, lambda y, tt=tt, sl=sl: y.dma_start(
                out=dr_vtm[2 * tt + 1, :, :],
                in_=p.b_vring[64:128, sl:sl + 1024]), 16)
            stylev[tt] = p.s_sty.n
        r.barrier(p)


def _zstage(p, cc_z_in, cc_z_out):
    r = p.r
    r.wait("v", p.s_sty)
    for c in range(NCH):
        fn = lambda v, c=c: v.tensor_tensor_scan(
            p.b_zss[:, c * NSEG:(c + 1) * NSEG],
            p.b_ones1[:, 0:1].broadcast_to([128, NSEG]),
            p.b_zss[:, c * NSEG:(c + 1) * NSEG],
            p.b_consts[:, 2:3], ALU.mult, ALU.add)
        if c == NCH - 1:
            r.inc("v", p.s_v, fn)
        else:
            r.v(fn)
    r.wait("v", p.s_v)
    r.inc("v", p.s_v, lambda v: v.tensor_copy(
        out=p.b_zs2[:, 0:8].unsqueeze(2),
        in_=p.b_zss[:].rearrange("q (c s) -> q c s", s=NSEG)[:, :, NSEG - 1:NSEG]))
    r.wait("y", p.s_v)
    r.wait("y", p.s_sty)
    r.inc("y", p.s_sty, lambda y: y.dma_start(
        out=cc_z_in[:], in_=p.b_zs2[:, 0:8]), 16)
    r.wait("g", p.s_sty)
    r.inc("g", p.s_cc, lambda g: g.collective_compute(
        "AllGather", ALU.bypass, ins=[cc_z_in[:]], outs=[cc_z_out[:]],
        replica_groups=RG))
    r.wait("y", p.s_cc)
    r.wait("y", p.s_ld)
    for rk in range(NCORES):
        r.inc("y", p.s_ld, lambda y, rk=rk: y.dma_start(
            out=p.b_zs2[:, rk * NCH:(rk + 1) * NCH],
            in_=cc_z_out[rk * 128:(rk + 1) * 128, :]), 16)
    r.wait("v", p.s_ld)
    z3 = p.b_zs2[:].rearrange("q (rk c) -> q rk c", c=NCH)
    for c in range(NCH):
        r.inc("v", p.s_v, lambda v, c=c, z3=z3: v.tensor_tensor(
            z3[:, :, c], z3[:, :, c], p.b_czmask[:], ALU.mult))
        r.wait("v", p.s_v)
        r.inc("v", p.s_v, lambda v, c=c, z3=z3: v.tensor_reduce(
            p.b_coef[:, 32 + c:33 + c], z3[:, :, c], AX.X, ALU.add))
        r.wait("v", p.s_v)
    r.inc("v", p.s_v, lambda v: v.memset(p.b_zall[:], 0.0))
    r.wait("v", p.s_v)
    W2 = 2 * (NSEG + 1)
    for lo, hi in ((0, 64), (64, 128)):
        e = 0 if lo == 0 else 1
        zl0 = p.b_zall[lo:hi, :].rearrange(
            "q (c s) -> q c s", s=W2)[:, :, e:e + 1]
        zls = p.b_zall[lo:hi, :].rearrange(
            "q (c s) -> q c s", s=W2)[:, :, 2 + e:W2:2]
        r.inc("v", p.s_v, lambda v, zl0=zl0, lo=lo, hi=hi: v.tensor_copy(
            out=zl0, in_=p.b_coef[lo:hi, 32:40].unsqueeze(2)))
        r.wait("v", p.s_v)
        r.inc("v", p.s_v, lambda v, zls=zls, lo=lo, hi=hi: v.tensor_tensor(
            zls,
            p.b_zss[lo:hi, :].rearrange("q (c s) -> q c s", s=NSEG),
            p.b_coef[lo:hi, 32:40].unsqueeze(2).broadcast_to([64, NCH, NSEG]),
            ALU.add))
        r.wait("v", p.s_v)
    r.wait("t", p.s_v)


def _chain(p, dr_sqskf, dr_pt, dr_vtm, dr_asm, ps_mg, cba, cbb):
    r = p.r
    r.wait("v", p.s_cc)
    r.inc("v", p.s_v, lambda v: v.memset(p.b_mgf[:], 0.0))
    r.wait("v", p.s_v)
    r.inc("v", p.s_v, lambda v: v.memset(p.b_aring[:], 0.0))
    r.wait("v", p.s_v)
    r.inc("v", p.s_v, lambda v: v.memset(p.b_vdu[:], 0.0))
    r.wait("v", p.s_v)
    r.inc("v", p.s_v, lambda v: v.memset(p.b_skt[:], 0.0))
    r.wait("v", p.s_v)
    r.inc("v", p.s_v, lambda v: v.memset(p.b_ptv[0:64, :], 0.0))
    r.wait("v", p.s_v)
    for h in range(H):
        par, pr = (h % 2) * 64, h // 2
        base = pr * 256 + (h % 2) * 128 + 64
        fn = lambda v, par=par, base=base: v.tensor_copy(
            out=p.b_mgf[par:par + 64, base:base + 64],
            in_=p.b_identf[par:par + 64, par:par + 64])
        if h == H - 1:
            r.inc("v", p.s_v, fn)
        else:
            r.v(fn)
    r.wait("v", p.s_v)
    r.inc("v", p.s_v, lambda v: v.tensor_copy(
        out=p.b_mg[:], in_=p.b_mgf[:]))
    r.wait("t", p.s_v)
    r.wait("y", p.s_v)
    mmlev, st2lev = {}, {}
    p._st2lev = st2lev
    for s_ in range(NSEG):
        sl = (s_ % 2) * 1024
        ring = slice(sl, sl + 1024)
        if s_ >= 2:
            r.wait("y", p.s_mm, mmlev[s_ - 2])
        r.wait("y", p.s_ld)
        r.inc("y", p.s_ld, lambda y, s_=s_, ring=ring: y.dma_start(
            out=p.b_sring[:, ring], in_=dr_sqskf[s_, :, :]), 16)
        r.inc("y", p.s_ld, lambda y, s_=s_, ring=ring: y.dma_start(
            out=p.b_ptv[64:128, ring], in_=dr_pt[s_, :, :]), 16)
        r.inc("y", p.s_ld, lambda y, s_=s_, ring=ring: y.dma_start(
            out=p.b_vring[64:128, ring], in_=dr_vtm[s_, :, :]), 16)
        r.wait("t", p.s_ld)
        _seg_core(p, s_, ring, dr_asm, ps_mg, cba, cbb)
        mmlev[s_] = p.s_mm.n
        st2lev[s_] = p.s_st2.n


def _seg_core(p, s_, ring, dr_asm, ps_mg, cba, cbb):
    r = p.r
    SEGM = 127
    s16 = slice(s_ * 16, (s_ + 1) * 16)
    r.wait("t", p.s_v)
    if SEGM & 1:
        for h in range(H):
            c = h // 2
            zcol = c * 2 * (NSEG + 1) + 2 * s_ + (h % 2)
            fn = lambda t, h=h, c=c, zcol=zcol, ring=ring: t.matmul(
                p.ps_mm[:, h:h + 1],
                p.b_sring[:, ring][:, c * 128:(c + 1) * 128],
                p.b_zall[:, zcol:zcol + 1],
                start=True, stop=True)
            if h == H - 1:
                r.inc("t", p.s_mm, fn)
            else:
                r.t(fn)
        r.wait("v", p.s_mm)
        r.inc("v", p.s_v, lambda v: v.tensor_scalar(
            p.b_coef[:, 48:64], p.ps_mm[:, 0:16], p.b_consts[:, 4:5], None,
            ALU.add))
        r.wait("v", p.s_v)
        r.inc("v", p.s_v, lambda v, s16=s16: v.reciprocal(
            p.b_rz[:, s16], p.b_coef[:, 48:64]))
    r.wait("t", p.s_v)
    if SEGM & 2:
        for pr in range(NPAIR):
            psd = cba if pr < 4 else cbb
            off = (pr % 4) * 256
            fn = lambda t, pr=pr, psd=psd, off=off, ring=ring: t.matmul(
                psd[:, off:off + 256],
                p.b_sring[:, ring][:, pr * 128:(pr + 1) * 128],
                p.b_mg[:, pr * 256:(pr + 1) * 256],
                start=True, stop=True)
            if pr == NPAIR - 1:
                r.inc("t", p.s_mm, fn)
            else:
                r.t(fn)
    if SEGM & 4:
        for h in range(H):
            c, par = h // 2, (h % 2) * 64
            psd = p.ps_mm if h < 8 else p.ps_mm2
            off = (h % 8) * 64
            fn = lambda t, c=c, par=par, psd=psd, off=off, h=h, ring=ring: \
                t.matmul(
                    psd[0:64, off:off + 64],
                    p.b_ptv[:, ring][:, c * 128 + par:c * 128 + par + 64],
                    p.b_vring[:, ring][:, h * 64:h * 64 + 64],
                    start=True, stop=True)
            if h == H - 1:
                r.inc("t", p.s_mm, fn)
            else:
                r.t(fn)
    r.wait("v", p.s_mm)
    asmv = p.b_aring[0:64, ring]
    if SEGM & 8:
        if s_ >= 2:
            r.wait("v", p.s_st2, p._st2lev[s_ - 2])
        for k, psd in ((0, p.ps_mm), (1, p.ps_mm2)):
            r.inc("v", p.s_v, lambda v, k=k, psd=psd, asmv=asmv: v.tensor_tensor(
                asmv.rearrange("q (h w) -> q h w", w=64)[:, 8 * k:8 * k + 8, :],
                psd[0:64, 0:512].rearrange("q (h w) -> q h w", w=64),
                p.b_coef[0:64, 16 + 8 * k:24 + 8 * k].unsqueeze(2)
                    .broadcast_to([64, 8, 64]),
                ALU.mult))
            r.wait("v", p.s_v)
        r.inc("v", p.s_v, lambda v, s16=s16: v.tensor_tensor(
            p.b_coef[0:64, 32:48], p.b_coef[0:64, 0:16], p.b_rz[0:64, s16],
            ALU.mult))
        r.wait("v", p.s_v)
        for k, psd in ((0, cba), (1, cbb)):
            r.inc("v", p.s_v, lambda v, k=k, psd=psd: v.tensor_tensor(
                p.b_vdu[0:64, 512 * k:512 * k + 512].rearrange(
                    "q (h w) -> q h w", w=64),
                psd[0:64, 0:1024].rearrange("q (h w) -> q h w", w=128)[:, :, 0:64],
                p.b_coef[0:64, 32 + 8 * k:40 + 8 * k].unsqueeze(2)
                    .broadcast_to([64, 8, 64]),
                ALU.mult))
            r.wait("v", p.s_v)
        r.inc("v", p.s_v, lambda v, asmv=asmv: v.tensor_tensor(
            asmv, asmv, p.b_vdu[0:64, 0:1024], ALU.add))
        r.wait("y", p.s_v)
        r.wait("y", p.s_st2)
        r.inc("y", p.s_st2, lambda y, s_=s_, asmv=asmv: y.dma_start(
            out=dr_asm[s_, :, :], in_=asmv), 16)
    if SEGM & 16:
        r.wait("v", p.s_v)
        r.inc("v", p.s_v, lambda v, s16=s16: v.tensor_scalar(
            p.b_coef[64:128, 32:48], p.b_rz[64:128, s16], -1.0, None, ALU.mult))
        r.wait("v", p.s_v)
        for k, psd in ((0, cba), (1, cbb)):
            r.inc("v", p.s_v, lambda v, k=k, psd=psd: v.tensor_tensor(
                p.b_vdu[64:128, 1024 * k:1024 * k + 1024].rearrange(
                    "q (h w) -> q h w", w=128),
                psd[64:128, 0:1024].rearrange("q (h w) -> q h w", w=128),
                p.b_coef[64:128, 32 + 8 * k:40 + 8 * k].unsqueeze(2)
                    .broadcast_to([64, 8, 128]),
                ALU.mult))
            r.wait("v", p.s_v)
        r.inc("v", p.s_v, lambda v, ring=ring: v.tensor_tensor(
            p.b_vdu[64:128, 0:2048].rearrange(
                "q (h w) -> q h w", w=128)[:, :, 0:64],
            p.b_vdu[64:128, 0:2048].rearrange(
                "q (h w) -> q h w", w=128)[:, :, 0:64],
            p.b_vring[64:128, ring].rearrange("q (h w) -> q h w", w=64),
            ALU.add))
    if SEGM & 32:
        r.wait("t", p.s_v)
        for c in range(NCH):
            fn = lambda t, c=c, ring=ring: t.transpose(
                cbb.bitcast(BF16)[:, c * 128:(c + 1) * 128],
                p.b_sring[:, ring][:, c * 128:(c + 1) * 128], p.b_ident[:])
            if c == NCH - 1:
                r.inc("t", p.s_mm, fn)
            else:
                r.t(fn)
        r.wait("v", p.s_mm)
        r.inc("v", p.s_v, lambda v: v.tensor_copy(
            out=p.b_skt[64:128, 0:1024], in_=cbb.bitcast(BF16)[64:128, 0:1024]))
    if SEGM & 64:
        r.wait("t", p.s_v)
        for h in range(H):
            c, par = h // 2, (h % 2) * 64
            fn = lambda t, h=h, c=c, par=par: t.matmul(
                ps_mg[par:par + 64, c * 128:(c + 1) * 128],
                p.b_skt[:, c * 128 + par:c * 128 + par + 64],
                p.b_vdu[:, h * 128:(h + 1) * 128],
                start=True, stop=True, tile_position=(0, par))
            if h == H - 1:
                r.inc("t", p.s_mm, fn)
            else:
                r.t(fn)
        r.wait("v", p.s_mm)
        mge = p.b_mgf[0:64, :].rearrange("q (pr w) -> q pr w", w=256)[:, :, 0:128]
        mgo = p.b_mgf[64:128, :].rearrange(
            "q (pr w) -> q pr w", w=256)[:, :, 128:256]
        pse = ps_mg[0:64, :].rearrange("q (pr w) -> q pr w", w=128)
        pso = ps_mg[64:128, :].rearrange("q (pr w) -> q pr w", w=128)
        r.inc("v", p.s_v, lambda v, mge=mge, pse=pse: v.tensor_tensor(
            mge, mge, pse, ALU.add))
        r.wait("v", p.s_v)
        r.inc("v", p.s_v, lambda v, mgo=mgo, pso=pso: v.tensor_tensor(
            mgo, mgo, pso, ALU.add))
        r.wait("v", p.s_v)
        r.inc("v", p.s_v, lambda v: v.tensor_copy(
            out=p.b_mg[:], in_=p.b_mgf[:]))
    r.wait("t", p.s_v)


def _mg_exchange(p, cc_mg_in, cc_mg_out, cba, cbb):
    r = p.r
    r.wait("v", p.s_v)
    r.inc("v", p.s_v, lambda v: v.memset(p.b_gt[:], 0.0))
    r.wait("v", p.s_v)
    r.inc("v", p.s_v, lambda v: v.memset(p.b_skt[:], 0.0))
    r.wait("v", p.s_v)
    for rd in range(2):
        r.wait("t", p.s_v)
        for hh in range(8):
            h = rd * 8 + hh
            par, pr = (h % 2) * 64, h // 2
            base = pr * 256 + (h % 2) * 128 + (64 if par == 0 else 0)
            fn = lambda t, hh=hh, base=base: t.transpose(
                cba.bitcast(BF16)[:, hh * 128:(hh + 1) * 128],
                p.b_mg[:, base:base + 128], p.b_ident[:])
            if hh == 7:
                r.inc("t", p.s_mm, fn)
            else:
                r.t(fn)
        r.wait("v", p.s_mm)
        cbab = cba.bitcast(BF16)
        ge = cbab[0:64, 0:1024].rearrange("q (b w) -> q b w", w=256)[:, :, 0:64]
        go = cbab[64:128, 0:1024].rearrange(
            "q (b w) -> q b w", w=256)[:, :, 192:256]
        gt3 = p.b_gt[:, rd * 512:(rd + 1) * 512].rearrange(
            "q (b w) -> q b w", w=128)
        r.v(lambda v, ge=ge, gt3=gt3: v.tensor_copy(
            out=gt3[0:64, :, 0:64], in_=ge))
        r.inc("v", p.s_v, lambda v, go=go, gt3=gt3: v.tensor_copy(
            out=gt3[64:128, :, 64:128], in_=go))
        r.wait("t", p.s_v)
    mg3 = p.b_mg[:].rearrange("q (pr w) -> q pr w", w=256)
    sk3 = p.b_skt[:].rearrange("q (pr w) -> q pr w", w=128)
    r.wait("v", p.s_v)
    r.v(lambda v, mg3=mg3, sk3=sk3: v.tensor_copy(
        out=sk3[0:64, :, 0:64], in_=mg3[0:64, :, 0:64]))
    r.inc("v", p.s_v, lambda v, mg3=mg3, sk3=sk3: v.tensor_copy(
        out=sk3[64:128, :, 64:128], in_=mg3[64:128, :, 128:192]))
    r.wait("y", p.s_v)
    r.wait("y", p.s_sty)
    r.inc("y", p.s_sty, lambda y: y.dma_start(
        out=cc_mg_in[0:128, :], in_=p.b_gt[:]), 16)
    r.inc("y", p.s_sty, lambda y: y.dma_start(
        out=cc_mg_in[128:256, :], in_=p.b_skt[:]), 16)
    r.wait("g", p.s_sty)
    r.inc("g", p.s_cc, lambda g: g.collective_compute(
        "AllGather", ALU.bypass, ins=[cc_mg_in[:]], outs=[cc_mg_out[:]],
        replica_groups=RG))
    r.wait("v", p.s_cc)
    r.inc("v", p.s_v, lambda v: v.memset(p.b_gtid[:], 0.0))
    r.wait("v", p.s_v)
    for h in range(H):
        par = (h % 2) * 64
        fn = lambda v, h=h, par=par: v.tensor_copy(
            out=p.b_gtid[par:par + 64, h * 64:h * 64 + 64],
            in_=p.b_ident[par:par + 64, par:par + 64])
        if h == H - 1:
            r.inc("v", p.s_v, fn)
        else:
            r.v(fn)
    r.wait("v", p.s_v)
    r.inc("v", p.s_v, lambda v: v.tensor_scalar(
        p.b_coef[:, 48:56], p.b_czmask[:], -1.0, 1.0, ALU.mult, ALU.add))
    r.wait("v", p.s_v)
    r.inc("v", p.s_v, lambda v: v.memset(p.b_mst[:], 0.0))
    for rr in range(NCORES - 1):
        r.wait("y", p.s_v)
        r.wait("y", p.s_ld)
        r.inc("y", p.s_ld, lambda y, rr=rr: y.dma_start(
            out=p.b_mgag[:, 0:1024],
            in_=cc_mg_out[rr * 256:rr * 256 + 128, :]), 16)
        r.inc("y", p.s_ld, lambda y, rr=rr: y.dma_start(
            out=p.b_mgag[:, 1024:2048],
            in_=cc_mg_out[rr * 256 + 128:rr * 256 + 256, :]), 16)
        r.wait("v", p.s_ld)
        r.inc("v", p.s_v, lambda v, rr=rr: v.tensor_scalar(
            p.b_gt[:], p.b_mgag[:, 0:1024], p.b_czmask[:, rr:rr + 1],
            None, ALU.mult))
        r.wait("v", p.s_v)
        r.inc("v", p.s_v, lambda v, rr=rr: v.scalar_tensor_tensor(
            p.b_gt[:], p.b_gtid[:], p.b_coef[:, 48 + rr:49 + rr],
            p.b_gt[:], ALU.mult, ALU.add))
        r.inc("v", p.s_v, lambda v, rr=rr: v.tensor_scalar(
            p.b_mgag[:, 1024:2048], p.b_mgag[:, 1024:2048],
            p.b_czmask[:, rr:rr + 1], None, ALU.mult))
        r.wait("t", p.s_v)
        for h in range(H):
            par = (h % 2) * 64
            r.t(lambda t, h=h, par=par: t.matmul(
                cbb[par:par + 64, h * 64:h * 64 + 64],
                p.b_gt[par:par + 64, h * 64:h * 64 + 64],
                p.b_mst[par:par + 64, h * 64:h * 64 + 64],
                start=True, stop=False, tile_position=(par, par)))
            fn = lambda t, h=h, par=par: t.matmul(
                cbb[par:par + 64, h * 64:h * 64 + 64],
                p.b_ident[par:par + 64, par:par + 64],
                p.b_mgag[par:par + 64, 1024 + h * 64:1024 + h * 64 + 64],
                start=False, stop=True, tile_position=(par, par))
            if h == H - 1:
                r.inc("t", p.s_mm, fn)
            else:
                r.t(fn)
        r.wait("v", p.s_mm)
        r.inc("v", p.s_v, lambda v: v.tensor_copy(
            out=p.b_mst[:], in_=cbb[:, 0:1024]))
        r.wait("t", p.s_v)
    r.wait("v", p.s_v)
    r.inc("v", p.s_v, lambda v: v.memset(p.b_wf[:], 0.0))
    r.wait("v", p.s_v)
    wfe, wfo = _wview(p.b_wf)
    cbe, cbo = _wview(cbb[:, 0:1024])
    r.v(lambda v, wfe=wfe, cbe=cbe: v.tensor_copy(out=wfe, in_=cbe))
    r.inc("v", p.s_v, lambda v, wfo=wfo, cbo=cbo: v.tensor_copy(out=wfo, in_=cbo))
    r.wait("v", p.s_v)
    r.inc("v", p.s_v, lambda v: v.tensor_copy(out=p.b_w[:], in_=p.b_wf[:]))
    r.wait("t", p.s_v)
    r.wait("y", p.s_v)


def _wview(ap):
    e = ap[0:64, 0:1024].rearrange("q (h w) -> q h w", w=128)[:, :, 0:64]
    o = ap[64:128, 0:1024].rearrange("q (h w) -> q h w", w=128)[:, :, 64:128]
    return [e, o]


def _rerun(p, dr_sqskf, dr_asm, a_fm, ps_mg, cba, cbb):
    r = p.r
    r.wait("v", p.s_v)
    r.inc("v", p.s_v, lambda v: v.memset(p.b_skt[0:64, :], 0.0))
    r.wait("t", p.s_v)
    mmlev = {}
    for s_ in range(NSEG):
        sl = (s_ % 2) * 1024
        ring = slice(sl, sl + 1024)
        s16 = slice(s_ * 16, (s_ + 1) * 16)
        if s_ >= 2:
            r.wait("y", p.s_mm, mmlev[s_ - 2])
        r.wait("y", p.s_ld)
        r.inc("y", p.s_ld, lambda y, s_=s_, ring=ring: y.dma_start(
            out=p.b_sring[:, ring], in_=dr_sqskf[s_, :, :]), 16)
        r.inc("y", p.s_ld, lambda y, s_=s_, ring=ring: y.dma_start(
            out=p.b_aring[0:64, ring], in_=dr_asm[s_, :, :]), 16)
        r.wait("t", p.s_ld)
        r.wait("t", p.s_v)
        for h in range(H):
            c = h // 2
            fn = lambda t, h=h, c=c, ring=ring: t.matmul(
                cba[:, h * 64:h * 64 + 64],
                p.b_sring[:, ring][:, c * 128:(c + 1) * 128],
                p.b_w[:, h * 64:h * 64 + 64],
                start=True, stop=True)
            if h == H - 1:
                r.inc("t", p.s_mm, fn)
            else:
                r.t(fn)
        r.wait("v", p.s_mm)
        r.inc("v", p.s_v, lambda v, s16=s16: v.tensor_tensor(
            p.b_coef[0:64, 32:48], p.b_coef[0:64, 0:16], p.b_rz[0:64, s16],
            ALU.mult))
        r.wait("v", p.s_v)
        r.inc("v", p.s_v, lambda v: v.tensor_tensor(
            p.b_vdu[0:64, 0:1024].rearrange("q (h w) -> q h w", w=64),
            cba[0:64, 0:1024].rearrange("q (h w) -> q h w", w=64),
            p.b_coef[0:64, 32:48].unsqueeze(2).broadcast_to([64, 16, 64]),
            ALU.mult))
        r.wait("v", p.s_v)
        r.inc("v", p.s_v, lambda v, ring=ring: v.tensor_tensor(
            p.b_aring[0:64, ring], p.b_aring[0:64, ring],
            p.b_vdu[0:64, 0:1024], ALU.add))
        r.wait("t", p.s_v)
        for c in range(NCH):
            fn = lambda t, c=c, ring=ring: t.transpose(
                cbb.bitcast(BF16)[:, c * 128:(c + 1) * 128],
                p.b_aring[:, ring][:, c * 128:(c + 1) * 128], p.b_ident[:])
            if c == NCH - 1:
                r.inc("t", p.s_mm, fn)
            else:
                r.t(fn)
        r.wait("v", p.s_mm)
        afm3 = a_fm[:].rearrange("q (c w) -> q c w", w=TL)
        r.inc("v", p.s_v, lambda v, s_=s_, afm3=afm3: v.tensor_copy(
            out=afm3[:, :, s_ * 64:(s_ + 1) * 64],
            in_=cbb.bitcast(BF16)[:, 0:1024].rearrange(
                "q (c w) -> q c w", w=128)[:, :, 0:64]))
        r.wait("v", p.s_v)
        r.inc("v", p.s_v, lambda v, s16=s16: v.tensor_scalar(
            p.b_coef[64:128, 32:48], p.b_rz[64:128, s16], -1.0, None, ALU.mult))
        r.wait("v", p.s_v)
        r.inc("v", p.s_v, lambda v: v.tensor_tensor(
            p.b_vdu[64:128, 0:1024].rearrange("q (h w) -> q h w", w=64),
            cba[64:128, 0:1024].rearrange("q (h w) -> q h w", w=64),
            p.b_coef[64:128, 32:48].unsqueeze(2).broadcast_to([64, 16, 64]),
            ALU.mult))
        r.wait("t", p.s_v)
        for c in range(NCH):
            fn = lambda t, c=c, ring=ring: t.transpose(
                cbb.bitcast(BF16)[:, c * 128:(c + 1) * 128],
                p.b_sring[:, ring][:, c * 128:(c + 1) * 128], p.b_ident[:])
            if c == NCH - 1:
                r.inc("t", p.s_mm, fn)
            else:
                r.t(fn)
        r.wait("v", p.s_mm)
        r.inc("v", p.s_v, lambda v: v.tensor_copy(
            out=p.b_skt[64:128, 0:1024], in_=cbb.bitcast(BF16)[64:128, 0:1024]))
        r.wait("t", p.s_v)
        for h in range(H):
            c, par = h // 2, (h % 2) * 64
            fn = lambda t, h=h, c=c, par=par: t.matmul(
                ps_mg[par:par + 64, h * 64:h * 64 + 64],
                p.b_skt[:, c * 128 + par:c * 128 + par + 64],
                p.b_vdu[:, h * 64:h * 64 + 64],
                start=True, stop=True, tile_position=(0, par))
            if h == H - 1:
                r.inc("t", p.s_mm, fn)
            else:
                r.t(fn)
        r.wait("v", p.s_mm)
        wfe, wfo = _wview(p.b_wf)
        pge, pgo = _wview(ps_mg[:, 0:1024])
        r.inc("v", p.s_v, lambda v, wfe=wfe, pge=pge: v.tensor_tensor(
            wfe, wfe, pge, ALU.add))
        r.wait("v", p.s_v)
        r.inc("v", p.s_v, lambda v, wfo=wfo, pgo=pgo: v.tensor_tensor(
            wfo, wfo, pgo, ALU.add))
        r.wait("v", p.s_v)
        r.inc("v", p.s_v, lambda v: v.tensor_copy(out=p.b_w[:], in_=p.b_wf[:]))
        mmlev[s_] = p.s_mm.n
    r.wait("t", p.s_v)



TC = 2048
NTC = T // TC
NDT = DI_SH // 128


def stage_attn_b(p):
    r = p.r
    nc = p.nc

    def dint(name, shape, dt=F32):
        return nc.dram_tensor(name, list(shape), dt, kind="Internal").ap()

    cc_halo_in = dint("cc_halo_in", [DI, 4], BF16)
    cc_halo_out = dint("cc_halo_out", [NCORES * DI, 4], BF16)
    cc_dt_in = dint("cc_dt_in", [DI, TL], BF16)
    cc_dx_in = dint("cc_dx_in", [DI, TL], BF16)
    cc_bc_in = dint("cc_bc_in", [2 * NST, TL], BF16)
    cc_st_in = dint("cc_st_in", [128, 512], BF16)
    cc_st_out = dint("cc_st_out", [NCORES * 128, 512], BF16)
    dr_z = dint("dr_z", [DI, TL], BF16)
    dr_xc = dint("dr_xc", [DI, TL], BF16)

    p.b_cw = p.sb("cw", [128, NCH_I * DCONV])
    p.b_cb = p.sb("cb", [128, NCH_I])
    p.b_bdt = p.sb("bdt", [128, NCH_I])
    p.b_dvec = p.sb("dvec", [128, NCH_I])
    p.b_wbc = p.sb("wbc", [128, NCH_I * 2 * NST], BF16)
    p.ps_bc = p.psb("ps_bc", [32, 512])
    r.inc("y", p.s_wl, lambda y: y.dma_start(
        out=p.b_cw[:].rearrange("q (c k) -> q c k", c=NCH_I),
        in_=p.d_convw.rearrange("(c q) k -> q c k", q=128),
        allow_slow_non_contiguous=True), 16)
    for dst, srcv in ((p.b_cb, p.d_convb), (p.b_bdt, p.d_bdt), (p.b_dvec, p.d_D)):
        r.inc("y", p.s_wl, lambda y, dst=dst, srcv=srcv: y.dma_start(
            out=dst[:].unsqueeze(2),
            in_=srcv.rearrange("o (c q) -> q c o", q=128),
            allow_slow_non_contiguous=True), 16)

    mm_per_j = TL // 512

    import contextlib
    esA = contextlib.ExitStack()
    xcv = esA.enter_context(nc.sbuf_tensor("xconv", [128, NCH_I * (TL + 3)], BF16))
    wcol = esA.enter_context(nc.sbuf_tensor("wcol", [128, 4096], BF16))
    dstg = p.b_sq[:, 0:1024].bitcast(BF16)
    sstg = p.b_tmp
    xv = chunk_views(p.b_x, NCH, TL)
    xcvv = [xcv[:, c * (TL + 3):(c + 1) * (TL + 3)] for c in range(NCH_I)]
    wslots = [wcol[:, b * 1024:(b + 1) * 1024] for b in range(4)]
    wslots2 = [wcol[:, 0:2048], wcol[:, 2048:4096]]

    r.inc("v", p.s_v, lambda v: v.memset(
        xcv[:].rearrange("q (c w) -> q c w", c=NCH_I)[:, :, 0:3], 0.0))
    with nc.sbuf_tensor("xn3", [128, NCH * TL], BF16) as xn3:
        xn3v = chunk_views(xn3, NCH, TL)
        r.barrier(p)
        emit_ln(p, xv, xn3v, gam=p.b_nw, bet=p.b_nb, double=True, add_into=xv)
        dbg_tap(p, "xn3", xn3[:], [128, NCH * TL], BF16)
        r.barrier(p)

        mm0 = p.s_mm.n
        def win_dma(g, j):
            slot = j % 4
            srcj = p.d_win[:, j * 128:(j + 1) * 128].rearrange("(kc q) o -> q kc o", q=128)
            return g.dma_start(
                out=wslots[slot][:].rearrange("q (kc o) -> q kc o", kc=NCH), in_=srcj)
        def win_throttle(lo):
            jmax = lo + 1
            if jmax >= 4:
                r.wait("g", p.s_mm, mm0 + (jmax - 3) * mm_per_j)
        wf_win = wave_load(p, 2 * NCH_I, win_dma, throttle=win_throttle)
        for j in range(2 * NCH_I):
            slot = j % 4
            r.wait("t", *wf_win[j])
            is_z = j >= NCH_I
            for it in range(mm_per_j):
                sl = slice(it * 512, (it + 1) * 512)
                psum = p.ps_mm if it % 2 == 0 else p.ps_mm2
                if j > 0 or it > 1:
                    r.wait("t", p.s_v)
                for kc in range(NCH):
                    fn = lambda t, kc=kc, sl=sl, slot=slot, psum=psum: t.matmul(
                        psum[:, 0:512], wslots[slot][:, kc * 128:(kc + 1) * 128],
                        xn3v[kc][:, sl], start=(kc == 0), stop=(kc == NCH - 1))
                    if kc == NCH - 1:
                        r.inc("t", p.s_mm, fn)
                    else:
                        r.t(fn)
                r.wait("v", p.s_mm)
                if not is_z:
                    r.inc("v", p.s_v, lambda v, j=j, sl=sl, psum=psum: v.tensor_copy(
                        out=xcvv[j][:, 3 + sl.start:3 + sl.stop], in_=psum[:, 0:512]))
                else:
                    zc = j - NCH_I
                    pp = it % 2
                    r.wait("v", p.s_sty)
                    r.inc("v", p.s_v, lambda v, sl=sl, psum=psum, pp=pp: v.tensor_copy(
                        out=sstg[:, pp * 512:pp * 512 + 512], in_=psum[:, 0:512]))
                    r.wait("s", p.s_v)
                    r.inc("s", p.s_s, lambda s, pp=pp: s.activation(
                        out=dstg[:, pp * 512:pp * 512 + 512],
                        in_=sstg[:, pp * 512:pp * 512 + 512], func=AF.Silu))
                    r.wait("y", p.s_s)
                    r.inc("y", p.s_sty, lambda y, zc=zc, sl=sl, pp=pp: y.dma_start(
                        out=dr_z[zc * 128:(zc + 1) * 128, sl],
                        in_=dstg[:, pp * 512:pp * 512 + 512]), 16)
        r.barrier(p)
        dbg_tap(p, "xpart", xcv[:], [128, NCH_I * (TL + 3)], BF16)

    for c in range(NCH_I):
        r.inc("y", p.s_sty, lambda y, c=c: y.dma_start(
            out=cc_halo_in[c * 128:(c + 1) * 128, 0:4],
            in_=xcvv[c][:, TL - 1:TL + 3], allow_slow_non_contiguous=True), 16)
    r.wait("g", p.s_sty)
    r.inc("g", p.s_cc, lambda g: g.collective_compute(
        "AllGather", ALU.bypass, ins=[cc_halo_in[:]], outs=[cc_halo_out[:]],
        replica_groups=RG))
    r.wait("g", p.s_cc)
    ho4 = cc_halo_out.rearrange("(rk c q) k -> rk q c k", q=128, c=NCH_I)
    r.wait("y", p.s_cc)
    r.inc("y", p.s_ld, lambda y: y.dma_start(
        out=xcv[:].rearrange("q (c w) -> q c w", c=NCH_I)[:, :, 0:3],
        in_=ho4[bass.ts(p.cidp_y, 1)].squeeze(0)[:, :, 1:4],
        allow_slow_non_contiguous=True), 16)
    r.wait("v", p.s_ld)
    r.inc("v", p.s_v, lambda v: v.tensor_scalar(
        xcv[:].rearrange("q (c w) -> q c w", c=NCH_I)[:, :, 0:3],
        xcv[:].rearrange("q (c w) -> q c w", c=NCH_I)[:, :, 0:3],
        p.b_czero[:, 0:1], None, ALU.mult))
    r.wait("v", p.s_v)
    r.wait("g", p.s_v)
    r.wait("s", p.s_v)

    convlev = {}
    for c in range(NCH_I):
        accA = [sstg[:, 0:1024], sstg[:, 1024:2048]]
        accB = [p.b_sq[:, 2048:3072], p.b_sq[:, 3072:4096]]
        r.wait("v", p.s_s)
        lev = {0: None, 1: None}
        for step in range(4):
            for it in range(2):
                s0 = it * 1024
                acc = accA if it == 0 else accB
                if lev[it] is not None:
                    r.wait("v", p.s_v, lev[it])
                src_ap = acc[(step + 1) % 2]
                if step == 0:
                    fn = lambda v, c=c, s0=s0, acc=acc: v.scalar_tensor_tensor(
                        acc[0], xcvv[c][:, s0:s0 + 1024],
                        p.b_cw[:, c * 4:c * 4 + 1],
                        p.b_cb[:, c:c + 1].broadcast_to([128, 1024]),
                        ALU.mult, ALU.add)
                else:
                    fn = lambda v, c=c, s0=s0, acc=acc, step=step: \
                        v.scalar_tensor_tensor(
                            acc[step % 2], xcvv[c][:, s0 + step:s0 + step + 1024],
                            p.b_cw[:, c * 4 + step:c * 4 + step + 1],
                            acc[(step + 1) % 2], ALU.mult, ALU.add)
                r.inc("v", p.s_v, fn)
                lev[it] = p.s_v.n
        r.wait("s", p.s_v)
        r.inc("s", p.s_s, lambda s, c=c: s.activation(
            out=xcvv[c][:, 3:1027], in_=sstg[:, 1024:2048], func=AF.Silu))
        r.inc("s", p.s_s, lambda s, c=c: s.activation(
            out=xcvv[c][:, 1027:2051], in_=p.b_sq[:, 3072:4096], func=AF.Silu))
    r.barrier(p)
    dbg_tap(p, "xconv", xcv[:], [128, NCH_I * (TL + 3)], BF16)

    mm1 = p.s_mm.n
    def wdt_dma(g, i):
        j, h = divmod(i, 2)
        slot4 = i % 4
        srcj = p.d_wdt[h * 1024:(h + 1) * 1024, j * 128:(j + 1) * 128].rearrange(
            "(kc q) o -> q kc o", q=128)
        return g.dma_start(
            out=wslots[slot4][:].rearrange("q (kc o) -> q kc o", kc=NCH), in_=srcj)
    def wdt_throttle(i):
        j = i // 2
        if j >= 2:
            r.wait("g", p.s_mm, mm1 + (j - 1) * mm_per_j)
    wf_wdt = wave_load(p, 2 * NCH_I, wdt_dma, throttle=wdt_throttle)
    for j in range(NCH_I):
        r.wait("t", *wf_wdt[2 * j])
        r.wait("t", *wf_wdt[2 * j + 1])
        for it in range(mm_per_j):
            sl = slice(it * 512, (it + 1) * 512)
            pp = it % 2
            psum = p.ps_mm if pp == 0 else p.ps_mm2
            if j > 0 or it > 1:
                r.wait("t", p.s_v)
            for kc in range(NCH_I):
                slot4 = (2 * j + kc // 8) % 4
                kcl = kc % 8
                fn = lambda t, kcl=kcl, sl=sl, slot4=slot4, psum=psum, kc=kc: t.matmul(
                    psum[:, 0:512], wslots[slot4][:, kcl * 128:(kcl + 1) * 128],
                    xcvv[kc][:, 3 + sl.start:3 + sl.stop],
                    start=(kc == 0), stop=(kc == NCH_I - 1))
                if kc == NCH_I - 1:
                    r.inc("t", p.s_mm, fn)
                else:
                    r.t(fn)
            r.wait("v", p.s_mm)
            r.inc("v", p.s_v, lambda v, j=j, psum=psum, pp=pp: v.tensor_scalar(
                sstg[:, pp * 1024:pp * 1024 + 512], psum[:, 0:512],
                p.b_bdt[:, j:j + 1], None, ALU.add))
            r.wait("s", p.s_v)
            r.inc("s", p.s_s, lambda s, pp=pp: s.activation(
                out=sstg[:, pp * 1024 + 512:pp * 1024 + 1024],
                in_=sstg[:, pp * 1024:pp * 1024 + 512], func=AF.Abs))
            r.wait("s", p.s_s)
            r.inc("s", p.s_s, lambda s, pp=pp: s.activation(
                out=sstg[:, pp * 1024 + 512:pp * 1024 + 1024],
                in_=sstg[:, pp * 1024 + 512:pp * 1024 + 1024], func=AF.Exp,
                bias=p.b_consts[:, 2:3], scale=p.b_consts[:, 5:6]))
            r.wait("s", p.s_s)
            r.inc("s", p.s_s, lambda s, pp=pp: s.activation(
                out=sstg[:, pp * 1024 + 512:pp * 1024 + 1024],
                in_=sstg[:, pp * 1024 + 512:pp * 1024 + 1024], func=AF.Ln,
                bias=p.b_consts[:, 1:2], scale=p.b_consts[:, 1:2]))
            r.wait("v", p.s_s)
            r.wait("v", p.s_sty)
            r.inc("v", p.s_v, lambda v, pp=pp: v.scalar_tensor_tensor(
                dstg[:, pp * 512:pp * 512 + 512],
                sstg[:, pp * 1024:pp * 1024 + 512], p.b_consts[:, 2:3],
                sstg[:, pp * 1024 + 512:pp * 1024 + 1024], ALU.max, ALU.add))
            r.wait("v", p.s_v)
            r.inc("v", p.s_v, lambda v, j=j, sl=sl, pp=pp: v.tensor_tensor(
                dstg[:, 1024 + pp * 512:1024 + pp * 512 + 512],
                dstg[:, pp * 512:pp * 512 + 512],
                xcvv[j][:, 3 + sl.start:3 + sl.stop], ALU.mult))
            r.wait("y", p.s_v)
            r.inc("y", p.s_sty, lambda y, j=j, sl=sl, pp=pp: y.dma_start(
                out=cc_dt_in[j * 128:(j + 1) * 128, sl],
                in_=dstg[:, pp * 512:pp * 512 + 512]), 16)
            r.inc("y", p.s_sty, lambda y, j=j, sl=sl, pp=pp: y.dma_start(
                out=cc_dx_in[j * 128:(j + 1) * 128, sl],
                in_=dstg[:, 1024 + pp * 512:1024 + pp * 512 + 512]), 16)
    r.barrier(p)

    def wbc_dma(g, i):
        if i == 0:
            return g.dma_start(
                out=p.b_wbc[:].rearrange("q (kc m) -> q kc m", kc=NCH_I)[:, :, 0:NST],
                in_=p.d_wB.rearrange("(kc q) n -> q kc n", q=128))
        return g.dma_start(
            out=p.b_wbc[:].rearrange("q (kc m) -> q kc m", kc=NCH_I)[:, :, NST:2 * NST],
            in_=p.d_wC.rearrange("(kc q) n -> q kc n", q=128))
    wf_bc = wave_load(p, 2, wbc_dma)
    r.wait("t", *wf_bc[1])
    for it in range(mm_per_j):
        sl = slice(it * 512, (it + 1) * 512)
        for kc in range(NCH_I):
            fn = lambda t, kc=kc, sl=sl: t.matmul(
                p.ps_bc[0:32, 0:512], p.b_wbc[:, kc * 32:kc * 32 + 32],
                xcvv[kc][:, 3 + sl.start:3 + sl.stop],
                start=(kc == 0), stop=(kc == NCH_I - 1))
            if kc == NCH_I - 1:
                r.inc("t", p.s_mm, fn)
            else:
                r.t(fn)
        r.wait("v", p.s_mm)
        r.inc("v", p.s_v, lambda v, sl=sl: v.tensor_copy(
            out=p.b_sq[0:32, 1024:2048].bitcast(BF16)[:, sl], in_=p.ps_bc[0:32, 0:512]))
        r.wait("t", p.s_v)
    r.wait("y", p.s_v)
    r.inc("y", p.s_sty, lambda y: y.dma_start(out=cc_bc_in[:], in_=p.b_sq[0:32, 1024:2048].bitcast(BF16)[:, 0:TL]), 16)
    for c in range(NCH_I):
        r.inc("y", p.s_sty, lambda y, c=c: y.dma_start(
            out=dr_xc[c * 128:(c + 1) * 128, :], in_=xcvv[c][:, 3:3 + TL]), 16)
    r.wait("g", p.s_sty)

    r.barrier(p)
    esA.close()

    with nc.sbuf_tensor("ygbuf", [128, NCH_I * TL], BF16) as ygb, \
         nc.sbuf_tensor("dt_ch", [128, TL], BF16) as dtch, \
         nc.sbuf_tensor("dx_ch", [128, TL], BF16) as dxch, \
         nc.sbuf_tensor("brep", [128, 6 * 2048], BF16) as brep, \
         nc.sbuf_tensor("y_f", [128, TL], F32) as y_f, \
         nc.psum_tensor("ps_yacc", [128, 2048], F32) as ps_yacc:
        ststg = p.b_str.bitcast(BF16)
        ygv = chunk_views(ygb, NCH_I, TL)
        a_f = p.b_sq[:, 0:2048]
        u_f = p.b_sq[:, 2048:4096]
        h_f = p.b_tmp[:, 0:2048]
        amat = p.b_rep[:, 0:256]
        sst = p.b_trin[:, 0:800]
        dts = sst[:, 0:16]
        tdec = sst[:, 16:272]
        hloc = sst[:, 272:528]
        hstart = sst[:, 528:784]
        scr16 = sst[:, 784:800]
        al3 = p.d_alog.rearrange("(c q) n -> q c n", q=128)
        r.wait("y", p.s_ld)
        r.inc("y", p.s_ld, lambda y: y.dma_start(
            out=amat[:].rearrange("q (c n) -> q c n", n=NST), in_=al3,
            allow_slow_non_contiguous=True), 16)
        r.wait("s", p.s_ld)
        r.inc("s", p.s_s, lambda s: s.activation(out=amat[:], in_=amat[:], func=AF.Exp))
        r.wait("v", p.s_s)
        r.inc("v", p.s_v, lambda v: v.tensor_scalar(amat[:], amat[:], -1.0, None, ALU.mult))
        r.wait("s", p.s_v)

        sqb = p.b_sq.bitcast(BF16)
        tmb = p.b_tmp.bitcast(BF16)
        aslot = [sqb[:, 0:2048], sqb[:, 2048:4096]]
        uslot = [sqb[:, 4096:6144], sqb[:, 6144:8192]]
        hslot = [tmb[:, 0:2048], tmb[:, 2048:4096]]
        yfb = y_f.bitcast(BF16)
        pslot = [yfb[:, 0:2048], yfb[:, 2048:4096]]
        p.ps_yacc = ps_yacc

        ldlev, scanlev, addlev, explev, umullev, prodlev = {}, {}, {}, {}, {}, {}
        r.wait("y", p.s_ld)
        r.inc("y", p.s_ld, lambda y: y.dma_start(
            out=dtch[:], in_=cc_dt_in[0:128, :]), 16)
        r.inc("y", p.s_ld, lambda y: y.dma_start(
            out=dxch[:], in_=cc_dx_in[0:128, :]), 16)
        ldlev[0] = p.s_ld.n
        r.wait("v", p.s_ld)
        r.inc("v", p.s_v, lambda v: v.tensor_reduce(
            dts[:, 0:1], dtch[:], AX.X, ALU.add))
        sprlev = {}
        for kk in range(3):
            n0 = kk
            bsl = (kk % 3) * 4096
            r.wait("y", p.s_br)
            r.inc("y", p.s_br, lambda y, n0=n0, bsl=bsl: y.dma_start(
                out=brep[:, bsl:bsl + 2048],
                in_=cc_bc_in[n0:n0 + 1, :].broadcast_to([128, TL]),
                allow_slow_non_contiguous=True), 16)
            r.inc("y", p.s_br, lambda y, n0=n0, bsl=bsl: y.dma_start(
                out=brep[:, bsl + 2048:bsl + 4096],
                in_=cc_bc_in[NST + n0:NST + n0 + 1, :].broadcast_to([128, TL]),
                allow_slow_non_contiguous=True), 16)
            sprlev[kk] = p.s_br.n

        NSTEP = NCH_I * NST
        for k in range(NSTEP):
            cc, n = divmod(k, NST)
            csl = slice(0, TL)
            col = k
            if k >= 2:
                r.wait("s", p.s_v, scanlev[k - 2])
            r.wait("s", p.s_ld, ldlev[cc])
            r.inc("s", p.s_s, lambda s, csl=csl, k=k, col=col: s.activation(
                out=aslot[k % 2], in_=dtch[:, csl], func=AF.Exp,
                scale=amat[:, col:col + 1]))
            explev[k] = p.s_s.n
            if n == NST - 1 and cc + 1 < NCH_I:
                pass
            r.wait("g", p.s_br, sprlev[k])
            r.wait("g", p.s_ld, ldlev[cc])
            if k >= 2:
                r.wait("g", p.s_v, scanlev[k - 2])
            r.inc("g", p.s_g, lambda g, csl=csl, k=k, bsl=(k % 3) * 4096:
                  g.tensor_tensor(uslot[k % 2], dxch[:, csl],
                                  brep[:, bsl:bsl + 2048], ALU.mult))
            umullev[k] = p.s_g.n
            if n == NST - 1 and cc + 1 < NCH_I:
                r.wait("y", p.s_ld)
                r.wait("y", p.s_s, explev[k])
                r.wait("y", p.s_g, umullev[k])
                r.wait("y", p.s_v)
                r.inc("y", p.s_ld, lambda y, cc=cc: y.dma_start(
                    out=dtch[:], in_=cc_dt_in[(cc + 1) * 128:(cc + 2) * 128, :]), 16)
                r.inc("y", p.s_ld, lambda y, cc=cc: y.dma_start(
                    out=dxch[:], in_=cc_dx_in[(cc + 1) * 128:(cc + 2) * 128, :]), 16)
                ldlev[cc + 1] = p.s_ld.n
            r.wait("v", p.s_s, explev[k])
            r.wait("v", p.s_g, umullev[k])
            r.inc("v", p.s_v, lambda v, k=k: v.tensor_tensor_scan(
                hslot[k % 2], aslot[k % 2], uslot[k % 2], p.b_consts[:, 2:3],
                ALU.mult, ALU.add))
            scanlev[k] = p.s_v.n
            if k >= 1:
                _p1_prod_add(p, k - 1, hslot, pslot, brep, ygv[(k - 1) // NST],
                             scanlev, prodlev, addlev, sprlev, hloc)
                if k + 2 < NSTEP:
                    n2 = (k + 2) % NST
                    bsl2 = ((k + 2) % 3) * 4096
                    r.wait("y", p.s_br)
                    r.wait("y", p.s_g, prodlev[k - 1])
                    r.inc("y", p.s_br, lambda y, n2=n2, bsl2=bsl2: y.dma_start(
                        out=brep[:, bsl2:bsl2 + 2048],
                        in_=cc_bc_in[n2:n2 + 1, :].broadcast_to([128, TL]),
                        allow_slow_non_contiguous=True), 16)
                    r.inc("y", p.s_br, lambda y, n2=n2, bsl2=bsl2: y.dma_start(
                        out=brep[:, bsl2 + 2048:bsl2 + 4096],
                        in_=cc_bc_in[NST + n2:NST + n2 + 1, :]
                            .broadcast_to([128, TL]),
                        allow_slow_non_contiguous=True), 16)
                    sprlev[k + 2] = p.s_br.n
            if n == 0 and cc >= 1:
                _p1_cc_tail(p, cc - 1, ygv, y_f, dts, tdec, scr16, amat,
                            dtch, addlev)
                r.wait("v", p.s_ld, ldlev[cc])
                r.wait("v", p.s_v)
                r.inc("v", p.s_v, lambda v, cc=cc: v.tensor_reduce(
                    dts[:, cc:cc + 1], dtch[:], AX.X, ALU.add))
        _p1_prod_add(p, NSTEP - 1, hslot, pslot, brep, ygv[NCH_I - 1],
                     scanlev, prodlev, addlev, sprlev, hloc)
        _p1_cc_tail(p, NCH_I - 1, ygv, y_f, dts, tdec, scr16, amat,
                    dtch, addlev)
        r.barrier(p)
        r.v(lambda v: v.tensor_copy(out=ststg[:, 0:256], in_=tdec[:]))
        r.inc("v", p.s_v, lambda v: v.tensor_copy(out=ststg[:, 256:512], in_=hloc[:]))
        r.wait("y", p.s_v)
        r.wait("y", p.s_sty)
        r.inc("y", p.s_sty, lambda y: y.dma_start(
            out=cc_st_in[:], in_=ststg[:, 0:512]), 16)
        r.wait("g", p.s_sty)
        r.inc("g", p.s_cc, lambda g: g.collective_compute(
            "AllGather", ALU.bypass, ins=[cc_st_in[:]], outs=[cc_st_out[:]],
            replica_groups=RG))
        r.wait("v", p.s_cc)
        r.inc("v", p.s_v, lambda v: v.memset(hstart[:], 0.0))
        r.wait("y", p.s_cc)
        for rr in range(NCORES - 1):
            r.wait("y", p.s_v)
            r.wait("y", p.s_ld)
            r.inc("y", p.s_ld, lambda y, rr=rr: y.dma_start(
                out=ststg[:, 512:1024], in_=cc_st_out[rr * 128:(rr + 1) * 128, :]),
                16)
            r.wait("v", p.s_ld)
            r.inc("v", p.s_v, lambda v, rr=rr: v.tensor_scalar(
                ststg[:, 512:768], ststg[:, 512:768],
                p.b_czmask[:, rr:rr + 1], p.b_czmaskc[:, rr:rr + 1],
                ALU.mult, ALU.add))
            r.wait("v", p.s_v)
            r.inc("v", p.s_v, lambda v, rr=rr: v.tensor_scalar(
                ststg[:, 768:1024], ststg[:, 768:1024],
                p.b_czmask[:, rr:rr + 1], None, ALU.mult))
            r.wait("v", p.s_v)
            r.inc("v", p.s_v, lambda v: v.tensor_tensor(
                hstart[:], hstart[:], ststg[:, 512:768], ALU.mult))
            r.wait("v", p.s_v)
            r.inc("v", p.s_v, lambda v: v.tensor_tensor(
                hstart[:], hstart[:], ststg[:, 768:1024], ALU.add))
            r.wait("v", p.s_v)
        cum_f = p.b_sq[:, 0:2048]
        decsl = hslot
        t1sl = [sqb[:, 4096:6144], sqb[:, 6144:8192]]
        zb = y_f.bitcast(BF16)[:, 0:2048]
        declev, t1lev, sttlev, cumlev, ld2 = {}, {}, {}, {}, {}
        r.wait("y", p.s_ld)
        r.inc("y", p.s_ld, lambda y: y.dma_start(
            out=dtch[:], in_=cc_dt_in[0:128, :]), 16)
        r.inc("y", p.s_ld, lambda y: y.dma_start(
            out=dxch[:], in_=dr_xc[0:128, :]), 16)
        r.inc("y", p.s_ld, lambda y: y.dma_start(
            out=zb, in_=dr_z[0:128, :]), 16)
        ld2[0] = p.s_ld.n
        r.wait("v", p.s_ld, ld2[0])
        r.wait("v", p.s_cc)
        r.inc("v", p.s_v, lambda v: v.tensor_tensor_scan(
            cum_f, p.b_ones1[:, 0:1].broadcast_to([128, TL]), dtch[:],
            p.b_consts[:, 2:3], ALU.mult, ALU.add))
        cumlev[0] = p.s_v.n
        for kk in range(2):
            bsl = (kk % 3) * 4096 + 2048
            r.wait("y", p.s_br)
            r.inc("y", p.s_br, lambda y, kk=kk, bsl=bsl: y.dma_start(
                out=brep[:, bsl:bsl + 2048],
                in_=cc_bc_in[NST + kk:NST + kk + 1, :].broadcast_to([128, TL]),
                allow_slow_non_contiguous=True), 16)
            sprlev[kk] = p.s_br.n
        for k in range(NSTEP):
            cc, n = divmod(k, NST)
            csl = slice(0, TL)
            if n == 0 and cc >= 1:
                _p2_gate(p, cc - 1, ygv, dxch, t1sl, zb, sttlev)
                r.wait("y", p.s_ld)
                r.wait("y", p.s_s)
                r.wait("y", p.s_v)
                r.inc("y", p.s_ld, lambda y, cc=cc: y.dma_start(
                    out=dtch[:], in_=cc_dt_in[cc * 128:(cc + 1) * 128, :]), 16)
                r.inc("y", p.s_ld, lambda y, cc=cc: y.dma_start(
                    out=dxch[:], in_=dr_xc[cc * 128:(cc + 1) * 128, :]), 16)
                r.inc("y", p.s_ld, lambda y, cc=cc: y.dma_start(
                    out=zb, in_=dr_z[cc * 128:(cc + 1) * 128, :]), 16)
                ld2[cc] = p.s_ld.n
                r.wait("v", p.s_ld, ld2[cc])
                r.wait("v", p.s_s)
                r.inc("v", p.s_v, lambda v: v.tensor_tensor_scan(
                    cum_f, p.b_ones1[:, 0:1].broadcast_to([128, TL]),
                    dtch[:], p.b_consts[:, 2:3], ALU.mult, ALU.add))
                cumlev[cc] = p.s_v.n
            if k >= 2:
                r.wait("s", p.s_v, t1lev[k - 2])
            r.wait("s", p.s_v, cumlev[cc])
            r.inc("s", p.s_s, lambda s, k=k: s.activation(
                out=decsl[k % 2], in_=cum_f, func=AF.Exp,
                scale=amat[:, k:k + 1]))
            declev[k] = p.s_s.n
            r.wait("v", p.s_br, sprlev[k])
            r.wait("v", p.s_s, declev[k])
            r.inc("v", p.s_v, lambda v, k=k, bsl=(k % 3) * 4096 + 2048:
                  v.tensor_tensor(t1sl[k % 2], decsl[k % 2],
                                  brep[:, bsl:bsl + 2048], ALU.mult))
            t1lev[k] = p.s_v.n
            r.wait("v", p.s_v)
            r.inc("v", p.s_v, lambda v, k=k, cc=cc: v.scalar_tensor_tensor(
                ygv[cc], t1sl[k % 2], hstart[:, k:k + 1], ygv[cc],
                ALU.mult, ALU.add))
            sttlev[k] = p.s_v.n
            k2 = k + 2
            if k2 < NSTEP:
                n2 = k2 % NST
                bsl2 = (k2 % 3) * 4096 + 2048
                r.wait("y", p.s_br)
                r.wait("y", p.s_v, t1lev.get(k - 1, 0))
                r.inc("y", p.s_br, lambda y, n2=n2, bsl2=bsl2: y.dma_start(
                    out=brep[:, bsl2:bsl2 + 2048],
                    in_=cc_bc_in[NST + n2:NST + n2 + 1, :].broadcast_to([128, TL]),
                    allow_slow_non_contiguous=True), 16)
                sprlev[k2] = p.s_br.n
        _p2_gate(p, NCH_I - 1, ygv, dxch, t1sl, zb, sttlev)
        r.barrier(p)

        if True:
            wcol2 = brep[:, 0:4096]
            mm2 = p.s_mm.n
            w2slots = [wcol2[:, b * 1024:(b + 1) * 1024] for b in range(4)]
            def wout_dma(g, i):
                j, h = divmod(i, 2)
                slot4 = i % 4
                srcj = p.d_wout[h * 1024:(h + 1) * 1024, j * 128:(j + 1) * 128].rearrange(
                    "(kc q) o -> q kc o", q=128)
                return g.dma_start(
                    out=w2slots[slot4][:].rearrange("q (kc o) -> q kc o", kc=NCH), in_=srcj)
            def wout_throttle(i):
                j = i // 2
                if j >= 2:
                    r.wait("g", p.s_mm, mm2 + (j - 1) * mm_per_j)
            wf_wo = wave_load(p, 2 * NCH, wout_dma, throttle=wout_throttle)
            for j in range(NCH):
                r.wait("t", *wf_wo[2 * j])
                r.wait("t", *wf_wo[2 * j + 1])
                for it in range(mm_per_j):
                    sl = slice(it * 512, (it + 1) * 512)
                    psum = p.ps_mm if it % 2 == 0 else p.ps_mm2
                    if j > 0 or it > 1:
                        r.wait("t", p.s_v)
                    for kc in range(NCH_I):
                        slot4 = (2 * j + kc // 8) % 4
                        kcl = kc % 8
                        fn = lambda t, kcl=kcl, sl=sl, slot4=slot4, psum=psum, kc=kc: t.matmul(
                            psum[:, 0:512], w2slots[slot4][:, kcl * 128:(kcl + 1) * 128],
                            ygv[kc][:, sl], start=(kc == 0), stop=(kc == NCH_I - 1))
                        if kc == NCH_I - 1:
                            r.inc("t", p.s_mm, fn)
                        else:
                            r.t(fn)
                    r.wait("v", p.s_mm)
                    r.inc("v", p.s_v, lambda v, j=j, sl=sl, psum=psum: v.tensor_tensor(
                        xv[j][:, sl], xv[j][:, sl], psum[:, 0:512], ALU.add))
            r.barrier(p)


def _p1_prod_add(p, k, hslot, pslot, brep, ygvcc, scanlev, prodlev, addlev,
                 sprlev, hloc):
    r = p.r
    n = k % 16
    bsl = (k % 3) * 4096 + 2048
    r.wait("g", p.s_v, scanlev[k])
    if k >= 2:
        r.wait("g", p.s_v, addlev[k - 2])
    r.wait("g", p.s_br, sprlev[k])
    r.g(lambda g, k=k: g.tensor_copy(
        out=hloc[:, k:k + 1], in_=hslot[k % 2][:, TL - 1:TL]))
    r.inc("g", p.s_g, lambda g, k=k, bsl=bsl: g.tensor_tensor(
        pslot[k % 2], hslot[k % 2], brep[:, bsl:bsl + 2048], ALU.mult))
    prodlev[k] = p.s_g.n
    r.wait("v", p.s_g, prodlev[k])
    if n == 0:
        r.inc("v", p.s_v, lambda v, k=k, ygvcc=ygvcc: v.tensor_copy(
            out=p.ps_yacc[:], in_=pslot[k % 2]))
    elif n == 15:
        r.inc("v", p.s_v, lambda v, k=k, ygvcc=ygvcc: v.tensor_tensor(
            ygvcc, p.ps_yacc[:], pslot[k % 2], ALU.add))
    else:
        r.inc("v", p.s_v, lambda v, k=k, ygvcc=ygvcc: v.tensor_tensor(
            p.ps_yacc[:], p.ps_yacc[:], pslot[k % 2], ALU.add))
    addlev[k] = p.s_v.n


def _p1_cc_tail(p, cc, ygv, y_f, dts, tdec, scr16, amat, dtch, addlev):
    r = p.r
    r.wait("v", p.s_s)
    r.wait("v", p.s_v)
    r.inc("v", p.s_v, lambda v, cc=cc: v.tensor_scalar(
        scr16[:], amat[:, cc * 16:(cc + 1) * 16], dts[:, cc:cc + 1],
        None, ALU.mult))
    r.wait("s", p.s_v)
    r.inc("s", p.s_s, lambda s, cc=cc: s.activation(
        out=tdec[:, cc * 16:(cc + 1) * 16], in_=scr16[:], func=AF.Exp))


def _p2_gate(p, cc, ygv, dxch, t1sl, zb, sttlev):
    r = p.r
    csl = slice(0, 2048)
    r.wait("v", p.s_v)
    r.inc("v", p.s_v, lambda v, cc=cc, csl=csl: v.scalar_tensor_tensor(
        t1sl[0], dxch[:, csl], p.b_dvec[:, cc:cc + 1], ygv[cc],
        ALU.mult, ALU.add))
    r.wait("v", p.s_v)
    r.inc("v", p.s_v, lambda v, cc=cc: v.tensor_tensor(
        ygv[cc], t1sl[0], zb, ALU.mult))


def kernel(**inputs):
    nc, out_names = build_program(stages=("attn_a", "ffn1", "attn_b", "ffn2"))
    in_maps = make_host_inputs(inputs)
    res = run_bass_kernel_spmd(nc, in_maps, core_ids=list(range(NCORES)))
    return assemble_output(res.results)
